# version 07
# speedup vs baseline: 1.4957x; 1.0647x over previous
import numpy as np
from contextlib import ExitStack
import concourse.bass as bass
import concourse.mybir as mybir
from concourse.bass_utils import run_bass_kernel_spmd

F32 = mybir.dt.float32
BF16 = mybir.dt.bfloat16
AF = mybir.ActivationFunctionType
ALU = mybir.AluOpType
AXX = mybir.AxisListType.X

D = 1024
HB = 8
NST = 16
FF = 4096
FB = 32
NH = 16
HP = NH // 2
DH = 64
EPS_LN = 1e-5
TOK = 2048
TC = 256
CD = 128
PASS = 1024
ENGS = ("sync", "tensor", "vector", "scalar", "gpsimd")


class Prog:
    def __init__(self, nc, es):
        self.nc, self.es = nc, es
        self.q = {e: [] for e in ENGS}
        self.sems, self.cnt = {}, {}

    def sem(self, name):
        if name not in self.sems:
            self.sems[name] = self.es.enter_context(self.nc.semaphore(name))
            self.cnt[name] = 0
        return self.sems[name]

    def op(self, eng, fn, sem=None, k=1):
        if sem is None:
            self.q[eng].append(fn)
            return None
        s = self.sem(sem)
        self.cnt[sem] += k

        def run(e, fn=fn, s=s, k=k):
            fn(e).then_inc(s, k)
        self.q[eng].append(run)
        return self.cnt[sem]

    def dma(self, eng, out, in_, sem):
        return self.op(eng, lambda e, o=out, i=in_: e.dma_start(out=o, in_=i),
                       sem, 16)

    def wait(self, eng, sem, val=None):
        s = self.sem(sem)
        v = self.cnt[sem] if val is None else val
        if v > 0:
            self.q[eng].append(lambda e, s=s, v=v: e.wait_ge(s, v))

    def current(self, sem):
        self.sem(sem)
        return self.cnt[sem]

    def emit(self, block):
        for e in ENGS:
            fns = self.q[e]
            if fns:
                def body(h, fns=fns):
                    for f in fns:
                        f(h)
                getattr(block, e)(body)


class Arena:

    def __init__(self, backing, cols):
        self.b = backing
        self.cols = cols
        self.off = 0
        self.marks = []

    def push(self):
        self.marks.append(self.off)

    def pop(self):
        self.off = self.marks.pop()

    def alloc(self, cols_f32, dt=F32, parts=128):
        cols_f32 = (cols_f32 + 3) & ~3
        assert self.off + cols_f32 <= self.cols, \
            f"arena overflow: {self.off}+{cols_f32} > {self.cols}"
        ap = self.b[0:parts, self.off:self.off + cols_f32]
        self.off += cols_f32
        if dt == BF16:
            ap = ap.bitcast(BF16)
        return ap

    def f32(self, n_elems_per_part, parts=128):
        return self.alloc(n_elems_per_part, F32, parts)

    def bf16(self, n_elems_per_part, parts=128):
        return self.alloc((n_elems_per_part + 1) // 2, BF16, parts)


def _f32(a):
    return np.ascontiguousarray(np.asarray(a), dtype=np.float32)


def host_prep(inputs, tok=TOK):
    x = np.asarray(inputs["L0_attn_x"])
    wT = {
        "WdT": _f32(np.asarray(inputs["L0_attn_Wd"]).T),
        "WbcT": _f32(np.asarray(inputs["L0_attn_Wbc"]).T),
        "W1T": _f32(np.asarray(inputs["L0_attn_W1"]).T),
        "W2T": _f32(np.asarray(inputs["L0_attn_W2"]).T),
        "WrT": _f32(np.asarray(inputs["L0_ffn_W_r"]).T),
        "WkT": _f32(np.asarray(inputs["L0_ffn_W_k"]).T),
        "WvT": _f32(np.asarray(inputs["L0_ffn_W_v"]).T),
        "Wq": _f32(inputs["L1_attn_Wq"]), "Wk1": _f32(inputs["L1_attn_Wk"]),
        "Wv1": _f32(inputs["L1_attn_Wv"]), "Wbeta": _f32(inputs["L1_attn_Wbeta"]),
        "Wout": _f32(inputs["L1_attn_Wout"]),
        "w1T": _f32(np.asarray(inputs["L1_ffn_w1_w"]).T),
        "w2T": _f32(np.asarray(inputs["L1_ffn_w2_w"]).T),
        "h1T": _f32(np.asarray(inputs["L1_ffn_h1_w"]).T),
        "h2T": _f32(np.asarray(inputs["L1_ffn_h2_w"]).T),
        "w3T": _f32(np.asarray(inputs["L1_ffn_w3_w"]).T),
    }
    fvec = lambda v: _f32(np.asarray(v).reshape(-1, 128).T)
    vecs = {"ln1_g": "L0_attn_ln1_g", "ln1_b": "L0_attn_ln1_b",
            "bd": "L0_attn_bd", "D0": "L0_attn_D", "ln2_g": "L0_attn_ln2_g",
            "ln2_b": "L0_attn_ln2_b", "b1": "L0_attn_b1", "b2": "L0_attn_b2",
            "mu_r": "L0_ffn_mu_r", "mu_k": "L0_ffn_mu_k",
            "w1s": "L1_ffn_w1_s", "w2s": "L1_ffn_w2_s", "h1s": "L1_ffn_h1_s",
            "h2s": "L1_ffn_h2_s", "w3s": "L1_ffn_w3_s"}
    vec_sb = {k: fvec(inputs[v]) for k, v in vecs.items()}
    A_log = np.asarray(inputs["L0_attn_A_log"])
    A_sb = _f32(A_log.reshape(HB, 128, NST).transpose(1, 0, 2).reshape(128, HB * NST))
    bbc_col = np.zeros((128, 1), np.float32)
    bbc_col[:32, 0] = np.asarray(inputs["L0_attn_bbc"])
    bbeta_r = _f32(np.broadcast_to(np.asarray(inputs["L1_attn_b_beta"])[None, :],
                                   (128, NH)))
    import ml_dtypes
    consts = {
        "A_sb": A_sb, "bbc_col": _f32(bbc_col), "bbeta_r": bbeta_r,
        "ident_bf": np.eye(128).astype(ml_dtypes.bfloat16),
        "ident_f32": _f32(np.eye(128)),
        "trilS": _f32(np.tril(np.ones((128, 128)), -1)),
        "triuI": _f32(np.triu(np.ones((128, 128)), 0)),
    }
    nsel = np.zeros((32, 32 * 128), np.float32)
    for n in range(32):
        nsel[n, n * 128:(n + 1) * 128] = 1.0
    consts["nsel"] = _f32(nsel)
    consts["trilS4"] = _f32(np.tile(np.tril(np.ones((128, 128)), -1), (1, 4)))
    consts["triuI4"] = _f32(np.tile(np.triu(np.ones((128, 128)), 0), (1, 4)))
    consts["id64s"] = _f32(np.tile(np.eye(64), (1, 8)))
    ims = []
    for c in range(8):
        b, q = divmod(c, 4)
        t0 = q * tok
        im = dict(wT)
        im.update(vec_sb)
        im.update(consts)
        im["xin"] = _f32(x[b, t0:t0 + tok, :])
        xp = np.zeros(D, np.float32) if q == 0 else np.asarray(x[b, t0 - 1, :])
        im["xprev"] = _f32(np.asarray(xp).reshape(HB, 128).T)
        m = np.array([1.0 if (s // 4 == b and s % 4 < q) else 0.0
                      for s in range(8)], np.float32)
        im["cmask"] = _f32(np.broadcast_to(m[None, :], (128, 8)))
        isel = np.zeros((64, 8 * 64), np.float32)
        for s in range(8):
            isel[:, s * 64:(s + 1) * 64] = (1.0 - m[s]) * np.eye(64)
        im["isel"] = _f32(isel)
        wout = np.asarray(inputs["L1_attn_Wout"])
        im["WoutH"] = _f32(wout.reshape(NH, DH, D).transpose(1, 0, 2)
                           .reshape(DH, NH * D))
        ims.append(im)
    return ims


def build(tok=TOK, taps=(), stop_after=None):
    assert tok % 1024 == 0
    nch = tok // TC
    ntt = TC // 128
    ndc = tok // CD
    npass = tok // PASS
    taps = set(taps)
    nc = bass.Bass("TRN2", target_bir_lowering=False, debug=False)
    es = ExitStack()

    def din(name, shape, dt=F32):
        return nc.dram_tensor(name, shape, dt, kind="ExternalInput").ap()

    outs = {}

    def dout(name, shape, dt=F32):
        outs[name] = nc.dram_tensor(name, shape, dt, kind="ExternalOutput").ap()
        return outs[name]

    xin = din("xin", [tok, D])
    xprev_d = din("xprev", [128, HB])
    Wd = {k: din(k, list(s)) for k, s in {
        "WdT": (D, D), "WbcT": (D, 2 * NST), "W1T": (D, FF), "W2T": (FF, D),
        "WrT": (D, D), "WkT": (D, D), "WvT": (D, D),
        "Wq": (D, D), "Wk1": (D, D), "Wv1": (D, D), "Wbeta": (D, NH),
        "Wout": (D, D), "w1T": (D, FF), "w2T": (D, FF),
        "h1T": (FF, FF), "h2T": (FF, FF), "w3T": (FF, D)}.items()}
    VEC8 = {k: din(k, [128, HB]) for k in ["ln1_g", "ln1_b", "bd", "D0",
                                           "ln2_g", "ln2_b", "b2", "mu_r",
                                           "mu_k", "w3s"]}
    VECF = {k: din(k, [128, FB]) for k in ["b1", "w1s", "w2s", "h1s", "h2s"]}
    A_sb_d = din("A_sb", [128, HB * NST])
    bbc_col_d = din("bbc_col", [128, 1])
    bbeta_d = din("bbeta_r", [128, NH])
    id_bf_d = din("ident_bf", [128, 128], BF16)
    id_f_d = din("ident_f32", [128, 128])
    trilS_d = din("trilS", [128, 128])
    triuI_d = din("triuI", [128, 128])
    nsel_d = din("nsel", [32, 32 * 128])
    trilS4_d = din("trilS4", [128, 512])
    triuI4_d = din("triuI4", [128, 512])
    id64s_d = din("id64s", [64, 512])
    cmask_d = din("cmask", [128, 8])
    isel_d = din("isel", [64, 8 * 64])
    Wd["WoutH"] = din("WoutH", [64, NH * D])
    out_d = dout("out", [tok, D])

    def scratch(name, shape, dt=F32):
        return nc.dram_tensor(name, shape, dt).ap()

    sp_cumd = scratch("sp_cumd", [nch, 128, HB * TC])
    sp_bc = scratch("sp_bc", [nch, 32, TC])
    mk_sp = dout if "dbg0" in taps else scratch
    sp_u = mk_sp("sp_u", [nch, 128, HB * TC])
    sp_y = mk_sp("sp_y", [nch, 128, HB * TC])
    sp_rp = mk_sp("sp_rp", [nch, 128, HB * TC])
    sp_v = mk_sp("sp_v", [nch, 128, HB * TC])
    if "x1" in taps:
        sp_x1 = dout("sp_x1", [tok, D])
    else:
        sp_x1 = scratch("sp_x1", [tok, D])
    sp_xn1b = scratch("sp_xn1b", [ndc, 128, HB * CD], BF16)
    sp_O1 = (dout if "dbgC" in taps else scratch)("sp_O1", [ndc, 64, NH * CD])
    sp_Qef = (dout if "dbgC" in taps else scratch)("sp_Qef", [ndc, 64, NH * CD], BF16)
    sp_Mp = (dout if "dbgC" in taps else scratch)("sp_Mp", [ndc, 64, NH * DH], BF16)
    sp_Stl = (dout if "dbgC" in taps else scratch)("sp_Stl", [ndc, 64, NH * DH])
    mk_sp2 = dout if "dbgC" in taps else scratch
    sp_Qfm = mk_sp2("sp_Qfm", [ndc, 64, NH * CD], BF16)
    sp_ha = scratch("sp_ha", [FB, 128, tok], BF16)
    sp_hb = scratch("sp_hb", [FB, 128, tok], BF16)
    sp_hc = scratch("sp_hc", [FB, 128, tok], BF16)
    WB = {k: scratch("wb_" + k, list(s), BF16) for k, s in {
        "W1T": (D, FF), "W2T": (FF, D), "w1T": (D, FF), "w2T": (D, FF),
        "h1T": (FF, FF), "h2T": (FF, FF), "w3T": (FF, D)}.items()}
    ag0_in = scratch("ag0_in", [128, 256])
    ag0_out = scratch("ag0_out", [8, 128, 256])
    ag1_in = scratch("ag1_in", [64, 2 * NH * DH], BF16)
    ag1_out = scratch("ag1_out", [8, 64, 2 * NH * DH], BF16)

    def sb(name, shape, dt=F32):
        return es.enter_context(nc.sbuf_tensor(name, shape, dt))

    ARENA_COLS = 44544

    with es:
        pr = Prog(nc, es)
        vec_t = {k: sb("v_" + k, [128, HB]) for k in VEC8}
        vecf_t = {k: sb("vf_" + k, [128, FB]) for k in VECF}
        A_sb = sb("A_sbuf", [128, HB * NST])
        bbc_col = sb("bbc_colt", [128, 1])
        bbeta_t = sb("bbeta_t", [128, NH])
        id_bf = sb("id_bf", [128, 128], BF16)
        id_f = sb("id_f", [128, 128])
        trilS = sb("trilSt", [128, 128])
        triuI = sb("triuIt", [128, 128])
        cmask_t = sb("cmask_t", [128, 8])
        trilS4 = sb("trilS4t", [128, 512])
        triuI4 = sb("triuI4t", [128, 512])
        id64s = sb("id64st", [64, 512])
        isel_t = sb("isel_t", [64, 8 * 64])
        epsv = sb("epsv", [128, 1])
        xprev_t = sb("xprev_t", [128, HB])
        carry8 = sb("carry8", [128, 8])
        hcarry = sb("hcarry", [128, 128])
        dloc = sb("dloc", [128, 128])
        hin = sb("hin", [128, 128])
        stat = sb("stat", [128, 16])
        sstat = sb("sstat", [128, 8])
        sqs = sb("sqs", [128, D])
        StIn = sb("StIn", [64, NH * DH])
        StInB = sb("StInB", [64, NH * DH], BF16)
        arena_t = sb("arena", [128, ARENA_COLS])
        ar = Arena(arena_t, ARENA_COLS)

        pg = [es.enter_context(nc.psum_tensor(f"pg{i}", [128, 512], F32))
              for i in range(8)]
        pb = [pg[6], pg[7]]
        pt = pg[5]
        ptb = pg[5][:, 256:512].bitcast(BF16)

        nsel_t = ar.alloc(32 * 128)[0:32, :]
        for dst, src in [(A_sb[:], A_sb_d), (bbc_col[:], bbc_col_d),
                         (bbeta_t[:], bbeta_d), (id_bf[:], id_bf_d),
                         (id_f[:], id_f_d), (trilS[:], trilS_d),
                         (triuI[:], triuI_d), (nsel_t, nsel_d),
                         (trilS4[:], trilS4_d), (triuI4[:], triuI4_d),
                         (id64s[:], id64s_d),
                         (cmask_t[:], cmask_d), (isel_t[:], isel_d),
                         (xprev_t[:], xprev_d)]:
            pr.dma("sync", dst, src, "sX")
        for k in VEC8:
            pr.dma("sync", vec_t[k][:], VEC8[k], "sX")
        for k in VECF:
            pr.dma("sync", vecf_t[k][:], VECF[k], "sX")

        ar.push()
        wd_b = ar.bf16(HB * D)
        wbc_b = ar.bf16(HB * 2 * NST)
        wr_b = ar.bf16(HB * D)
        wk_b = ar.bf16(HB * D)
        wv_b = ar.bf16(HB * D)
        xtm = ar.f32(D)
        xntm = ar.f32(D)
        xn_fm = ar.f32(HB * (TC + 1))
        xn1_fm = ar.f32(HB * TC)
        xn1b_fm = ar.bf16(HB * TC)
        delta0 = ar.f32(HB * TC)
        dx0 = ar.f32(HB * TC)
        cumd = ar.f32(HB * TC)
        u_fm = ar.f32(HB * TC)
        y_fm = ar.f32(HB * TC)
        xr_b = ar.bf16(HB * TC)
        xk_b = ar.bf16(HB * TC)
        kact_b = ar.bf16(HB * TC)
        rpv_st = ar.f32(2 * TC)
        bc_ch = ar.alloc(TC)[0:32, :]
        a_scr = ar.f32(3 * TC)
        zring_b = ar.f32(3 * TC)
        zring = [zring_b[:, _zi * TC:(_zi + 1) * TC] for _zi in range(3)]
        b_scr = ar.f32(TC)
        h_scr = ar.f32(2 * TC)
        z_scr = b_scr

        for name, dst, w in [("WdT", wd_b, D), ("WbcT", wbc_b, 2 * NST),
                             ("WrT", wr_b, D), ("WkT", wk_b, D),
                             ("WvT", wv_b, D)]:
            for kb in range(HB):
                pr.dma("gpsimd", dst[:, kb * w:(kb + 1) * w],
                       Wd[name][kb * 128:(kb + 1) * 128, :], "sW")
        p_done = {}
        for name in ["W1T", "W2T", "w1T", "w2T", "w3T", "h1T", "h2T"]:
            rows = Wd[name].shape[0]
            step = 512
            for r0 in range(0, rows, step):
                pr.dma("gpsimd", WB[name][r0:r0 + step, :],
                       Wd[name][r0:r0 + step, :], "sP")
            p_done[name] = pr.current("sP")
        pr.wait("vector", "sX")
        pr.op("vector", lambda v: v.memset(epsv[:], EPS_LN))
        pr.op("vector", lambda v: v.memset(hcarry[:], 0.0), "sV")
        pr.wait("scalar", "sX")
        pr.op("scalar", lambda s: s.activation(A_sb[:], A_sb[:], AF.Exp), "sS")
        pr.wait("vector", "sS")
        pr.op("vector", lambda v: v.tensor_scalar_mul(A_sb[:], A_sb[:], -1.0), "sV")
        pr.wait("scalar", "sV")

        tap_jobs = []

        def tap(name, tile, shape):
            if name in taps:
                dd = dout("tap_" + name, shape)
                tap_jobs.append((dd, tile))


        def checkpoint(name):
            if stop_after == name:
                pr.wait("sync", "sV")
                pr.wait("sync", "sEv")
                pr.wait("sync", "sS")
                pr.wait("sync", "sSp")
                for dd, tile in tap_jobs:
                    pr.dma("sync", dd, tile[:], "sO")
                pr.dma("sync", out_d[0:128, :], sqs[:], "sO")
                pr.wait("sync", "sO")
                pr.emit(es.enter_context(nc.Block()))
                return True
            return False

        def mm(pj, lhsT, rhs, start, stop, sem=None):
            return pr.op("tensor",
                         lambda t, pj=pj, l=lhsT, r=rhs, a=start, o=stop:
                         t.matmul(pj, l, r, start=a, stop=o), sem)

        def sv(fn):
            pr.op("vector", fn, "sV")
            pr.wait("vector", "sV")

        def tm_ln(xt, c0, sst_col, out, gb=None):
            sv(lambda v: v.reduce_sum(stat[:, c0:c0 + 1], xt, axis=AXX))
            pr.wait("scalar", "sV")
            pr.op("scalar", lambda s: s.activation(sqs[:], xt, AF.Square), "sS")
            pr.wait("vector", "sS")
            sv(lambda v: v.reduce_sum(stat[:, c0 + 1:c0 + 2], sqs[:], axis=AXX))
            sv(lambda v: v.tensor_scalar(stat[:, c0:c0 + 1], stat[:, c0:c0 + 1],
                                         1.0 / D, None, ALU.mult))
            sv(lambda v: v.tensor_scalar(stat[:, c0 + 1:c0 + 2],
                                         stat[:, c0 + 1:c0 + 2], 1.0 / D, None,
                                         ALU.mult))
            sv(lambda v: v.tensor_mul(stat[:, c0 + 2:c0 + 3], stat[:, c0:c0 + 1],
                                      stat[:, c0:c0 + 1]))
            sv(lambda v: v.tensor_sub(stat[:, c0 + 1:c0 + 2],
                                      stat[:, c0 + 1:c0 + 2],
                                      stat[:, c0 + 2:c0 + 3]))
            pr.wait("scalar", "sV")

            def f(s):
                s.activation(sstat[:, sst_col:sst_col + 1],
                             stat[:, c0 + 1:c0 + 2], AF.Ln, bias=epsv[:])
                return s.activation(sstat[:, sst_col:sst_col + 1],
                                    sstat[:, sst_col:sst_col + 1], AF.Exp,
                                    scale=-0.5)
            pr.op("scalar", f, "sS")
            pr.wait("vector", "sS")
            sv(lambda v: v.tensor_mul(stat[:, c0 + 3:c0 + 4], stat[:, c0:c0 + 1],
                                      sstat[:, sst_col:sst_col + 1]))
            sv(lambda v: v.tensor_scalar_mul(stat[:, c0 + 3:c0 + 4],
                                             stat[:, c0 + 3:c0 + 4], -1.0))
            sv(lambda v: v.tensor_scalar(out, xt, sstat[:, sst_col:sst_col + 1],
                                         stat[:, c0 + 3:c0 + 4], ALU.mult,
                                         ALU.add))

        last_transpose = [0]

        for ch in range(nch):
            t0 = ch * TC
            dcur = delta0
            dxc = dx0

            for tt in range(ntt):
                row = t0 + tt * 128
                pr.wait("sync", "sT", last_transpose[0])
                pr.dma("sync", xtm[:], xin[row:row + 128, :], "sX")
                xv = pr.current("sX")
                pr.wait("vector", "sX", xv)
                pr.wait("scalar", "sX", xv)
                tm_ln(xtm[:], 0, 0, xntm[:])
                pr.wait("scalar", "sV")
                tm_ln(xntm[:], 4, 4, xtm[:])
                vready = pr.current("sV")

                pr.wait("tensor", "sV", vready)
                for hb in range(HB):
                    pr.wait("tensor", "sEv", pr.current("sEv"))
                    pr.op("tensor", lambda t, hb=hb: t.transpose(
                        pt[:, 0:128], xntm[:, hb * 128:(hb + 1) * 128], id_f[:]))
                    tv = pr.op("tensor", lambda t, hb=hb: t.transpose(
                        pt[:, 128:256], xtm[:, hb * 128:(hb + 1) * 128],
                        id_f[:]), "sT")
                    pr.wait("vector", "sT", tv)
                    c0 = hb * (TC + 1) + 1 + tt * 128
                    c1 = hb * TC + tt * 128

                    def f(v, c0=c0, c1=c1, hb=hb):
                        v.tensor_copy(xn_fm[:, c0:c0 + 128], pt[:, 0:128])
                        v.tensor_scalar(xn1_fm[:, c1:c1 + 128], pt[:, 128:256],
                                        vec_t["ln1_g"][:, hb:hb + 1],
                                        vec_t["ln1_b"][:, hb:hb + 1],
                                        ALU.mult, ALU.add)
                        return v.tensor_copy(xn1b_fm[:, c1:c1 + 128],
                                             xn1_fm[:, c1:c1 + 128])
                    pr.op("vector", f, "sEv")
                last_transpose[0] = pr.current("sT")

            pr.wait("vector", "sSp")

            def f(v):
                last = None
                for hb in range(HB):
                    cb = hb * (TC + 1)
                    last = v.tensor_copy(xn_fm[:, cb:cb + 1], xprev_t[:, hb:hb + 1])
                return last
            pr.op("vector", f, "sV")
            pr.wait("vector", "sV")

            def f(v):
                for hb in range(HB):
                    cb = hb * (TC + 1)
                    c1 = hb * TC
                    v.scalar_tensor_tensor(u_fm[:, c1:c1 + TC],
                                           xn1_fm[:, c1:c1 + TC],
                                           vec_t["D0"][:, hb:hb + 1],
                                           xn_fm[:, cb + 1:cb + 1 + TC],
                                           ALU.mult, ALU.add)
                    v.tensor_sub(z_scr[:, 0:TC], xn_fm[:, cb + 1:cb + 1 + TC],
                                 xn_fm[:, cb:cb + TC])
                    v.scalar_tensor_tensor(xr_b[:, c1:c1 + TC], z_scr[:, 0:TC],
                                           vec_t["mu_r"][:, hb:hb + 1],
                                           xn_fm[:, cb:cb + TC], ALU.mult, ALU.add)
                    v.scalar_tensor_tensor(xk_b[:, c1:c1 + TC], z_scr[:, 0:TC],
                                           vec_t["mu_k"][:, hb:hb + 1],
                                           xn_fm[:, cb:cb + TC], ALU.mult, ALU.add)
                    v.tensor_copy(xprev_t[:, hb:hb + 1],
                                  xn_fm[:, cb + TC:cb + TC + 1])
                return v.engine_nop()
            pr.op("vector", f, "sV")
            pr.wait("sync", "sV")
            pr.dma("sync", sp_u[ch], u_fm[:], "sSp")

            pr.wait("tensor", "sW")

            def gemm_fm(evac, wsb, width, rhs_sb, bank0=0):
                for o in range(width // 128):
                    bank = pg[(bank0 + o) % 4]
                    pr.wait("tensor", "sEv", pr.current("sEv"))
                    for kb in range(HB):
                        mm(bank[:, 0:TC],
                           wsb[:, kb * width + o * 128: kb * width + o * 128 + 128],
                           rhs_sb[:, kb * TC:(kb + 1) * TC],
                           kb == 0, kb == HB - 1, "sT" if kb == HB - 1 else None)
                    evac(o, bank[:, 0:TC], pr.current("sT"))

            def evac_delta(o, pap, tval):
                pr.wait("scalar", "sT", tval)

                def f(s, o=o, pap=pap):
                    s.activation(sqs[:, 0:TC], pap, AF.Exp,
                                 bias=vec_t["bd"][:, o:o + 1])
                    return s.activation(dcur[:, o * TC:(o + 1) * TC],
                                        sqs[:, 0:TC], AF.Ln, bias=1.0)
                pr.op("scalar", f, "sEv")
            gemm_fm(evac_delta, wd_b, D, xn1b_fm, 0)

            pr.wait("tensor", "sEv", pr.current("sEv"))
            for kb in range(HB):
                mm(pg[0][0:32, 0:TC], wbc_b[:, kb * 32:(kb + 1) * 32],
                   xn1b_fm[:, kb * TC:(kb + 1) * TC], kb == 0, kb == HB - 1,
                   "sT" if kb == HB - 1 else None)
            pr.wait("vector", "sT")
            pr.wait("vector", "sSp")
            pr.op("vector", lambda v: v.tensor_scalar_add(
                bc_ch, pg[0][0:32, 0:TC], bbc_col[0:32, :]), "sEv")
            pr.wait("sync", "sEv")
            pr.dma("sync", sp_bc[ch], bc_ch, "sSp")

            pr.wait("tensor", "sV")

            def evac_spill(dst_dram):
                def evac(o, pap, tval):
                    slot = rpv_st[:, (o % 2) * TC:(o % 2 + 1) * TC]
                    pr.wait("scalar", "sSp", max(0, pr.current("sSp") - 16))
                    pr.wait("scalar", "sT", tval)
                    ev = pr.op("scalar", lambda s, slot=slot, pap=pap:
                               s.activation(slot, pap, AF.Copy), "sEv")
                    pr.wait("sync", "sEv", ev)
                    pr.dma("sync", dst_dram[:, o * TC:(o + 1) * TC], slot, "sSp")
                return evac
            gemm_fm(evac_spill(sp_rp[ch]), wr_b, D, xr_b, 1)

            def evac_kact(o, pap, tval):
                pr.wait("scalar", "sT", tval)

                def f(s, o=o, pap=pap):
                    s.activation(sqs[:, 0:TC], pap, AF.Relu)
                    return s.activation(kact_b[:, o * TC:(o + 1) * TC],
                                        sqs[:, 0:TC], AF.Square)
                pr.op("scalar", f, "sEv")
            gemm_fm(evac_kact, wk_b, D, xk_b, 2)
            gemm_fm(evac_spill(sp_v[ch]), wv_b, D, kact_b, 3)

            pr.wait("vector", "sEv")

            def f(v):
                for hb in range(HB):
                    c = hb * TC
                    v.tensor_mul(dxc[:, c:c + TC], dcur[:, c:c + TC],
                                 xn1_fm[:, c:c + TC])
                return v.engine_nop()
            pr.op("vector", f, "sV")
            pr.wait("vector", "sSp")
            if ch > 0:
                def f(v):
                    last = None
                    for hb in range(HB):
                        last = v.tensor_copy(carry8[:, hb:hb + 1],
                                             cumd[:, hb * TC + TC - 1:hb * TC + TC])
                    return last
                pr.op("vector", f, "sV")
                pr.wait("vector", "sV")

            def f(v, ch=ch):
                for hb in range(HB):
                    c = hb * TC
                    ini = carry8[:, hb:hb + 1] if ch > 0 else 0.0
                    v.tensor_tensor_scan(cumd[:, c:c + TC], dcur[:, c:c + TC],
                                         dcur[:, c:c + TC], ini, ALU.add,
                                         ALU.bypass)
                return v.engine_nop()
            pr.op("vector", f, "sV")
            pr.wait("sync", "sV")
            pr.dma("sync", sp_cumd[ch], cumd[:], "sSp")

            pr.wait("vector", "sSp")
            pr.wait("vector", "sG2")
            sc_base = pr.current("sSc")
            for n in range(NST):
                bank = pb[n % 2]
                if n == 0:
                    gv = sc_base - 8
                elif n == 1:
                    gv = sc_base
                else:
                    gv = sc_base + 8 * (n - 1)
                pr.wait("tensor", "sSc", max(0, gv))
                selB = nsel_t[:, n * 128:(n + 1) * 128]
                selC = nsel_t[:, (NST + n) * 128:(NST + n + 1) * 128]
                mm(bank[:, 0:TC], selB, bc_ch, True, True)
                btv = mm(bank[:, TC:2 * TC], selC, bc_ch, True,
                         True, "sT")
                for hb in range(HB):
                    idx = hb * NST + n
                    col = n * HB + hb
                    asl = a_scr[:, (col % 3) * TC:(col % 3 + 1) * TC]
                    pr.wait("scalar", "sSc", max(0, pr.current("sSc") - 2))
                    svv = pr.op("scalar", lambda s, asl=asl, hb=hb, idx=idx:
                                s.activation(asl, dcur[:, hb * TC:hb * TC + TC],
                                             AF.Exp, scale=A_sb[:, idx:idx + 1]),
                                "sS")
                    pr.wait("vector", "sT", btv)
                    pr.wait("vector", "sS", svv)
                    bsl = b_scr[:, 0:TC]
                    hsl = h_scr[:, (col % 2) * TC:(col % 2 + 1) * TC]

                    zsl = zring[col % 3]
                    if n > 0:
                        pr.wait("vector", "sG2", max(0, pr.current("sG2") - 2))

                    def f(v, bsl=bsl, hsl=hsl, asl=asl, hb=hb, n=n, col=col,
                          bank=bank, zsl=zsl):
                        if col >= 2:
                            pc = col - 2
                            v.tensor_copy(hcarry[:, pc:pc + 1],
                                          h_scr[:, (pc % 2) * TC + TC - 1:
                                                (pc % 2) * TC + TC])
                        v.tensor_mul(bsl, dxc[:, hb * TC:hb * TC + TC],
                                     bank[:, 0:TC])
                        v.tensor_tensor_scan(hsl, asl, bsl,
                                             hcarry[:, col:col + 1],
                                             ALU.mult, ALU.add)
                        if n == 0:
                            return v.tensor_mul(y_fm[:, hb * TC:hb * TC + TC],
                                                hsl, bank[:, TC:2 * TC])
                        return v.tensor_mul(zsl, hsl, bank[:, TC:2 * TC])
                    pr.op("vector", f, "sSc")
                    if n > 0:
                        pr.wait("gpsimd", "sSc")
                        pr.op("gpsimd", lambda g, hb=hb, zsl=zsl: g.tensor_add(
                            y_fm[:, hb * TC:hb * TC + TC],
                            y_fm[:, hb * TC:hb * TC + TC], zsl), "sG2")
            pr.wait("vector", "sSc")

            def f(v):
                last = None
                for pc in (NST * HB - 2, NST * HB - 1):
                    last = v.tensor_copy(hcarry[:, pc:pc + 1],
                                         h_scr[:, (pc % 2) * TC + TC - 1:
                                               (pc % 2) * TC + TC])
                return last
            pr.op("vector", f, "sV")
            pr.wait("sync", "sSc")
            pr.wait("sync", "sG2")
            pr.dma("sync", sp_y[ch], y_fm[:], "sSp")

        pr.wait("scalar", "sV")
        pr.wait("scalar", "sSc")

        def f(s):
            last = None
            for n in range(NST):
                for hb in range(HB):
                    idx = hb * NST + n
                    col = n * HB + hb
                    last = s.activation(dloc[:, col:col + 1],
                                        cumd[:, hb * TC + TC - 1:hb * TC + TC],
                                        AF.Exp, scale=A_sb[:, idx:idx + 1])
            return last
        pr.op("scalar", f, "sS")

        tap("hcarry", hcarry, [128, 128])
        tap("dloc", dloc, [128, 128])

        if stop_after == "A":
            pr.wait("sync", "sS")
            pr.wait("sync", "sV")
            for dd, tile in tap_jobs:
                pr.dma("sync", dd, tile[:], "sO")
            pr.dma("sync", out_d[0:128, :], xtm[:], "sO")
            pr.wait("sync", "sO")
            pr.emit(es.enter_context(nc.Block()))
            return nc, outs

        if checkpoint("A2"):
            return nc, outs
        pr.wait("gpsimd", "sV")
        pr.wait("gpsimd", "sS")
        pr.wait("gpsimd", "sSp")
        agout = ar.f32(256)
        pr.dma("gpsimd", ag0_in[:, 0:128], hcarry[:], "sG")
        pr.dma("gpsimd", ag0_in[:, 128:256], dloc[:], "sG")
        pr.wait("gpsimd", "sG")
        pr.op("gpsimd", lambda g: g.collective_compute(
            "AllGather", ALU.bypass, replica_groups=[list(range(8))],
            ins=[ag0_in.opt()], outs=[ag0_out.opt()]), "sCC")
        pr.wait("gpsimd", "sCC")
        pr.op("vector", lambda v: v.memset(hin[:], 0.0), "sV")
        for s in range(8):
            pr.wait("gpsimd", "sV")
            pr.dma("gpsimd", agout[:, 0:256], ag0_out[s], "sG")
            gv = pr.current("sG")
            pr.wait("vector", "sG", gv)
            hs = agout[:, 0:128]
            ds = agout[:, 128:256]

            def f(v, hs=hs, ds=ds, s=s):
                v.tensor_scalar(h_scr[:, 0:128], ds, 1.0, None, ALU.subtract)
                v.tensor_scalar(h_scr[:, 0:128], h_scr[:, 0:128],
                                cmask_t[:, s:s + 1], 1.0, ALU.mult, ALU.add)
                v.tensor_mul(hin[:], hin[:], h_scr[:, 0:128])
                v.tensor_scalar(h_scr[:, 128:256], hs, cmask_t[:, s:s + 1],
                                None, ALU.mult)
                return v.tensor_add(hin[:], hin[:], h_scr[:, 128:256])
            pr.op("vector", f, "sV")
        pr.wait("vector", "sV")
        tap("hin", hin, [128, 128])


        if checkpoint("AG0"):
            return nc, outs
        ar.pop()
        ar.push()
        xn2b = ar.bf16(HB * PASS)
        hff = ar.bf16(FB * PASS)
        uch = ar.f32(HB * TC)
        ych = ar.f32(HB * TC)
        cumd_b = ar.f32(HB * TC)
        w_st = [ar.bf16(1024) for _ in range(8)]
        rv_b = ar.f32(2 * HB * TC)
        ssr = ych
        sq_b = ar.f32(TC)
        a_b = ar.f32(3 * TC)
        z_b = ar.f32(2 * TC)
        rows = ar.f32(2 * TC)
        bc_b = ar.alloc(TC)[0:32, :]
        zringB_b = ar.f32(3 * TC)
        zringB = [zringB_b[:, _zi * TC:(_zi + 1) * TC] for _zi in range(3)]
        sgt = ar.f32(D)
        xtm_b = ar.f32(D)
        ones_col = triuI[:, 127:128]
        ones_row = triuI[0:1, 0:128]

        for p in range(npass):
            for ci in range(PASS // TC):
                ch = p * (PASS // TC) + ci
                t0 = ch * TC
                co = ci * TC
                pr.wait("sync", "sV")
                pr.wait("sync", "sSc")
                pr.wait("sync", "sG2")
                pr.wait("sync", "sSp")
                pr.dma("sync", ych[:], sp_y[ch], "sX")
                pr.dma("sync", cumd_b[:], sp_cumd[ch], "sX")
                pr.dma("sync", uch[:], sp_u[ch], "sX")
                pr.wait("sync", "sT", pr.current("sT"))
                pr.dma("sync", bc_b, sp_bc[ch], "sX")
                xv = pr.current("sX")
                pr.wait("vector", "sX", xv)
                pr.wait("scalar", "sX", xv)
                pr.wait("tensor", "sX", xv)
                sc_base = pr.current("sSc")
                for n in range(NST):
                    bank = pb[n % 2]
                    gv = sc_base - 8 if n == 0 else (sc_base if n == 1
                                                     else sc_base + 8 * (n - 1))
                    pr.wait("tensor", "sSc", max(0, gv))
                    selC = nsel_t[:, (NST + n) * 128:(NST + n + 1) * 128]
                    btv = mm(bank[:, 0:TC], selC, bc_b, True, True, "sT")
                    for hb in range(HB):
                        idx = hb * NST + n
                        col = n * HB + hb
                        asl = a_b[:, (col % 3) * TC:(col % 3 + 1) * TC]
                        pr.wait("scalar", "sSc", max(0, pr.current("sSc") - 2))
                        svv = pr.op("scalar", lambda s, asl=asl, hb=hb, idx=idx:
                                    s.activation(asl,
                                                 cumd_b[:, hb * TC:hb * TC + TC],
                                                 AF.Exp,
                                                 scale=A_sb[:, idx:idx + 1]),
                                    "sS")
                        pr.wait("vector", "sT", btv)
                        pr.wait("vector", "sS", svv)

                        zsl = zringB[col % 3]
                        pr.wait("vector", "sG2", max(0, pr.current("sG2") - 2))

                        def f(v, asl=asl, hb=hb, col=col, bank=bank, zsl=zsl):
                            v.tensor_scalar(z_b[:, 0:TC], asl,
                                            hin[:, col:col + 1], None, ALU.mult)
                            return v.tensor_mul(zsl, z_b[:, 0:TC],
                                                bank[:, 0:TC])
                        pr.op("vector", f, "sSc")
                        pr.wait("gpsimd", "sSc")
                        pr.op("gpsimd", lambda g, hb=hb, zsl=zsl: g.tensor_add(
                            ych[:, hb * TC:hb * TC + TC],
                            ych[:, hb * TC:hb * TC + TC], zsl), "sG2")
                pr.wait("vector", "sSc")
                pr.wait("vector", "sG2")

                def f(v):
                    for hb in range(HB):
                        c = hb * TC
                        v.tensor_add(uch[:, c:c + TC], uch[:, c:c + TC],
                                     ych[:, c:c + TC])
                    return v.engine_nop()
                pr.op("vector", f, "sV")
                pr.wait("tensor", "sV")
                pr.wait("tensor", "sEv", pr.current("sEv"))
                for hb in range(HB):
                    mm(pg[0][0:1, 0:TC], ones_col, uch[:, hb * TC:hb * TC + TC],
                       hb == 0, hb == HB - 1, "sT" if hb == HB - 1 else None)
                pr.wait("scalar", "sV")
                for hb in range(HB):
                    if hb > 0:
                        pr.wait("scalar", "sT", pr.current("sT"))
                    stv = pr.op("scalar", lambda s, hb=hb: s.activation(
                        sqs[:, 0:TC], uch[:, hb * TC:hb * TC + TC], AF.Square),
                        "sS")
                    pr.wait("tensor", "sS", stv)
                    mm(pg[1][0:1, 0:TC], ones_col, sqs[:, 0:TC],
                       hb == 0, hb == HB - 1, "sT")
                tv = pr.current("sT")
                pr.wait("vector", "sT", tv)

                def f(v):
                    v.tensor_scalar(rows[0:1, 0:TC], pg[0][0:1, 0:TC],
                                    1.0 / D, None, ALU.mult)
                    v.tensor_scalar(rows[0:1, TC:2 * TC], pg[1][0:1, 0:TC],
                                    1.0 / D, None, ALU.mult)
                    v.tensor_mul(sq_b[0:1, 0:TC], rows[0:1, 0:TC],
                                 rows[0:1, 0:TC])
                    return v.tensor_sub(rows[0:1, TC:2 * TC],
                                        rows[0:1, TC:2 * TC], sq_b[0:1, 0:TC])
                pr.op("vector", f, "sV")
                pr.wait("scalar", "sV")

                def f(s):
                    s.activation(sq_b[0:1, 0:TC], rows[0:1, TC:2 * TC], AF.Ln,
                                 bias=epsv[0:1, :])
                    return s.activation(sq_b[0:1, 0:TC], sq_b[0:1, 0:TC],
                                        AF.Exp, scale=-0.5)
                pr.op("scalar", f, "sS")
                pr.wait("vector", "sS")

                def f(v):
                    v.tensor_mul(rows[0:1, 0:TC], rows[0:1, 0:TC],
                                 sq_b[0:1, 0:TC])
                    return v.tensor_scalar_mul(rows[0:1, 0:TC],
                                               rows[0:1, 0:TC], -1.0)
                pr.op("vector", f, "sV")
                pr.wait("tensor", "sV")
                pr.wait("tensor", "sEv", pr.current("sEv"))
                mm(pg[2][:, 0:TC], ones_row, sq_b[0:1, 0:TC], True, True)
                tv = mm(pg[3][:, 0:TC], ones_row, rows[0:1, 0:TC], True, True,
                        "sT")
                pr.wait("vector", "sT", tv)

                def f(v, co=co):
                    for hb in range(HB):
                        v.tensor_mul(z_b[:, 0:TC], uch[:, hb * TC:hb * TC + TC],
                                     pg[2][:, 0:TC])
                        v.tensor_add(z_b[:, 0:TC], z_b[:, 0:TC],
                                     pg[3][:, 0:TC])
                        v.tensor_scalar(
                            xn2b[:, hb * PASS + co:hb * PASS + co + TC],
                            z_b[:, 0:TC], vec_t["ln2_g"][:, hb:hb + 1],
                            vec_t["ln2_b"][:, hb:hb + 1], ALU.mult, ALU.add)
                    return v.engine_nop()
                pr.op("vector", f, "sEv")
                pr.wait("sync", "sEv")
                pr.dma("sync", sp_u[ch], uch[:], "sSp")

            if "dbgB" in taps and p == 0:
                d_rows = dout("d_rows", [1, 2 * TC])
                pr.wait("sync", "sV")
                pr.wait("sync", "sS")
                pr.dma("sync", d_rows, rows[0:1, :], "sO")
                d_sqb = dout("d_sqb", [1, TC])
                pr.dma("sync", d_sqb, sq_b[0:1, 0:TC], "sO")
                d_xn2b = dout("d_xn2b", [128, HB * PASS], BF16)
                pr.wait("sync", "sEv")
                pr.dma("sync", d_xn2b, xn2b[:], "sO")
            for og in range(4):
                pr.wait("gpsimd", "sT", pr.current("sT"))
                pr.wait("gpsimd", "sP", p_done["W1T"])
                for kb in range(HB):
                    pr.dma("gpsimd", w_st[kb],
                           WB["W1T"][kb * 128:(kb + 1) * 128,
                                     og * 1024:(og + 1) * 1024], "sW")
                wv = pr.current("sW")
                pr.wait("tensor", "sW", wv)
                for fi in range(PASS // 512):
                    fc = fi * 512
                    pr.wait("tensor", "sEv", pr.current("sEv"))
                    for kb in range(HB):
                        for ob in range(8):
                            mm(pg[ob][:, 0:512],
                               w_st[kb][:, ob * 128:(ob + 1) * 128],
                               xn2b[:, kb * PASS + fc:kb * PASS + fc + 512],
                               kb == 0, kb == HB - 1,
                               "sT" if (kb == HB - 1 and ob == 7) else None)
                    tv = pr.current("sT")
                    pr.wait("scalar", "sT", tv)
                    for ob in range(8):
                        o = og * 8 + ob
                        pr.op("scalar", lambda s, o=o, ob=ob, fc=fc:
                              s.activation(hff[:, o * PASS + fc:
                                               o * PASS + fc + 512],
                                           pg[ob][:, 0:512], AF.Gelu,
                                           bias=vecf_t["b1"][:, o:o + 1]),
                              "sEv")

            if "dbgB" in taps and p == 0:
                d_hff = dout("d_hff", [128, FB * PASS], BF16)
                pr.wait("sync", "sEv")
                pr.dma("sync", d_hff, hff[:], "sO")
            for fi in range(PASS // 512):
                fc = fi * 512
                pr.wait("tensor", "sEv", pr.current("sEv"))
                pr.wait("gpsimd", "sT", pr.current("sT"))
                pr.wait("gpsimd", "sP", p_done["W2T"])
                kb_tv = {}
                for kb in range(FB):
                    slot = w_st[kb % 8]
                    if kb >= 8:
                        pr.wait("gpsimd", "sT", kb_tv[kb - 8])
                    pr.dma("gpsimd", slot,
                           WB["W2T"][kb * 128:(kb + 1) * 128, :], "sW")
                    wv = pr.current("sW")
                    pr.wait("tensor", "sW", wv)
                    for ob in range(8):
                        kb_tv[kb] = mm(pg[ob][:, 0:512],
                                       slot[:, ob * 128:(ob + 1) * 128],
                                       hff[:, kb * PASS + fc:
                                           kb * PASS + fc + 512],
                                       kb == 0, kb == FB - 1,
                                       "sT" if ob == 7 else None)
                tv = pr.current("sT")
                for sci in range(512 // TC):
                    ch = p * (PASS // TC) + fi * (512 // TC) + sci
                    t0 = ch * TC
                    scc = sci * TC
                    pr.wait("sync", "sV")
                    pr.dma("sync", rv_b[:, 0:HB * TC], sp_rp[ch], "sX")
                    pr.dma("sync", rv_b[:, HB * TC:2 * HB * TC], sp_v[ch], "sX")
                    pr.dma("sync", ssr[:], sp_u[ch], "sX")
                    xv = pr.current("sX")
                    pr.wait("scalar", "sX", xv)
                    for hb in range(HB):
                        pr.op("scalar", lambda s, hb=hb: s.activation(
                            rv_b[:, hb * TC:hb * TC + TC],
                            rv_b[:, hb * TC:hb * TC + TC], AF.Sigmoid), "sS")
                    pr.wait("vector", "sS")
                    pr.wait("vector", "sX", xv)

                    def f(v):
                        for hb in range(HB):
                            c = hb * TC
                            v.tensor_mul(rv_b[:, c:c + TC], rv_b[:, c:c + TC],
                                         rv_b[:, HB * TC + c:HB * TC + c + TC])
                        return v.engine_nop()
                    pr.op("vector", f, "sV")
                    pr.wait("scalar", "sV")
                    for hb in range(HB):
                        pr.op("scalar", lambda s, hb=hb: s.activation(
                            rv_b[:, hb * TC:hb * TC + TC],
                            rv_b[:, hb * TC:hb * TC + TC], AF.Silu), "sS")
                    gsv = pr.current("sS")
                    pr.wait("vector", "sS", gsv)
                    pr.wait("vector", "sT", tv)

                    def f(v, scc=scc):
                        for hb in range(HB):
                            c = hb * TC
                            v.scalar_tensor_tensor(z_b[:, 0:TC],
                                                   pg[hb][:, scc:scc + TC],
                                                   vec_t["b2"][:, hb:hb + 1],
                                                   ssr[:, c:c + TC],
                                                   ALU.add, ALU.add)
                            v.tensor_mul(ssr[:, c:c + TC], z_b[:, 0:TC],
                                         rv_b[:, c:c + TC])
                        return v.engine_nop()
                    pr.op("vector", f, "sEv")
                    for tt2 in range(TC // 128):
                        row = t0 + tt2 * 128
                        pr.wait("sync", "sO")
                        pr.dma("sync", xtm_b[:], xin[row:row + 128, :], "sX")
                        xv2 = pr.current("sX")
                        for hb in range(HB):
                            pr.wait("tensor", "sEv", pr.current("sEv"))
                            ttv = pr.op("tensor", lambda t, hb=hb, tt2=tt2:
                                        t.transpose(
                                            pt[:, 0:128],
                                            ssr[:, hb * TC + tt2 * 128:
                                                hb * TC + tt2 * 128 + 128],
                                            id_f[:]), "sT")
                            pr.wait("vector", "sT", ttv)
                            pr.op("vector", lambda v, hb=hb: v.tensor_copy(
                                sgt[:, hb * 128:(hb + 1) * 128], pt[:, 0:128]),
                                "sEv")
                        pr.wait("vector", "sX", xv2)
                        pr.wait("vector", "sEv")
                        pr.op("vector", lambda v: v.tensor_add(
                            xtm_b[:], xtm_b[:], sgt[:]), "sV")
                        pr.wait("sync", "sV")
                        pr.dma("sync", sp_x1[row:row + 128, :], xtm_b[:], "sO")

        if checkpoint("B"):
            return nc, outs
        ar.pop()
        ar.push()
        wq_b = ar.bf16(HB * D)
        wk1_b = ar.bf16(HB * D)
        wv1_b = ar.bf16(HB * D)
        wbeta_b = ar.bf16(HB * NH)
        x1tm = ar.f32(D)
        x1tm2 = ar.f32(D)
        xn1tm = ar.f32(D)
        xn1bL = ar.bf16(HB * CD)
        qfm = ar.bf16(NH * CD)
        khat = ar.bf16(NH * DH)
        vtm = ar.bf16(NH * DH)
        kfm = ar.bf16(NH * CD)
        beta_c = ar.f32(NH)
        rstdk = ar.f32(NH)
        negb_c = ar.f32(NH)
        kss = ar.f32(NH)
        eps24 = ar.f32(4)
        M_all = ar.bf16(NH * 128)
        MT_all = ar.bf16(NH * 128)
        M2_all = ar.bf16(NH * 128)
        MT2_all = ar.bf16(NH * 128)
        Y_all = ar.bf16(NH * 128)
        W2T_all = ar.bf16(NH * 128)
        ImG_all = ar.bf16(NH * 64)
        ImGT_all = ar.bf16(NH * 64)
        Mp_run = ar.bf16(NH * DH)
        MpT_run = ar.bf16(NH * DH)
        St_run = ar.f32(NH * DH)
        StB_run = ar.bf16(NH * DH)
        O1ch = ar.f32(NH * CD)
        PWch = ar.bf16(NH * CD)
        agst = ar.bf16(2 * NH * DH)
        csc = ar.f32(512)

        for name, dst, w in [("Wq", wq_b, D), ("Wk1", wk1_b, D),
                             ("Wv1", wv1_b, D), ("Wbeta", wbeta_b, NH)]:
            for kb in range(HB):
                pr.dma("gpsimd", dst[:, kb * w:(kb + 1) * w],
                       Wd[name][kb * 128:(kb + 1) * 128, :], "sW")
        wv_c = pr.current("sW")
        pr.op("vector", lambda v: v.memset(eps24[:], 1e-24))
        pr.op("vector", lambda v: v.memset(St_run[0:64, :], 0.0))
        pr.op("vector", lambda v: v.memset(StB_run[0:64, :], 0.0))

        def f(v):
            for h in range(NH):
                v.tensor_copy(Mp_run[0:64, h * DH:(h + 1) * DH],
                              id_bf[0:64, 0:64])
                v.tensor_copy(MpT_run[0:64, h * DH:(h + 1) * DH],
                              id_bf[0:64, 0:64])
            return v.engine_nop()
        pr.op("vector", f, "sV")
        pr.wait("vector", "sV")

        x1bufs = [x1tm, x1tm2]
        pr.wait("sync", "sV")
        pr.wait("sync", "sEv")
        pr.dma("sync", x1bufs[0][:], sp_x1[0:CD, :], "sX")
        vv_ln = {}
        for dc in range(ndc):
            r0 = dc * CD
            if dc + 1 < ndc:
                pr.wait("sync", "sV", vv_ln.get(dc - 1, 0))
                pr.dma("sync", x1bufs[(dc + 1) % 2][:],
                       sp_x1[r0 + CD:r0 + 2 * CD, :], "sX")
            pr.wait("sync", "sV")
            pr.wait("sync", "sEv")
            pr.dma("sync", sp_Mp[dc], Mp_run[0:64, :], "sSp")
            pr.dma("sync", sp_Stl[dc], St_run[0:64, :], "sSp")
            xv = pr.current("sX")
            pr.wait("vector", "sX", xv)
            pr.wait("scalar", "sX", xv)
            tm_ln(x1bufs[dc % 2][:], 0, 0, xn1tm[:])
            vv_ln[dc] = pr.current("sV")
            vready = pr.current("sV")
            pr.wait("tensor", "sV", vready)
            pr.wait("tensor", "sW", wv_c)
            pr.wait("tensor", "sEv", pr.current("sEv"))
            for g in range(2):
                tbank = pg[5] if g == 0 else pg[6]
                for q in range(4):
                    hb = g * 4 + q
                    pr.op("tensor", lambda t, hb=hb, q=q, tbank=tbank:
                          t.transpose(tbank[:, q * 128:(q + 1) * 128],
                                      xn1tm[:, hb * 128:(hb + 1) * 128],
                                      id_f[:]), "sT" if q == 3 else None)
                pr.wait("scalar", "sT", pr.current("sT"))
                pr.op("scalar", lambda s, g=g, tbank=tbank: s.activation(
                    xn1bL[:, g * 512:(g + 1) * 512], tbank[:, 0:512],
                    AF.Copy), "sEv")
            pr.wait("sync", "sEv")
            pr.dma("sync", sp_xn1b[dc], xn1bL[:], "sSp")
            if dc == 0 and checkpoint("Cln"):
                return nc, outs
            pr.wait("tensor", "sEv", pr.current("sEv"))
            for h in range(NH):
                bank = pg[h // 4]
                qc = (h % 4) * CD
                for kb in range(HB):
                    mm(bank[0:64, qc:qc + CD],
                       wq_b[:, kb * D + h * DH:kb * D + h * DH + DH],
                       xn1bL[:, kb * CD:(kb + 1) * CD],
                       kb == 0, kb == HB - 1,
                       "sT" if (kb == HB - 1 and h % 4 == 3) else None)
                if h % 4 == 3:
                    b = h // 4
                    pr.wait("scalar", "sT", pr.current("sT"))
                    pr.op("scalar", lambda s, b=b: s.activation(
                        qfm[0:64, b * 512:(b + 1) * 512],
                        pg[b][0:64, 0:512], AF.Copy), "sEv")
            if dc == 0 and checkpoint("Cq"):
                return nc, outs
            pr.wait("tensor", "sEv", pr.current("sEv"))
            for half in range(2):
                for kb in range(HB):
                    mm(pg[2 + half][:, 0:512], xn1bL[:, kb * CD:(kb + 1) * CD],
                       wk1_b[:, kb * D + half * 512:kb * D + half * 512 + 512],
                       kb == 0, kb == HB - 1, "sT" if kb == HB - 1 else None)
            for half in range(2):
                pr.wait("tensor", "sEv", pr.current("sEv"))
                for kb in range(HB):
                    mm(pg[4][:, 0:512], xn1bL[:, kb * CD:(kb + 1) * CD],
                       wv1_b[:, kb * D + half * 512:kb * D + half * 512 + 512],
                       kb == 0, kb == HB - 1, "sT" if kb == HB - 1 else None)
                tv = pr.current("sT")
                pr.wait("scalar", "sT", tv)
                pr.op("scalar", lambda s, half=half: s.activation(
                    vtm[:, half * 512:half * 512 + 512], pg[4][:, 0:512],
                    AF.Copy), "sEv")
            ktv = pr.current("sT")
            for half in range(2):
                pr.wait("scalar", "sT", ktv)
                if half > 0:
                    pr.wait("scalar", "sV")
                pr.op("scalar", lambda s, half=half: s.activation(
                    sqs[:, 0:512], pg[2 + half][:, 0:512], AF.Square), "sS")
                ssv = pr.current("sS")
                pr.wait("vector", "sS", ssv)
                pr.op("vector", lambda v, half=half: v.tensor_reduce(
                    kss[:, half * 8:half * 8 + 8],
                    sqs[:, 0:512].rearrange("p (h d) -> p h d", d=DH),
                    axis=AXX, op=ALU.add), "sV")
            pr.wait("scalar", "sV")

            def f(s):
                s.activation(rstdk[:, 0:16], kss[:, 0:16], AF.Ln,
                             bias=eps24[:, 0:1])
                return s.activation(rstdk[:, 0:16], rstdk[:, 0:16], AF.Exp,
                                    scale=-0.5)
            pr.op("scalar", f, "sS")
            pr.wait("vector", "sS")
            for h in range(NH):
                half = h // 8
                pr.op("vector", lambda v, h=h, half=half: v.tensor_scalar(
                    khat[:, h * DH:(h + 1) * DH],
                    pg[2 + half][:, (h % 8) * DH:(h % 8) * DH + DH],
                    rstdk[:, h:h + 1], None, ALU.mult),
                    "sEv")
            pr.wait("tensor", "sEv", pr.current("sEv"))
            for kb in range(HB):
                mm(pg[4][:, 0:NH], xn1bL[:, kb * CD:(kb + 1) * CD],
                   wbeta_b[:, kb * NH:(kb + 1) * NH], kb == 0, kb == HB - 1,
                   "sT" if kb == HB - 1 else None)
            tv = pr.current("sT")
            pr.wait("vector", "sT", tv)
            pr.op("vector", lambda v: v.tensor_add(beta_c[:, 0:NH],
                                                   pg[4][:, 0:NH],
                                                   bbeta_t[:]), "sV")
            pr.wait("scalar", "sV")
            pr.op("scalar", lambda s: s.activation(beta_c[:, 0:NH],
                                                   beta_c[:, 0:NH],
                                                   AF.Sigmoid), "sS")
            pr.wait("vector", "sS")
            pr.op("vector", lambda v: v.tensor_scalar_mul(negb_c[:, 0:NH],
                                                          beta_c[:, 0:NH],
                                                          -1.0), "sV")
            pr.wait("vector", "sV")
            pr.wait("scalar", "sV")
            if dc == 0 and checkpoint("Ckv"):
                return nc, outs
            for h in range(NH):
                pr.wait("tensor", "sEv", pr.current("sEv"))
                ttv = pr.op("tensor", lambda t, h=h: t.transpose(
                    ptb[0:64, 0:128], khat[:, h * DH:(h + 1) * DH], id_bf[:]),
                    "sT")
                pr.wait("vector", "sT", ttv)
                pr.op("vector", lambda v, h=h: v.tensor_copy(
                    kfm[0:64, h * CD:(h + 1) * CD], ptb[0:64, 0:128]), "sEv")

            if dc == 0 and checkpoint("Ckf"):
                return nc, outs
            def hsl_(t, h):
                return t[0:128, h * 128:(h + 1) * 128]

            pr.wait("tensor", "sEv", pr.current("sEv"))
            for h in range(NH):
                bank = pg[h // 4]
                qc = (h % 4) * 128
                mm(bank[:, qc:qc + 128], kfm[0:64, h * CD:(h + 1) * CD],
                   kfm[0:64, h * CD:(h + 1) * CD], True, True,
                   "sT" if h == NH - 1 else None)
            tv = pr.current("sT")
            pr.wait("vector", "sT", tv)
            for b in range(4):
                pr.op("vector", lambda v, b=b: v.tensor_mul(
                    csc[:, 0:512], pg[b][:, 0:512], trilS4[:]), None)

                def f(v, b=b):
                    last = None
                    for hq in range(4):
                        h = b * 4 + hq
                        last = v.tensor_scalar(
                            hsl_(M_all, h), csc[:, hq * 128:(hq + 1) * 128],
                            negb_c[:, h:h + 1], None, ALU.mult)
                    return last
                pr.op("vector", f, "sEv")

            def f(v):
                last = None
                for h in range(NH):
                    last = v.tensor_scalar(Y_all[0:128, h * 128:h * 128 + 64],
                                           vtm[:, h * DH:(h + 1) * DH],
                                           beta_c[:, h:h + 1], None, ALU.mult)
                    last = v.tensor_scalar(Y_all[0:128, h * 128 + 64:
                                                 h * 128 + 128],
                                           khat[:, h * DH:(h + 1) * DH],
                                           beta_c[:, h:h + 1], None, ALU.mult)
                return last
            pr.op("vector", f, "sEv")
            pr.wait("tensor", "sEv", pr.current("sEv"))
            for r4 in range(4):
                for k4 in range(4):
                    h = r4 * 4 + k4
                    pr.op("tensor", lambda t, h=h, k4=k4: t.transpose(
                        ptb[0:128, k4 * 128:(k4 + 1) * 128], hsl_(M_all, h),
                        id_bf[:]), "sT" if k4 == 3 else None)
                pr.wait("vector", "sT", pr.current("sT"))
                pr.op("vector", lambda v, r4=r4: v.tensor_copy(
                    MT_all[0:128, r4 * 512:(r4 + 1) * 512], ptb[0:128, 0:512]),
                    "sEv")
                if r4 < 3:
                    pr.wait("tensor", "sEv", pr.current("sEv"))
            curM, curMT = M_all, MT_all
            nxtM, nxtMT = M2_all, MT2_all
            for lvl in range(7):
                pr.wait("tensor", "sEv", pr.current("sEv"))
                for h in range(NH):
                    bank = pg[h // 4]
                    qc = (h % 4) * 128
                    mm(bank[:, qc:qc + 128], hsl_(curMT, h), hsl_(Y_all, h),
                       True, True, "sT" if h % 4 == 3 else None)
                    if h % 4 == 3:
                        b = h // 4
                        pr.wait("vector", "sT", pr.current("sT"))
                        pr.op("vector", lambda v, b=b: v.tensor_add(
                            Y_all[0:128, b * 512:(b + 1) * 512],
                            Y_all[0:128, b * 512:(b + 1) * 512],
                            pg[b][:, 0:512]), "sEv")
                yev = pr.current("sEv")
                if lvl < 6:
                    for h in range(NH):
                        bank = pg[4 + h // 4]
                        qc = (h % 4) * 128
                        mm(bank[:, qc:qc + 128], hsl_(curMT, h), hsl_(curM, h),
                           True, True, "sT" if h % 4 == 3 else None)
                        if h % 4 == 3:
                            b = h // 4
                            pr.wait("scalar", "sT", pr.current("sT"))
                            pr.op("scalar", lambda s, b=b, nxtM=nxtM:
                                  s.activation(
                                      nxtM[0:128, b * 512:(b + 1) * 512],
                                      pg[4 + b][:, 0:512], AF.Copy), "sEv")
                    pr.wait("tensor", "sEv", yev)
                    for h in range(NH):
                        bank = pg[h // 4]
                        qc = (h % 4) * 128
                        mm(bank[:, qc:qc + 128], hsl_(curM, h), hsl_(curMT, h),
                           True, True, "sT" if h % 4 == 3 else None)
                        if h % 4 == 3:
                            b = h // 4
                            pr.wait("vector", "sT", pr.current("sT"))
                            pr.op("vector", lambda v, b=b, nxtMT=nxtMT:
                                  v.tensor_copy(
                                      nxtMT[0:128, b * 512:(b + 1) * 512],
                                      pg[b][:, 0:512]), "sEv")
                    curM, nxtM = nxtM, curM
                    curMT, nxtMT = nxtMT, curMT
            pr.wait("tensor", "sEv", pr.current("sEv"))
            for h in range(NH):
                bank = pg[h // 8]
                qc = (h % 8) * 64
                mm(bank[0:64, qc:qc + 64],
                   Y_all[0:128, h * 128 + 64:h * 128 + 128],
                   khat[:, h * DH:(h + 1) * DH], True, True,
                   "sT" if h == NH - 1 else None)
            for h in range(NH):
                bank = pg[2 + h // 8]
                qc = (h % 8) * 64
                mm(bank[0:64, qc:qc + 64], khat[:, h * DH:(h + 1) * DH],
                   Y_all[0:128, h * 128 + 64:h * 128 + 128], True, True,
                   "sT" if h == NH - 1 else None)
            tv = pr.current("sT")
            pr.wait("vector", "sT", tv)
            for b in range(2):
                pr.op("vector", lambda v, b=b: v.tensor_sub(
                    ImG_all[0:64, b * 512:(b + 1) * 512], id64s[:],
                    pg[b][0:64, 0:512]), "sEv")
            for b in range(2):
                pr.op("vector", lambda v, b=b: v.tensor_sub(
                    ImGT_all[0:64, b * 512:(b + 1) * 512], id64s[:],
                    pg[2 + b][0:64, 0:512]), "sEv")
            pr.wait("tensor", "sEv", pr.current("sEv"))
            for h in range(NH):
                bank = pg[h // 4]
                qc = (h % 4) * 128
                mm(bank[:, qc:qc + 128], kfm[0:64, h * CD:(h + 1) * CD],
                   qfm[0:64, h * CD:(h + 1) * CD], True, True,
                   "sT" if h == NH - 1 else None)
            tv = pr.current("sT")
            pr.wait("vector", "sT", tv)
            for b in range(4):
                pr.op("vector", lambda v, b=b: v.tensor_mul(
                    csc[:, 0:512], pg[b][:, 0:512], triuI4[:]), None)
                pr.op("vector", lambda v, b=b: v.tensor_copy(
                    W2T_all[0:128, b * 512:(b + 1) * 512], csc[:, 0:512]),
                    "sEv")
            pr.wait("tensor", "sEv", pr.current("sEv"))
            for h in range(NH):
                bank = pg[h // 8]
                qc = (h % 8) * 64
                mm(bank[0:64, qc:qc + 64],
                   MpT_run[0:64, h * DH:(h + 1) * DH],
                   ImG_all[0:64, h * 64:(h + 1) * 64], True, True,
                   "sT" if h == NH - 1 else None)
            for h in range(NH):
                bank = pg[2 + h // 8]
                qc = (h % 8) * 64
                mm(bank[0:64, qc:qc + 64],
                   ImG_all[0:64, h * 64:(h + 1) * 64],
                   MpT_run[0:64, h * DH:(h + 1) * DH], True, True,
                   "sT" if h == NH - 1 else None)
            for h in range(NH):
                bank = pg[4 + h // 8]
                qc = (h % 8) * 64
                mm(bank[0:64, qc:qc + 64], khat[:, h * DH:(h + 1) * DH],
                   Y_all[0:128, h * 128:h * 128 + 64], True, False)
                mm(bank[0:64, qc:qc + 64],
                   ImG_all[0:64, h * 64:(h + 1) * 64],
                   StB_run[0:64, h * DH:(h + 1) * DH], False, True,
                   "sT" if h == NH - 1 else None)
            tv = pr.current("sT")
            pr.wait("vector", "sT", tv)
            for b in range(2):
                pr.op("vector", lambda v, b=b: v.tensor_copy(
                    Mp_run[0:64, b * 512:(b + 1) * 512], pg[b][0:64, 0:512]),
                    None)
                pr.op("vector", lambda v, b=b: v.tensor_copy(
                    MpT_run[0:64, b * 512:(b + 1) * 512],
                    pg[2 + b][0:64, 0:512]), None)
                pr.op("vector", lambda v, b=b: v.tensor_copy(
                    St_run[0:64, b * 512:(b + 1) * 512],
                    pg[4 + b][0:64, 0:512]), None)
                pr.op("vector", lambda v, b=b: v.tensor_copy(
                    StB_run[0:64, b * 512:(b + 1) * 512],
                    pg[4 + b][0:64, 0:512]), "sEv")
            pr.wait("tensor", "sEv", pr.current("sEv"))
            for h in range(NH):
                bank = pg[h // 4]
                qc = (h % 4) * 128
                mm(bank[0:64, qc:qc + 128],
                   Y_all[0:128, h * 128:h * 128 + 64],
                   W2T_all[0:128, h * 128:(h + 1) * 128], True, True,
                   "sT" if h == NH - 1 else None)
            for h in range(NH):
                bank = pg[4 + h // 4]
                qc = (h % 4) * 128
                mm(bank[0:64, qc:qc + 128],
                   Y_all[0:128, h * 128 + 64:h * 128 + 128],
                   W2T_all[0:128, h * 128:(h + 1) * 128], True, True,
                   "sT" if h == NH - 1 else None)
            tv = pr.current("sT")
            pr.wait("vector", "sT", tv)
            for b in range(4):
                pr.op("vector", lambda v, b=b: v.tensor_copy(
                    O1ch[0:64, b * 512:(b + 1) * 512], pg[b][0:64, 0:512]),
                    None)

                def f(v, b=b):
                    v.tensor_scalar_mul(csc[0:64, 0:512],
                                        pg[4 + b][0:64, 0:512], -1.0)
                    return v.tensor_copy(PWch[0:64, b * 512:(b + 1) * 512],
                                         csc[0:64, 0:512])
                pr.op("vector", f, "sEv")
            pr.wait("sync", "sEv")
            pr.dma("sync", sp_O1[dc], O1ch[0:64, :], "sSp")
            pr.dma("sync", sp_Qef[dc], PWch[0:64, :], "sSp")
            pr.dma("sync", sp_Qfm[dc], qfm[0:64, :], "sSp")

        if checkpoint("C"):
            return nc, outs
        pr.wait("gpsimd", "sEv")
        pr.wait("gpsimd", "sV")
        pr.dma("gpsimd", agst[0:64, 0:NH * DH], St_run[0:64, :], "sG")
        pr.dma("gpsimd", agst[0:64, NH * DH:2 * NH * DH], Mp_run[0:64, :],
               "sG")
        pr.wait("gpsimd", "sG")
        pr.dma("gpsimd", ag1_in, agst[0:64, :], "sG")
        pr.wait("gpsimd", "sG")
        pr.op("gpsimd", lambda g: g.collective_compute(
            "AllGather", ALU.bypass, replica_groups=[list(range(8))],
            ins=[ag1_in.opt()], outs=[ag1_out.opt()]), "sCC")
        pr.wait("gpsimd", "sCC")
        ag1t = ar.bf16(2 * NH * DH)
        msel_f = ar.f32(64)
        msel_b = ar.bf16(64)
        stf = ar.f32(64)
        pr.op("vector", lambda v: v.memset(StIn[0:64, :], 0.0))
        pr.op("vector", lambda v: v.memset(StInB[0:64, :], 0.0), "sV")
        pr.wait("vector", "sV")
        for s in range(8):
            pr.wait("gpsimd", "sEv")
            pr.dma("gpsimd", ag1t[0:64, :], ag1_out[s], "sG")
            pr.wait("vector", "sG")
            base = 0
            for h in range(NH):
                Cs = ag1t[0:64, base + h * DH:base + (h + 1) * DH]
                Ms = ag1t[0:64, base + NH * DH + h * DH:
                          base + NH * DH + (h + 1) * DH]

                def f(v, Cs=Cs, Ms=Ms, s=s):
                    v.tensor_scalar(msel_f[0:64, :], Ms,
                                    cmask_t[0:64, s:s + 1], None, ALU.mult)
                    v.tensor_add(msel_f[0:64, :], msel_f[0:64, :],
                                 isel_t[0:64, s * 64:(s + 1) * 64])
                    return v.tensor_copy(msel_b[0:64, 0:64], msel_f[0:64, :])
                pr.op("vector", f, "sV")
                pr.wait("vector", "sV")
                pr.wait("tensor", "sV")
                pr.wait("tensor", "sEv", pr.current("sEv"))
                tv = mm(pg[0][0:64, 0:64], msel_b[0:64, 0:64],
                        StInB[0:64, h * DH:(h + 1) * DH], True, True, "sT")
                pr.wait("vector", "sT", tv)

                def f(v, Cs=Cs, s=s, h=h):
                    v.tensor_scalar(stf[0:64, :], Cs, cmask_t[0:64, s:s + 1],
                                    None, ALU.mult)
                    v.tensor_add(StIn[0:64, h * DH:(h + 1) * DH],
                                 pg[0][0:64, 0:64], stf[0:64, :])
                    return v.tensor_copy(StInB[0:64, h * DH:(h + 1) * DH],
                                         StIn[0:64, h * DH:(h + 1) * DH])
                pr.op("vector", f, "sEv")
                pr.wait("vector", "sEv")

        if checkpoint("AG1"):
            return nc, outs
        ar.pop()
        ar.push()
        FP = tok
        NFC = FP // 1024
        _xo = ar.off
        xn1f = ar.bf16(HB * FP)
        wstD = [ar.bf16(1024) for _ in range(8)]
        hbuf = ar.bf16(4 * FP)
        hsl2 = [ar.bf16(1024) for _ in range(6)]
        g1b = ar.b[0:128, _xo:_xo + HB * 1024]
        woutD = ar.bf16(NH * D, parts=64)
        MpD = ar.bf16(NH * DH)
        StlD = ar.f32(NH * DH)
        O1D = ar.f32(NH * CD)
        PWD = ar.bf16(NH * CD)
        QfD = ar.bf16(NH * CD)
        StaD = ar.bf16(NH * DH)
        Ofm = ar.bf16(NH * CD)
        s1b = ar.f32(HB * CD)
        x1tmD = ar.f32(D)
        sgtD = ar.f32(D)
        dsc = ar.f32(512)

        pr.wait("sync", "sSp")
        xn1f3 = xn1f.rearrange("p (hb t) -> p hb t", hb=HB)
        for dc in range(ndc):
            pr.dma("sync", xn1f3[:, :, dc * CD:(dc + 1) * CD], sp_xn1b[dc],
                   "sX")
        pr.wait("tensor", "sX")

        w_last = {}
        h_last = {}

        def stage(w_name, scales, act, nk, nob, src_sbuf, src_dram, dst_dram,
                  second=None):
            for ogr in range(nob // 4):
                for fc in range(NFC):
                    f0 = fc * 1024
                    pr.wait("tensor", "sEv", pr.current("sEv"))
                    kb_tv = {}
                    for kb in range(nk):
                        wslot = wstD[kb % 8]
                        pr.wait("gpsimd", "sT",
                                w_last.get(kb % 8, pr.current("sT")))
                        pr.dma("gpsimd", wslot[:, 0:512],
                               WB[w_name][kb * 128:(kb + 1) * 128,
                                          ogr * 512:(ogr + 1) * 512], "sW")
                        if src_dram is not None:
                            hslot = hsl2[kb % 6]
                            pr.wait("sync", "sT",
                                    h_last.get(kb % 6, pr.current("sT")))
                            pr.dma("sync", hslot[:, 0:1024],
                                   src_dram[kb][:, f0:f0 + 1024], "sX")
                            def rr(fi, hslot=hslot):
                                return hslot[:, fi * 512:fi * 512 + 512]
                        else:
                            def rr(fi, kb=kb, f0=f0):
                                return src_sbuf[:, kb * FP + f0 + fi * 512:
                                                kb * FP + f0 + fi * 512 + 512]
                        pr.wait("tensor", "sW", pr.current("sW"))
                        pr.wait("tensor", "sX", pr.current("sX"))
                        for ob in range(4):
                            for fi in range(2):
                                kb_tv[kb] = mm(
                                    pg[2 * ob + fi][:, 0:512],
                                    wslot[:, ob * 128:(ob + 1) * 128],
                                    rr(fi),
                                    kb == 0, kb == nk - 1,
                                    "sT" if (ob == 3 and fi == 1) else None)
                        w_last[kb % 8] = kb_tv[kb]
                        if src_dram is not None:
                            h_last[kb % 6] = kb_tv[kb]
                    tv = pr.current("sT")
                    pr.wait("scalar", "sT", tv)
                    if dst_dram is not None:
                        pr.wait("scalar", "sSp", pr.current("sSp"))
                    for ob in range(4):
                        o = ogr * 4 + ob
                        for fi in range(2):
                            pr.op("scalar", lambda s, o=o, ob=ob, fi=fi:
                                  s.activation(
                                      hbuf[:, ob * FP + fi * 512:
                                           ob * FP + fi * 512 + 512],
                                      pg[2 * ob + fi][:, 0:512], act,
                                      scale=scales[:, o:o + 1]), "sEv")
                    if second is not None:
                        w2n, s2 = second
                        pr.wait("tensor", "sEv", pr.current("sEv"))
                        kb_tv2 = {}
                        for kb in range(nk):
                            wslot = wstD[kb % 8]
                            pr.wait("gpsimd", "sT",
                                    w_last.get(kb % 8, pr.current("sT")))
                            pr.dma("gpsimd", wslot[:, 0:512],
                                   WB[w2n][kb * 128:(kb + 1) * 128,
                                           ogr * 512:(ogr + 1) * 512], "sW")
                            pr.wait("tensor", "sW", pr.current("sW"))
                            for ob in range(4):
                                for fi in range(2):
                                    kb_tv2[kb] = mm(
                                        pg[2 * ob + fi][:, 0:512],
                                        wslot[:, ob * 128:(ob + 1) * 128],
                                        src_sbuf[:, kb * FP + f0 + fi * 512:
                                                 kb * FP + f0 + fi * 512 + 512],
                                        kb == 0, kb == nk - 1,
                                        "sT" if (ob == 3 and fi == 1) else None)
                            w_last[kb % 8] = kb_tv2[kb]
                        tv = pr.current("sT")
                        pr.wait("vector", "sT", tv)
                        for ob in range(4):
                            o = ogr * 4 + ob
                            for fi in range(2):
                                pr.op("vector", lambda v, o=o, ob=ob, fi=fi:
                                      v.scalar_tensor_tensor(
                                          hbuf[:, ob * FP + fi * 512:
                                               ob * FP + fi * 512 + 512],
                                          pg[2 * ob + fi][:, 0:512],
                                          s2[:, o:o + 1],
                                          hbuf[:, ob * FP + fi * 512:
                                               ob * FP + fi * 512 + 512],
                                          ALU.mult, ALU.mult), "sEv")
                    if dst_dram is not None:
                        pr.wait("sync", "sEv")
                        for ob in range(4):
                            o = ogr * 4 + ob
                            pr.dma("sync", dst_dram[o][:, f0:f0 + 1024],
                                   hbuf[:, ob * FP:ob * FP + 1024], "sSp")
            return

        pr.wait("gpsimd", "sP", p_done["h2T"])
        stage("w1T", vecf_t["w1s"], AF.Silu, HB, FB, xn1f, None, sp_ha,
              second=("w2T", vecf_t["w2s"]))
        stage("h1T", vecf_t["h1s"], AF.Gelu, FB, FB, None, sp_ha, sp_hb)
        stage("h2T", vecf_t["h2s"], AF.Gelu, FB, FB, None, sp_hb, sp_hc)

        pr.wait("gpsimd", "sSp")
        for kb in range(HB):
            pr.dma("gpsimd", woutD[0:64, kb * 2048:(kb + 1) * 2048],
                   Wd["WoutH"][:, kb * 2048:(kb + 1) * 2048], "sW")
        wv_d = pr.current("sW")
        for fc in range(NFC):
            f0 = fc * 1024
            for ogr in range(2):
                pr.wait("tensor", "sEv", pr.current("sEv"))
                kb_tv = {}
                for kb in range(FB):
                    wslot = wstD[kb % 8]
                    pr.wait("gpsimd", "sT",
                            w_last.get(kb % 8, pr.current("sT")))
                    pr.dma("gpsimd", wslot[:, 0:512],
                           WB["w3T"][kb * 128:(kb + 1) * 128,
                                     ogr * 512:(ogr + 1) * 512], "sW")
                    hslot = hsl2[kb % 6]
                    pr.wait("sync", "sT",
                            h_last.get(kb % 6, pr.current("sT")))
                    pr.dma("sync", hslot[:, 0:1024], sp_hc[kb][:, f0:f0 + 1024],
                           "sX")
                    pr.wait("tensor", "sW", pr.current("sW"))
                    pr.wait("tensor", "sX", pr.current("sX"))
                    for ob in range(4):
                        for fi in range(2):
                            kb_tv[kb] = mm(
                                pg[2 * ob + fi][:, 0:512],
                                wslot[:, ob * 128:(ob + 1) * 128],
                                hslot[:, fi * 512:fi * 512 + 512],
                                kb == 0, kb == FB - 1,
                                "sT" if (ob == 3 and fi == 1) else None)
                    w_last[kb % 8] = kb_tv[kb]
                    h_last[kb % 6] = kb_tv[kb]
                tv = pr.current("sT")
                pr.wait("scalar", "sT", tv)
                for ob in range(4):
                    o = ogr * 4 + ob
                    for fi in range(2):
                        pr.op("scalar", lambda s, o=o, ob=ob, fi=fi:
                              s.activation(
                                  g1b[:, o * 1024 + fi * 512:
                                      o * 1024 + fi * 512 + 512],
                                  pg[2 * ob + fi][:, 0:512], AF.Silu,
                                  scale=vec_t["w3s"][:, o:o + 1]), "sEv")
            if "dbgD" in taps:
                if fc == 0:
                    d_g1 = dout("d_g1", [NFC, 128, HB * 1024])
                    d_s1 = dout("d_s1", [ndc, 128, HB * CD])
                pr.wait("sync", "sEv")
                pr.dma("sync", d_g1[fc], g1b[:], "sO")
            pr.wait("tensor", "sW", wv_d)
            for dci in range(1024 // CD):
                dc = fc * (1024 // CD) + dci
                r0 = dc * CD
                gco = dci * CD
                pr.wait("sync", "sV")
                pr.dma("sync", MpD[0:64, :], sp_Mp[dc], "sX")
                pr.dma("sync", StlD[0:64, :], sp_Stl[dc], "sX")
                pr.dma("sync", O1D[0:64, :], sp_O1[dc], "sX")
                pr.dma("sync", PWD[0:64, :], sp_Qef[dc], "sX")
                pr.dma("sync", QfD[0:64, :], sp_Qfm[dc], "sX")
                pr.dma("sync", x1tmD[:], sp_x1[r0:r0 + CD, :], "sX")
                xv = pr.current("sX")
                pr.wait("tensor", "sX", xv)
                pr.wait("vector", "sX", xv)
                for h in range(NH):
                    pr.wait("tensor", "sEv", pr.current("sEv"))
                    tv = mm(pg[0][0:64, 0:64], MpD[0:64, h * DH:(h + 1) * DH],
                            StInB[0:64, h * DH:(h + 1) * DH], True, True, "sT")
                    pr.wait("vector", "sT", tv)
                    pr.op("vector", lambda v, h=h: v.tensor_add(
                        dsc[0:64, 0:64], pg[0][0:64, 0:64],
                        StlD[0:64, h * DH:(h + 1) * DH]), None)
                    pr.op("vector", lambda v, h=h: v.tensor_copy(
                        StaD[0:64, h * DH:(h + 1) * DH], dsc[0:64, 0:64]),
                        "sEv")
                    pr.wait("tensor", "sEv", pr.current("sEv"))
                    mm(pg[1][0:64, 0:128], StaD[0:64, h * DH:(h + 1) * DH],
                       QfD[0:64, h * CD:(h + 1) * CD], True, False)
                    tv = mm(pg[1][0:64, 0:128], StaD[0:64, h * DH:(h + 1) * DH],
                            PWD[0:64, h * CD:(h + 1) * CD], False, True, "sT")
                    pr.wait("vector", "sT", tv)

                    def f(v, h=h):
                        v.tensor_add(dsc[0:64, 0:128], pg[1][0:64, 0:128],
                                     O1D[0:64, h * CD:(h + 1) * CD])
                        return v.tensor_copy(Ofm[0:64, h * CD:(h + 1) * CD],
                                             dsc[0:64, 0:128])
                    pr.op("vector", f, "sEv")
                for o in range(HB):
                    bank = pg[2 + (o % 2)]
                    pr.wait("tensor", "sEv", pr.current("sEv"))
                    for h in range(NH):
                        mm(bank[:, 0:CD],
                           woutD[0:64, h * D + o * 128:h * D + o * 128 + 128],
                           Ofm[0:64, h * CD:(h + 1) * CD],
                           h == 0, h == NH - 1, "sT" if h == NH - 1 else None)
                    tv = pr.current("sT")
                    pr.wait("vector", "sT", tv)
                    pr.op("vector", lambda v, o=o, bank=bank: v.tensor_copy(
                        s1b[:, o * CD:(o + 1) * CD], bank[:, 0:CD]), "sEv")
                if "dbgD" in taps:
                    pr.wait("sync", "sEv")
                    pr.dma("sync", d_s1[dc], s1b[:], "sO")
                pr.wait("vector", "sEv")

                def f(v, gco=gco):
                    for hb in range(HB):
                        v.tensor_mul(s1b[:, hb * CD:(hb + 1) * CD],
                                     s1b[:, hb * CD:(hb + 1) * CD],
                                     g1b[:, hb * 1024 + gco:
                                         hb * 1024 + gco + CD])
                    return v.engine_nop()
                pr.op("vector", f, "sV")
                pr.wait("tensor", "sV")
                for hb in range(HB):
                    pr.wait("tensor", "sEv", pr.current("sEv"))
                    ttv = pr.op("tensor", lambda t, hb=hb: t.transpose(
                        pt[:, 0:128], s1b[:, hb * CD:(hb + 1) * CD], id_f[:]),
                        "sT")
                    pr.wait("vector", "sT", ttv)
                    pr.op("vector", lambda v, hb=hb: v.tensor_copy(
                        sgtD[:, hb * 128:(hb + 1) * 128], pt[:, 0:128]), "sEv")
                pr.wait("vector", "sEv")
                pr.op("vector", lambda v: v.tensor_add(x1tmD[:], x1tmD[:],
                                                       sgtD[:]), "sV")
                pr.wait("sync", "sV")
                pr.dma("sync", out_d[r0:r0 + CD, :], x1tmD[:], "sO")
        pr.wait("sync", "sO")

        pr.emit(es.enter_context(nc.Block()))
    return nc, outs


def kernel(**inputs):
    ims = host_prep(inputs, TOK)
    nc, outs = build(TOK)
    res = run_bass_kernel_spmd(nc, ims, core_ids=list(range(8)))
    shards = [np.asarray(res.results[c]["out"]) for c in range(8)]
    full = np.stack([np.concatenate(shards[0:4], 0),
                     np.concatenate(shards[4:8], 0)], 0)
    return np.ascontiguousarray(full, np.float32)


# version 08 (final)
# speedup vs baseline: 1.5595x; 1.0427x over previous
import numpy as np
from contextlib import ExitStack
import concourse.bass as bass
import concourse.mybir as mybir
from concourse.bass_utils import run_bass_kernel_spmd

F32 = mybir.dt.float32
BF16 = mybir.dt.bfloat16
AF = mybir.ActivationFunctionType
ALU = mybir.AluOpType
AXX = mybir.AxisListType.X

D = 1024
HB = 8
NST = 16
FF = 4096
FB = 32
NH = 16
HP = NH // 2
DH = 64
EPS_LN = 1e-5
TOK = 2048
TC = 256
CD = 128
PASS = 1024
ENGS = ("sync", "tensor", "vector", "scalar", "gpsimd")


class Prog:
    def __init__(self, nc, es):
        self.nc, self.es = nc, es
        self.q = {e: [] for e in ENGS}
        self.sems, self.cnt = {}, {}

    def sem(self, name):
        if name not in self.sems:
            self.sems[name] = self.es.enter_context(self.nc.semaphore(name))
            self.cnt[name] = 0
        return self.sems[name]

    def op(self, eng, fn, sem=None, k=1):
        if sem is None:
            self.q[eng].append(fn)
            return None
        s = self.sem(sem)
        self.cnt[sem] += k

        def run(e, fn=fn, s=s, k=k):
            fn(e).then_inc(s, k)
        self.q[eng].append(run)
        return self.cnt[sem]

    def dma(self, eng, out, in_, sem):
        return self.op(eng, lambda e, o=out, i=in_: e.dma_start(out=o, in_=i),
                       sem, 16)

    def wait(self, eng, sem, val=None):
        s = self.sem(sem)
        v = self.cnt[sem] if val is None else val
        if v > 0:
            self.q[eng].append(lambda e, s=s, v=v: e.wait_ge(s, v))

    def current(self, sem):
        self.sem(sem)
        return self.cnt[sem]

    def emit(self, block):
        for e in ENGS:
            fns = self.q[e]
            if fns:
                def body(h, fns=fns):
                    for f in fns:
                        f(h)
                getattr(block, e)(body)


class Arena:

    def __init__(self, backing, cols):
        self.b = backing
        self.cols = cols
        self.off = 0
        self.marks = []

    def push(self):
        self.marks.append(self.off)

    def pop(self):
        self.off = self.marks.pop()

    def alloc(self, cols_f32, dt=F32, parts=128):
        cols_f32 = (cols_f32 + 3) & ~3
        assert self.off + cols_f32 <= self.cols, \
            f"arena overflow: {self.off}+{cols_f32} > {self.cols}"
        ap = self.b[0:parts, self.off:self.off + cols_f32]
        self.off += cols_f32
        if dt == BF16:
            ap = ap.bitcast(BF16)
        return ap

    def f32(self, n_elems_per_part, parts=128):
        return self.alloc(n_elems_per_part, F32, parts)

    def bf16(self, n_elems_per_part, parts=128):
        return self.alloc((n_elems_per_part + 1) // 2, BF16, parts)


def _f32(a):
    return np.ascontiguousarray(np.asarray(a), dtype=np.float32)


def host_prep(inputs, tok=TOK):
    x = np.asarray(inputs["L0_attn_x"])
    wT = {
        "WdT": _f32(np.asarray(inputs["L0_attn_Wd"]).T),
        "WbcT": _f32(np.asarray(inputs["L0_attn_Wbc"]).T),
        "W1T": _f32(np.asarray(inputs["L0_attn_W1"]).T),
        "W2T": _f32(np.asarray(inputs["L0_attn_W2"]).T),
        "WrT": _f32(np.asarray(inputs["L0_ffn_W_r"]).T),
        "WkT": _f32(np.asarray(inputs["L0_ffn_W_k"]).T),
        "WvT": _f32(np.asarray(inputs["L0_ffn_W_v"]).T),
        "Wq": _f32(inputs["L1_attn_Wq"]), "Wk1": _f32(inputs["L1_attn_Wk"]),
        "Wv1": _f32(inputs["L1_attn_Wv"]), "Wbeta": _f32(inputs["L1_attn_Wbeta"]),
        "Wout": _f32(inputs["L1_attn_Wout"]),
        "w1T": _f32(np.asarray(inputs["L1_ffn_w1_w"]).T),
        "w2T": _f32(np.asarray(inputs["L1_ffn_w2_w"]).T),
        "h1T": _f32(np.asarray(inputs["L1_ffn_h1_w"]).T),
        "h2T": _f32(np.asarray(inputs["L1_ffn_h2_w"]).T),
        "w3T": _f32(np.asarray(inputs["L1_ffn_w3_w"]).T),
    }
    fvec = lambda v: _f32(np.asarray(v).reshape(-1, 128).T)
    vecs = {"ln1_g": "L0_attn_ln1_g", "ln1_b": "L0_attn_ln1_b",
            "bd": "L0_attn_bd", "D0": "L0_attn_D", "ln2_g": "L0_attn_ln2_g",
            "ln2_b": "L0_attn_ln2_b", "b1": "L0_attn_b1", "b2": "L0_attn_b2",
            "mu_r": "L0_ffn_mu_r", "mu_k": "L0_ffn_mu_k",
            "w1s": "L1_ffn_w1_s", "w2s": "L1_ffn_w2_s", "h1s": "L1_ffn_h1_s",
            "h2s": "L1_ffn_h2_s", "w3s": "L1_ffn_w3_s"}
    vec_sb = {k: fvec(inputs[v]) for k, v in vecs.items()}
    A_log = np.asarray(inputs["L0_attn_A_log"])
    A_sb = _f32(A_log.reshape(HB, 128, NST).transpose(1, 0, 2).reshape(128, HB * NST))
    bbc_col = np.zeros((128, 1), np.float32)
    bbc_col[:32, 0] = np.asarray(inputs["L0_attn_bbc"])
    bbeta_r = _f32(np.broadcast_to(np.asarray(inputs["L1_attn_b_beta"])[None, :],
                                   (128, NH)))
    import ml_dtypes
    consts = {
        "A_sb": A_sb, "bbc_col": _f32(bbc_col), "bbeta_r": bbeta_r,
        "ident_bf": np.eye(128).astype(ml_dtypes.bfloat16),
        "ident_f32": _f32(np.eye(128)),
        "trilS": _f32(np.tril(np.ones((128, 128)), -1)),
        "triuI": _f32(np.triu(np.ones((128, 128)), 0)),
    }
    nsel = np.zeros((32, 32 * 128), np.float32)
    for n in range(32):
        nsel[n, n * 128:(n + 1) * 128] = 1.0
    consts["nsel"] = _f32(nsel)
    consts["trilS4"] = _f32(np.tile(np.tril(np.ones((128, 128)), -1), (1, 4)))
    consts["triuI4"] = _f32(np.tile(np.triu(np.ones((128, 128)), 0), (1, 4)))
    consts["id64s"] = _f32(np.tile(np.eye(64), (1, 8)))
    ims = []
    for c in range(8):
        b, q = divmod(c, 4)
        t0 = q * tok
        im = dict(wT)
        im.update(vec_sb)
        im.update(consts)
        im["xin"] = _f32(x[b, t0:t0 + tok, :])
        xp = np.zeros(D, np.float32) if q == 0 else np.asarray(x[b, t0 - 1, :])
        im["xprev"] = _f32(np.asarray(xp).reshape(HB, 128).T)
        m = np.array([1.0 if (s // 4 == b and s % 4 < q) else 0.0
                      for s in range(8)], np.float32)
        im["cmask"] = _f32(np.broadcast_to(m[None, :], (128, 8)))
        isel = np.zeros((64, 8 * 64), np.float32)
        for s in range(8):
            isel[:, s * 64:(s + 1) * 64] = (1.0 - m[s]) * np.eye(64)
        im["isel"] = _f32(isel)
        wout = np.asarray(inputs["L1_attn_Wout"])
        im["WoutH"] = _f32(wout.reshape(NH, DH, D).transpose(1, 0, 2)
                           .reshape(DH, NH * D))
        ims.append(im)
    return ims


def build(tok=TOK, taps=(), stop_after=None):
    assert tok % 1024 == 0
    nch = tok // TC
    ntt = TC // 128
    ndc = tok // CD
    npass = tok // PASS
    taps = set(taps)
    nc = bass.Bass("TRN2", target_bir_lowering=False, debug=False)
    es = ExitStack()

    def din(name, shape, dt=F32):
        return nc.dram_tensor(name, shape, dt, kind="ExternalInput").ap()

    outs = {}

    def dout(name, shape, dt=F32):
        outs[name] = nc.dram_tensor(name, shape, dt, kind="ExternalOutput").ap()
        return outs[name]

    xin = din("xin", [tok, D])
    xprev_d = din("xprev", [128, HB])
    Wd = {k: din(k, list(s)) for k, s in {
        "WdT": (D, D), "WbcT": (D, 2 * NST), "W1T": (D, FF), "W2T": (FF, D),
        "WrT": (D, D), "WkT": (D, D), "WvT": (D, D),
        "Wq": (D, D), "Wk1": (D, D), "Wv1": (D, D), "Wbeta": (D, NH),
        "Wout": (D, D), "w1T": (D, FF), "w2T": (D, FF),
        "h1T": (FF, FF), "h2T": (FF, FF), "w3T": (FF, D)}.items()}
    VEC8 = {k: din(k, [128, HB]) for k in ["ln1_g", "ln1_b", "bd", "D0",
                                           "ln2_g", "ln2_b", "b2", "mu_r",
                                           "mu_k", "w3s"]}
    VECF = {k: din(k, [128, FB]) for k in ["b1", "w1s", "w2s", "h1s", "h2s"]}
    A_sb_d = din("A_sb", [128, HB * NST])
    bbc_col_d = din("bbc_col", [128, 1])
    bbeta_d = din("bbeta_r", [128, NH])
    id_bf_d = din("ident_bf", [128, 128], BF16)
    id_f_d = din("ident_f32", [128, 128])
    trilS_d = din("trilS", [128, 128])
    triuI_d = din("triuI", [128, 128])
    nsel_d = din("nsel", [32, 32 * 128])
    trilS4_d = din("trilS4", [128, 512])
    triuI4_d = din("triuI4", [128, 512])
    id64s_d = din("id64s", [64, 512])
    cmask_d = din("cmask", [128, 8])
    isel_d = din("isel", [64, 8 * 64])
    Wd["WoutH"] = din("WoutH", [64, NH * D])
    out_d = dout("out", [tok, D])

    def scratch(name, shape, dt=F32):
        return nc.dram_tensor(name, shape, dt).ap()

    sp_cumd = scratch("sp_cumd", [nch, 128, HB * TC])
    sp_bc = scratch("sp_bc", [nch, 32, TC])
    mk_sp = dout if "dbg0" in taps else scratch
    sp_u = mk_sp("sp_u", [nch, 128, HB * TC])
    sp_y = mk_sp("sp_y", [nch, 128, HB * TC])
    sp_rp = mk_sp("sp_rp", [nch, 128, HB * TC])
    sp_v = mk_sp("sp_v", [nch, 128, HB * TC])
    if "x1" in taps:
        sp_x1 = dout("sp_x1", [tok, D])
    else:
        sp_x1 = scratch("sp_x1", [tok, D])
    sp_xn1b = scratch("sp_xn1b", [ndc, 128, HB * CD], BF16)
    sp_O1 = (dout if "dbgC" in taps else scratch)("sp_O1", [ndc, 64, NH * CD])
    sp_Qef = (dout if "dbgC" in taps else scratch)("sp_Qef", [ndc, 64, NH * CD], BF16)
    sp_Mp = (dout if "dbgC" in taps else scratch)("sp_Mp", [ndc, 64, NH * DH], BF16)
    sp_Stl = (dout if "dbgC" in taps else scratch)("sp_Stl", [ndc, 64, NH * DH])
    mk_sp2 = dout if "dbgC" in taps else scratch
    sp_Qfm = mk_sp2("sp_Qfm", [ndc, 64, NH * CD], BF16)
    sp_ha = scratch("sp_ha", [FB, 128, tok], BF16)
    sp_hb = scratch("sp_hb", [FB, 128, tok], BF16)
    sp_hc = scratch("sp_hc", [FB, 128, tok], BF16)
    WB = {k: scratch("wb_" + k, list(s), BF16) for k, s in {
        "W1T": (D, FF), "W2T": (FF, D), "w1T": (D, FF), "w2T": (D, FF),
        "h1T": (FF, FF), "h2T": (FF, FF), "w3T": (FF, D)}.items()}
    ag0_in = scratch("ag0_in", [128, 256])
    ag0_out = scratch("ag0_out", [8, 128, 256])
    ag1_in = scratch("ag1_in", [64, 2 * NH * DH], BF16)
    ag1_out = scratch("ag1_out", [8, 64, 2 * NH * DH], BF16)

    def sb(name, shape, dt=F32):
        return es.enter_context(nc.sbuf_tensor(name, shape, dt))

    ARENA_COLS = 44544

    with es:
        pr = Prog(nc, es)
        vec_t = {k: sb("v_" + k, [128, HB]) for k in VEC8}
        vecf_t = {k: sb("vf_" + k, [128, FB]) for k in VECF}
        A_sb = sb("A_sbuf", [128, HB * NST])
        bbc_col = sb("bbc_colt", [128, 1])
        bbeta_t = sb("bbeta_t", [128, NH])
        id_bf = sb("id_bf", [128, 128], BF16)
        id_f = sb("id_f", [128, 128])
        trilS = sb("trilSt", [128, 128])
        triuI = sb("triuIt", [128, 128])
        cmask_t = sb("cmask_t", [128, 8])
        trilS4 = sb("trilS4t", [128, 512])
        triuI4 = sb("triuI4t", [128, 512])
        id64s = sb("id64st", [64, 512])
        isel_t = sb("isel_t", [64, 8 * 64])
        epsv = sb("epsv", [128, 1])
        xprev_t = sb("xprev_t", [128, HB])
        carry8 = sb("carry8", [128, 8])
        hcarry = sb("hcarry", [128, 128])
        dloc = sb("dloc", [128, 128])
        hin = sb("hin", [128, 128])
        stat = sb("stat", [128, 16])
        sstat = sb("sstat", [128, 8])
        sqs = sb("sqs", [128, D])
        StIn = sb("StIn", [64, NH * DH])
        StInB = sb("StInB", [64, NH * DH], BF16)
        arena_t = sb("arena", [128, ARENA_COLS])
        ar = Arena(arena_t, ARENA_COLS)

        pg = [es.enter_context(nc.psum_tensor(f"pg{i}", [128, 512], F32))
              for i in range(8)]
        pb = [pg[6], pg[7]]
        pt = pg[5]
        ptb = pg[5][:, 256:512].bitcast(BF16)

        nsel_t = ar.alloc(32 * 128)[0:32, :]
        for dst, src in [(A_sb[:], A_sb_d), (bbc_col[:], bbc_col_d),
                         (bbeta_t[:], bbeta_d), (id_bf[:], id_bf_d),
                         (id_f[:], id_f_d), (trilS[:], trilS_d),
                         (triuI[:], triuI_d), (nsel_t, nsel_d),
                         (trilS4[:], trilS4_d), (triuI4[:], triuI4_d),
                         (id64s[:], id64s_d),
                         (cmask_t[:], cmask_d), (isel_t[:], isel_d),
                         (xprev_t[:], xprev_d)]:
            pr.dma("sync", dst, src, "sX")
        for k in VEC8:
            pr.dma("sync", vec_t[k][:], VEC8[k], "sX")
        for k in VECF:
            pr.dma("sync", vecf_t[k][:], VECF[k], "sX")

        ar.push()
        wd_b = ar.bf16(HB * D)
        wbc_b = ar.bf16(HB * 2 * NST)
        wr_b = ar.bf16(HB * D)
        wk_b = ar.bf16(HB * D)
        wv_b = ar.bf16(HB * D)
        xtm = ar.f32(D)
        xntm = ar.f32(D)
        xn_fm = ar.f32(HB * (TC + 1))
        xn1_fm = ar.f32(HB * TC)
        xn1b_fm = ar.bf16(HB * TC)
        delta0 = ar.f32(HB * TC)
        dx0 = ar.f32(HB * TC)
        cumd = ar.f32(HB * TC)
        u_fm = ar.f32(HB * TC)
        y_fm = ar.f32(HB * TC)
        xr_b = ar.bf16(HB * TC)
        xk_b = ar.bf16(HB * TC)
        kact_b = ar.bf16(HB * TC)
        rpv_st = ar.f32(2 * TC)
        bc_ch = ar.alloc(TC)[0:32, :]
        a_scr = ar.f32(3 * TC)
        zring_b = ar.f32(3 * TC)
        zring = [zring_b[:, _zi * TC:(_zi + 1) * TC] for _zi in range(3)]
        b_scr = ar.f32(TC)
        h_scr = ar.f32(2 * TC)
        z_scr = b_scr

        for name, dst, w in [("WdT", wd_b, D), ("WbcT", wbc_b, 2 * NST),
                             ("WrT", wr_b, D), ("WkT", wk_b, D),
                             ("WvT", wv_b, D)]:
            for kb in range(HB):
                pr.dma("gpsimd", dst[:, kb * w:(kb + 1) * w],
                       Wd[name][kb * 128:(kb + 1) * 128, :], "sW")
        p_done = {}
        pr.wait("vector", "sX")
        pr.op("vector", lambda v: v.memset(epsv[:], EPS_LN))
        pr.op("vector", lambda v: v.memset(hcarry[:], 0.0), "sV")
        pr.wait("scalar", "sX")
        pr.op("scalar", lambda s: s.activation(A_sb[:], A_sb[:], AF.Exp), "sS")
        pr.wait("vector", "sS")
        pr.op("vector", lambda v: v.tensor_scalar_mul(A_sb[:], A_sb[:], -1.0), "sV")
        pr.wait("scalar", "sV")

        tap_jobs = []

        def tap(name, tile, shape):
            if name in taps:
                dd = dout("tap_" + name, shape)
                tap_jobs.append((dd, tile))


        def checkpoint(name):
            if stop_after == name:
                pr.wait("sync", "sV")
                pr.wait("sync", "sEv")
                pr.wait("sync", "sS")
                pr.wait("sync", "sSp")
                for dd, tile in tap_jobs:
                    pr.dma("sync", dd, tile[:], "sO")
                pr.dma("sync", out_d[0:128, :], sqs[:], "sO")
                pr.wait("sync", "sO")
                pr.emit(es.enter_context(nc.Block()))
                return True
            return False

        def mm(pj, lhsT, rhs, start, stop, sem=None):
            return pr.op("tensor",
                         lambda t, pj=pj, l=lhsT, r=rhs, a=start, o=stop:
                         t.matmul(pj, l, r, start=a, stop=o), sem)

        def sv(fn):
            pr.op("vector", fn, "sV")
            pr.wait("vector", "sV")

        def tm_ln(xt, c0, sst_col, out, gb=None):
            sv(lambda v: v.reduce_sum(stat[:, c0:c0 + 1], xt, axis=AXX))
            pr.wait("scalar", "sV")
            pr.op("scalar", lambda s: s.activation(sqs[:], xt, AF.Square), "sS")
            pr.wait("vector", "sS")
            sv(lambda v: v.reduce_sum(stat[:, c0 + 1:c0 + 2], sqs[:], axis=AXX))
            sv(lambda v: v.tensor_scalar(stat[:, c0:c0 + 1], stat[:, c0:c0 + 1],
                                         1.0 / D, None, ALU.mult))
            sv(lambda v: v.tensor_scalar(stat[:, c0 + 1:c0 + 2],
                                         stat[:, c0 + 1:c0 + 2], 1.0 / D, None,
                                         ALU.mult))
            sv(lambda v: v.tensor_mul(stat[:, c0 + 2:c0 + 3], stat[:, c0:c0 + 1],
                                      stat[:, c0:c0 + 1]))
            sv(lambda v: v.tensor_sub(stat[:, c0 + 1:c0 + 2],
                                      stat[:, c0 + 1:c0 + 2],
                                      stat[:, c0 + 2:c0 + 3]))
            pr.wait("scalar", "sV")

            def f(s):
                s.activation(sstat[:, sst_col:sst_col + 1],
                             stat[:, c0 + 1:c0 + 2], AF.Ln, bias=epsv[:])
                return s.activation(sstat[:, sst_col:sst_col + 1],
                                    sstat[:, sst_col:sst_col + 1], AF.Exp,
                                    scale=-0.5)
            pr.op("scalar", f, "sS")
            pr.wait("vector", "sS")
            sv(lambda v: v.tensor_mul(stat[:, c0 + 3:c0 + 4], stat[:, c0:c0 + 1],
                                      sstat[:, sst_col:sst_col + 1]))
            sv(lambda v: v.tensor_scalar_mul(stat[:, c0 + 3:c0 + 4],
                                             stat[:, c0 + 3:c0 + 4], -1.0))
            sv(lambda v: v.tensor_scalar(out, xt, sstat[:, sst_col:sst_col + 1],
                                         stat[:, c0 + 3:c0 + 4], ALU.mult,
                                         ALU.add))

        last_transpose = [0]

        for ch in range(nch):
            t0 = ch * TC
            dcur = delta0
            dxc = dx0

            for tt in range(ntt):
                row = t0 + tt * 128
                pr.wait("sync", "sT", last_transpose[0])
                pr.dma("sync", xtm[:], xin[row:row + 128, :], "sX")
                xv = pr.current("sX")
                pr.wait("vector", "sX", xv)
                pr.wait("scalar", "sX", xv)
                tm_ln(xtm[:], 0, 0, xntm[:])
                pr.wait("scalar", "sV")
                tm_ln(xntm[:], 4, 4, xtm[:])
                vready = pr.current("sV")

                pr.wait("tensor", "sV", vready)
                for hb in range(HB):
                    pr.wait("tensor", "sEv", pr.current("sEv"))
                    pr.op("tensor", lambda t, hb=hb: t.transpose(
                        pt[:, 0:128], xntm[:, hb * 128:(hb + 1) * 128], id_f[:]))
                    tv = pr.op("tensor", lambda t, hb=hb: t.transpose(
                        pt[:, 128:256], xtm[:, hb * 128:(hb + 1) * 128],
                        id_f[:]), "sT")
                    pr.wait("vector", "sT", tv)
                    c0 = hb * (TC + 1) + 1 + tt * 128
                    c1 = hb * TC + tt * 128

                    def f(v, c0=c0, c1=c1, hb=hb):
                        v.tensor_copy(xn_fm[:, c0:c0 + 128], pt[:, 0:128])
                        v.tensor_scalar(xn1_fm[:, c1:c1 + 128], pt[:, 128:256],
                                        vec_t["ln1_g"][:, hb:hb + 1],
                                        vec_t["ln1_b"][:, hb:hb + 1],
                                        ALU.mult, ALU.add)
                        return v.tensor_copy(xn1b_fm[:, c1:c1 + 128],
                                             xn1_fm[:, c1:c1 + 128])
                    pr.op("vector", f, "sEv")
                last_transpose[0] = pr.current("sT")

            pr.wait("vector", "sSp")

            def f(v):
                last = None
                for hb in range(HB):
                    cb = hb * (TC + 1)
                    last = v.tensor_copy(xn_fm[:, cb:cb + 1], xprev_t[:, hb:hb + 1])
                return last
            pr.op("vector", f, "sV")
            pr.wait("vector", "sV")

            def f(v):
                for hb in range(HB):
                    cb = hb * (TC + 1)
                    c1 = hb * TC
                    v.scalar_tensor_tensor(u_fm[:, c1:c1 + TC],
                                           xn1_fm[:, c1:c1 + TC],
                                           vec_t["D0"][:, hb:hb + 1],
                                           xn_fm[:, cb + 1:cb + 1 + TC],
                                           ALU.mult, ALU.add)
                    v.tensor_sub(z_scr[:, 0:TC], xn_fm[:, cb + 1:cb + 1 + TC],
                                 xn_fm[:, cb:cb + TC])
                    v.scalar_tensor_tensor(xr_b[:, c1:c1 + TC], z_scr[:, 0:TC],
                                           vec_t["mu_r"][:, hb:hb + 1],
                                           xn_fm[:, cb:cb + TC], ALU.mult, ALU.add)
                    v.scalar_tensor_tensor(xk_b[:, c1:c1 + TC], z_scr[:, 0:TC],
                                           vec_t["mu_k"][:, hb:hb + 1],
                                           xn_fm[:, cb:cb + TC], ALU.mult, ALU.add)
                    v.tensor_copy(xprev_t[:, hb:hb + 1],
                                  xn_fm[:, cb + TC:cb + TC + 1])
                return v.engine_nop()
            pr.op("vector", f, "sV")
            pr.wait("sync", "sV")
            pr.dma("sync", sp_u[ch], u_fm[:], "sSp")

            pr.wait("tensor", "sW")

            def gemm_fm(evac, wsb, width, rhs_sb, bank0=0):
                for o in range(width // 128):
                    bank = pg[(bank0 + o) % 4]
                    pr.wait("tensor", "sEv", pr.current("sEv"))
                    for kb in range(HB):
                        mm(bank[:, 0:TC],
                           wsb[:, kb * width + o * 128: kb * width + o * 128 + 128],
                           rhs_sb[:, kb * TC:(kb + 1) * TC],
                           kb == 0, kb == HB - 1, "sT" if kb == HB - 1 else None)
                    evac(o, bank[:, 0:TC], pr.current("sT"))

            def evac_delta(o, pap, tval):
                pr.wait("scalar", "sT", tval)

                def f(s, o=o, pap=pap):
                    s.activation(sqs[:, 0:TC], pap, AF.Exp,
                                 bias=vec_t["bd"][:, o:o + 1])
                    return s.activation(dcur[:, o * TC:(o + 1) * TC],
                                        sqs[:, 0:TC], AF.Ln, bias=1.0)
                pr.op("scalar", f, "sEv")
            gemm_fm(evac_delta, wd_b, D, xn1b_fm, 0)

            pr.wait("tensor", "sEv", pr.current("sEv"))
            for kb in range(HB):
                mm(pg[0][0:32, 0:TC], wbc_b[:, kb * 32:(kb + 1) * 32],
                   xn1b_fm[:, kb * TC:(kb + 1) * TC], kb == 0, kb == HB - 1,
                   "sT" if kb == HB - 1 else None)
            pr.wait("vector", "sT")
            pr.wait("vector", "sSp")
            pr.op("vector", lambda v: v.tensor_scalar_add(
                bc_ch, pg[0][0:32, 0:TC], bbc_col[0:32, :]), "sEv")
            pr.wait("sync", "sEv")
            pr.dma("sync", sp_bc[ch], bc_ch, "sSp")

            pr.wait("tensor", "sV")

            def evac_spill(dst_dram):
                def evac(o, pap, tval):
                    slot = rpv_st[:, (o % 2) * TC:(o % 2 + 1) * TC]
                    pr.wait("scalar", "sSp", max(0, pr.current("sSp") - 16))
                    pr.wait("scalar", "sT", tval)
                    ev = pr.op("scalar", lambda s, slot=slot, pap=pap:
                               s.activation(slot, pap, AF.Copy), "sEv")
                    pr.wait("sync", "sEv", ev)
                    pr.dma("sync", dst_dram[:, o * TC:(o + 1) * TC], slot, "sSp")
                return evac
            gemm_fm(evac_spill(sp_rp[ch]), wr_b, D, xr_b, 1)

            def evac_kact(o, pap, tval):
                pr.wait("scalar", "sT", tval)

                def f(s, o=o, pap=pap):
                    s.activation(sqs[:, 0:TC], pap, AF.Relu)
                    return s.activation(kact_b[:, o * TC:(o + 1) * TC],
                                        sqs[:, 0:TC], AF.Square)
                pr.op("scalar", f, "sEv")
            gemm_fm(evac_kact, wk_b, D, xk_b, 2)
            gemm_fm(evac_spill(sp_v[ch]), wv_b, D, kact_b, 3)

            pr.wait("vector", "sEv")

            def f(v):
                for hb in range(HB):
                    c = hb * TC
                    v.tensor_mul(dxc[:, c:c + TC], dcur[:, c:c + TC],
                                 xn1_fm[:, c:c + TC])
                return v.engine_nop()
            pr.op("vector", f, "sV")
            pr.wait("vector", "sSp")
            if ch > 0:
                def f(v):
                    last = None
                    for hb in range(HB):
                        last = v.tensor_copy(carry8[:, hb:hb + 1],
                                             cumd[:, hb * TC + TC - 1:hb * TC + TC])
                    return last
                pr.op("vector", f, "sV")
                pr.wait("vector", "sV")

            def f(v, ch=ch):
                for hb in range(HB):
                    c = hb * TC
                    ini = carry8[:, hb:hb + 1] if ch > 0 else 0.0
                    v.tensor_tensor_scan(cumd[:, c:c + TC], dcur[:, c:c + TC],
                                         dcur[:, c:c + TC], ini, ALU.add,
                                         ALU.bypass)
                return v.engine_nop()
            pr.op("vector", f, "sV")
            pr.wait("sync", "sV")
            pr.dma("sync", sp_cumd[ch], cumd[:], "sSp")

            pr.wait("vector", "sSp")
            pr.wait("vector", "sG2")
            sc_base = pr.current("sSc")
            for n in range(NST):
                bank = pb[n % 2]
                if n == 0:
                    gv = sc_base - 8
                elif n == 1:
                    gv = sc_base
                else:
                    gv = sc_base + 8 * (n - 1)
                pr.wait("tensor", "sSc", max(0, gv))
                selB = nsel_t[:, n * 128:(n + 1) * 128]
                selC = nsel_t[:, (NST + n) * 128:(NST + n + 1) * 128]
                mm(bank[:, 0:TC], selB, bc_ch, True, True)
                btv = mm(bank[:, TC:2 * TC], selC, bc_ch, True,
                         True, "sT")
                for hb in range(HB):
                    idx = hb * NST + n
                    col = n * HB + hb
                    asl = a_scr[:, (col % 3) * TC:(col % 3 + 1) * TC]
                    pr.wait("scalar", "sSc", max(0, pr.current("sSc") - 2))
                    svv = pr.op("scalar", lambda s, asl=asl, hb=hb, idx=idx:
                                s.activation(asl, dcur[:, hb * TC:hb * TC + TC],
                                             AF.Exp, scale=A_sb[:, idx:idx + 1]),
                                "sS")
                    pr.wait("vector", "sT", btv)
                    pr.wait("vector", "sS", svv)
                    bsl = b_scr[:, 0:TC]
                    hsl = h_scr[:, (col % 2) * TC:(col % 2 + 1) * TC]

                    zsl = zring[col % 3]
                    if n > 0:
                        pr.wait("vector", "sG2", max(0, pr.current("sG2") - 2))

                    def f(v, bsl=bsl, hsl=hsl, asl=asl, hb=hb, n=n, col=col,
                          bank=bank, zsl=zsl):
                        if col >= 2:
                            pc = col - 2
                            v.tensor_copy(hcarry[:, pc:pc + 1],
                                          h_scr[:, (pc % 2) * TC + TC - 1:
                                                (pc % 2) * TC + TC])
                        v.tensor_mul(bsl, dxc[:, hb * TC:hb * TC + TC],
                                     bank[:, 0:TC])
                        v.tensor_tensor_scan(hsl, asl, bsl,
                                             hcarry[:, col:col + 1],
                                             ALU.mult, ALU.add)
                        if n == 0:
                            return v.tensor_mul(y_fm[:, hb * TC:hb * TC + TC],
                                                hsl, bank[:, TC:2 * TC])
                        return v.tensor_mul(zsl, hsl, bank[:, TC:2 * TC])
                    pr.op("vector", f, "sSc")
                    if n > 0:
                        pr.wait("gpsimd", "sSc")
                        pr.op("gpsimd", lambda g, hb=hb, zsl=zsl: g.tensor_add(
                            y_fm[:, hb * TC:hb * TC + TC],
                            y_fm[:, hb * TC:hb * TC + TC], zsl), "sG2")
            pr.wait("vector", "sSc")

            def f(v):
                last = None
                for pc in (NST * HB - 2, NST * HB - 1):
                    last = v.tensor_copy(hcarry[:, pc:pc + 1],
                                         h_scr[:, (pc % 2) * TC + TC - 1:
                                               (pc % 2) * TC + TC])
                return last
            pr.op("vector", f, "sV")
            pr.wait("sync", "sSc")
            pr.wait("sync", "sG2")
            pr.dma("sync", sp_y[ch], y_fm[:], "sSp")

        pr.wait("scalar", "sV")
        pr.wait("scalar", "sSc")

        def f(s):
            last = None
            for n in range(NST):
                for hb in range(HB):
                    idx = hb * NST + n
                    col = n * HB + hb
                    last = s.activation(dloc[:, col:col + 1],
                                        cumd[:, hb * TC + TC - 1:hb * TC + TC],
                                        AF.Exp, scale=A_sb[:, idx:idx + 1])
            return last
        pr.op("scalar", f, "sS")

        for name in ["W1T", "W2T"]:
            rows = Wd[name].shape[0]
            step = 512
            for r0 in range(0, rows, step):
                pr.dma("gpsimd", WB[name][r0:r0 + step, :],
                       Wd[name][r0:r0 + step, :], "sP")
            p_done[name] = pr.current("sP")

        tap("hcarry", hcarry, [128, 128])
        tap("dloc", dloc, [128, 128])

        if stop_after == "A":
            pr.wait("sync", "sS")
            pr.wait("sync", "sV")
            for dd, tile in tap_jobs:
                pr.dma("sync", dd, tile[:], "sO")
            pr.dma("sync", out_d[0:128, :], xtm[:], "sO")
            pr.wait("sync", "sO")
            pr.emit(es.enter_context(nc.Block()))
            return nc, outs

        if checkpoint("A2"):
            return nc, outs
        pr.wait("gpsimd", "sV")
        pr.wait("gpsimd", "sS")
        pr.wait("gpsimd", "sSp")
        agout = ar.f32(256)
        pr.dma("gpsimd", ag0_in[:, 0:128], hcarry[:], "sG")
        pr.dma("gpsimd", ag0_in[:, 128:256], dloc[:], "sG")
        pr.wait("gpsimd", "sG")
        pr.op("gpsimd", lambda g: g.collective_compute(
            "AllGather", ALU.bypass, replica_groups=[list(range(8))],
            ins=[ag0_in.opt()], outs=[ag0_out.opt()]), "sCC")
        pr.wait("gpsimd", "sCC")
        pr.op("vector", lambda v: v.memset(hin[:], 0.0), "sV")
        for s in range(8):
            pr.wait("gpsimd", "sV")
            pr.dma("gpsimd", agout[:, 0:256], ag0_out[s], "sG")
            gv = pr.current("sG")
            pr.wait("vector", "sG", gv)
            hs = agout[:, 0:128]
            ds = agout[:, 128:256]

            def f(v, hs=hs, ds=ds, s=s):
                v.tensor_scalar(h_scr[:, 0:128], ds, 1.0, None, ALU.subtract)
                v.tensor_scalar(h_scr[:, 0:128], h_scr[:, 0:128],
                                cmask_t[:, s:s + 1], 1.0, ALU.mult, ALU.add)
                v.tensor_mul(hin[:], hin[:], h_scr[:, 0:128])
                v.tensor_scalar(h_scr[:, 128:256], hs, cmask_t[:, s:s + 1],
                                None, ALU.mult)
                return v.tensor_add(hin[:], hin[:], h_scr[:, 128:256])
            pr.op("vector", f, "sV")
        pr.wait("vector", "sV")
        tap("hin", hin, [128, 128])


        if checkpoint("AG0"):
            return nc, outs
        ar.pop()
        ar.push()
        xn2b = ar.bf16(HB * PASS)
        hff = ar.bf16(FB * PASS)
        uch = ar.f32(HB * TC)
        ych = ar.f32(HB * TC)
        cumd_b = ar.f32(HB * TC)
        w_st = [ar.bf16(1024) for _ in range(8)]
        rv_b = ar.f32(2 * HB * TC)
        ssr = ych
        sq_b = ar.f32(TC)
        a_b = ar.f32(3 * TC)
        z_b = ar.f32(2 * TC)
        rows = ar.f32(2 * TC)
        bc_b = ar.alloc(TC)[0:32, :]
        zringB_b = ar.f32(3 * TC)
        zringB = [zringB_b[:, _zi * TC:(_zi + 1) * TC] for _zi in range(3)]
        sgt = ar.f32(D)
        xtm_b = ar.f32(D)
        ones_col = triuI[:, 127:128]
        ones_row = triuI[0:1, 0:128]

        for p in range(npass):
            for ci in range(PASS // TC):
                ch = p * (PASS // TC) + ci
                t0 = ch * TC
                co = ci * TC
                pr.wait("sync", "sV")
                pr.wait("sync", "sSc")
                pr.wait("sync", "sG2")
                pr.wait("sync", "sSp")
                pr.dma("sync", ych[:], sp_y[ch], "sX")
                pr.dma("sync", cumd_b[:], sp_cumd[ch], "sX")
                pr.dma("sync", uch[:], sp_u[ch], "sX")
                pr.wait("sync", "sT", pr.current("sT"))
                pr.dma("sync", bc_b, sp_bc[ch], "sX")
                xv = pr.current("sX")
                pr.wait("vector", "sX", xv)
                pr.wait("scalar", "sX", xv)
                pr.wait("tensor", "sX", xv)
                sc_base = pr.current("sSc")
                for n in range(NST):
                    bank = pb[n % 2]
                    gv = sc_base - 8 if n == 0 else (sc_base if n == 1
                                                     else sc_base + 8 * (n - 1))
                    pr.wait("tensor", "sSc", max(0, gv))
                    selC = nsel_t[:, (NST + n) * 128:(NST + n + 1) * 128]
                    btv = mm(bank[:, 0:TC], selC, bc_b, True, True, "sT")
                    for hb in range(HB):
                        idx = hb * NST + n
                        col = n * HB + hb
                        asl = a_b[:, (col % 3) * TC:(col % 3 + 1) * TC]
                        pr.wait("scalar", "sSc", max(0, pr.current("sSc") - 2))
                        svv = pr.op("scalar", lambda s, asl=asl, hb=hb, idx=idx:
                                    s.activation(asl,
                                                 cumd_b[:, hb * TC:hb * TC + TC],
                                                 AF.Exp,
                                                 scale=A_sb[:, idx:idx + 1]),
                                    "sS")
                        pr.wait("vector", "sT", btv)
                        pr.wait("vector", "sS", svv)

                        zsl = zringB[col % 3]
                        pr.wait("vector", "sG2", max(0, pr.current("sG2") - 2))

                        def f(v, asl=asl, hb=hb, col=col, bank=bank, zsl=zsl):
                            v.tensor_scalar(z_b[:, 0:TC], asl,
                                            hin[:, col:col + 1], None, ALU.mult)
                            return v.tensor_mul(zsl, z_b[:, 0:TC],
                                                bank[:, 0:TC])
                        pr.op("vector", f, "sSc")
                        pr.wait("gpsimd", "sSc")
                        pr.op("gpsimd", lambda g, hb=hb, zsl=zsl: g.tensor_add(
                            ych[:, hb * TC:hb * TC + TC],
                            ych[:, hb * TC:hb * TC + TC], zsl), "sG2")
                pr.wait("vector", "sSc")
                pr.wait("vector", "sG2")

                def f(v):
                    for hb in range(HB):
                        c = hb * TC
                        v.tensor_add(uch[:, c:c + TC], uch[:, c:c + TC],
                                     ych[:, c:c + TC])
                    return v.engine_nop()
                pr.op("vector", f, "sV")
                pr.wait("tensor", "sV")
                pr.wait("tensor", "sEv", pr.current("sEv"))
                for hb in range(HB):
                    mm(pg[0][0:1, 0:TC], ones_col, uch[:, hb * TC:hb * TC + TC],
                       hb == 0, hb == HB - 1, "sT" if hb == HB - 1 else None)
                pr.wait("scalar", "sV")
                for hb in range(HB):
                    if hb > 0:
                        pr.wait("scalar", "sT", pr.current("sT"))
                    stv = pr.op("scalar", lambda s, hb=hb: s.activation(
                        sqs[:, 0:TC], uch[:, hb * TC:hb * TC + TC], AF.Square),
                        "sS")
                    pr.wait("tensor", "sS", stv)
                    mm(pg[1][0:1, 0:TC], ones_col, sqs[:, 0:TC],
                       hb == 0, hb == HB - 1, "sT")
                tv = pr.current("sT")
                pr.wait("vector", "sT", tv)

                def f(v):
                    v.tensor_scalar(rows[0:1, 0:TC], pg[0][0:1, 0:TC],
                                    1.0 / D, None, ALU.mult)
                    v.tensor_scalar(rows[0:1, TC:2 * TC], pg[1][0:1, 0:TC],
                                    1.0 / D, None, ALU.mult)
                    v.tensor_mul(sq_b[0:1, 0:TC], rows[0:1, 0:TC],
                                 rows[0:1, 0:TC])
                    return v.tensor_sub(rows[0:1, TC:2 * TC],
                                        rows[0:1, TC:2 * TC], sq_b[0:1, 0:TC])
                pr.op("vector", f, "sV")
                pr.wait("scalar", "sV")

                def f(s):
                    s.activation(sq_b[0:1, 0:TC], rows[0:1, TC:2 * TC], AF.Ln,
                                 bias=epsv[0:1, :])
                    return s.activation(sq_b[0:1, 0:TC], sq_b[0:1, 0:TC],
                                        AF.Exp, scale=-0.5)
                pr.op("scalar", f, "sS")
                pr.wait("vector", "sS")

                def f(v):
                    v.tensor_mul(rows[0:1, 0:TC], rows[0:1, 0:TC],
                                 sq_b[0:1, 0:TC])
                    return v.tensor_scalar_mul(rows[0:1, 0:TC],
                                               rows[0:1, 0:TC], -1.0)
                pr.op("vector", f, "sV")
                pr.wait("tensor", "sV")
                pr.wait("tensor", "sEv", pr.current("sEv"))
                mm(pg[2][:, 0:TC], ones_row, sq_b[0:1, 0:TC], True, True)
                tv = mm(pg[3][:, 0:TC], ones_row, rows[0:1, 0:TC], True, True,
                        "sT")
                pr.wait("vector", "sT", tv)

                def f(v, co=co):
                    for hb in range(HB):
                        v.tensor_mul(z_b[:, 0:TC], uch[:, hb * TC:hb * TC + TC],
                                     pg[2][:, 0:TC])
                        v.tensor_add(z_b[:, 0:TC], z_b[:, 0:TC],
                                     pg[3][:, 0:TC])
                        v.tensor_scalar(
                            xn2b[:, hb * PASS + co:hb * PASS + co + TC],
                            z_b[:, 0:TC], vec_t["ln2_g"][:, hb:hb + 1],
                            vec_t["ln2_b"][:, hb:hb + 1], ALU.mult, ALU.add)
                    return v.engine_nop()
                pr.op("vector", f, "sEv")
                pr.wait("sync", "sEv")
                pr.dma("sync", sp_u[ch], uch[:], "sSp")

            if "dbgB" in taps and p == 0:
                d_rows = dout("d_rows", [1, 2 * TC])
                pr.wait("sync", "sV")
                pr.wait("sync", "sS")
                pr.dma("sync", d_rows, rows[0:1, :], "sO")
                d_sqb = dout("d_sqb", [1, TC])
                pr.dma("sync", d_sqb, sq_b[0:1, 0:TC], "sO")
                d_xn2b = dout("d_xn2b", [128, HB * PASS], BF16)
                pr.wait("sync", "sEv")
                pr.dma("sync", d_xn2b, xn2b[:], "sO")
            for og in range(4):
                pr.wait("gpsimd", "sT", pr.current("sT"))
                pr.wait("gpsimd", "sP", p_done["W1T"])
                for kb in range(HB):
                    pr.dma("gpsimd", w_st[kb],
                           WB["W1T"][kb * 128:(kb + 1) * 128,
                                     og * 1024:(og + 1) * 1024], "sW")
                wv = pr.current("sW")
                pr.wait("tensor", "sW", wv)
                for fi in range(PASS // 512):
                    fc = fi * 512
                    pr.wait("tensor", "sEv", pr.current("sEv"))
                    for kb in range(HB):
                        for ob in range(8):
                            mm(pg[ob][:, 0:512],
                               w_st[kb][:, ob * 128:(ob + 1) * 128],
                               xn2b[:, kb * PASS + fc:kb * PASS + fc + 512],
                               kb == 0, kb == HB - 1,
                               "sT" if (kb == HB - 1 and ob == 7) else None)
                    tv = pr.current("sT")
                    pr.wait("scalar", "sT", tv)
                    for ob in range(8):
                        o = og * 8 + ob
                        pr.op("scalar", lambda s, o=o, ob=ob, fc=fc:
                              s.activation(hff[:, o * PASS + fc:
                                               o * PASS + fc + 512],
                                           pg[ob][:, 0:512], AF.Gelu,
                                           bias=vecf_t["b1"][:, o:o + 1]),
                              "sEv")

            if "dbgB" in taps and p == 0:
                d_hff = dout("d_hff", [128, FB * PASS], BF16)
                pr.wait("sync", "sEv")
                pr.dma("sync", d_hff, hff[:], "sO")
            for fi in range(PASS // 512):
                fc = fi * 512
                pr.wait("tensor", "sEv", pr.current("sEv"))
                pr.wait("gpsimd", "sT", pr.current("sT"))
                pr.wait("gpsimd", "sP", p_done["W2T"])
                kb_tv = {}
                for kb in range(FB):
                    slot = w_st[kb % 8]
                    if kb >= 8:
                        pr.wait("gpsimd", "sT", kb_tv[kb - 8])
                    pr.dma("gpsimd", slot,
                           WB["W2T"][kb * 128:(kb + 1) * 128, :], "sW")
                    wv = pr.current("sW")
                    pr.wait("tensor", "sW", wv)
                    for ob in range(8):
                        kb_tv[kb] = mm(pg[ob][:, 0:512],
                                       slot[:, ob * 128:(ob + 1) * 128],
                                       hff[:, kb * PASS + fc:
                                           kb * PASS + fc + 512],
                                       kb == 0, kb == FB - 1,
                                       "sT" if ob == 7 else None)
                tv = pr.current("sT")
                for sci in range(512 // TC):
                    ch = p * (PASS // TC) + fi * (512 // TC) + sci
                    t0 = ch * TC
                    scc = sci * TC
                    pr.wait("sync", "sV")
                    pr.dma("sync", rv_b[:, 0:HB * TC], sp_rp[ch], "sX")
                    pr.dma("sync", rv_b[:, HB * TC:2 * HB * TC], sp_v[ch], "sX")
                    pr.dma("sync", ssr[:], sp_u[ch], "sX")
                    xv = pr.current("sX")
                    pr.wait("scalar", "sX", xv)
                    for hb in range(HB):
                        pr.op("scalar", lambda s, hb=hb: s.activation(
                            rv_b[:, hb * TC:hb * TC + TC],
                            rv_b[:, hb * TC:hb * TC + TC], AF.Sigmoid), "sS")
                    pr.wait("vector", "sS")
                    pr.wait("vector", "sX", xv)

                    def f(v):
                        for hb in range(HB):
                            c = hb * TC
                            v.tensor_mul(rv_b[:, c:c + TC], rv_b[:, c:c + TC],
                                         rv_b[:, HB * TC + c:HB * TC + c + TC])
                        return v.engine_nop()
                    pr.op("vector", f, "sV")
                    pr.wait("scalar", "sV")
                    for hb in range(HB):
                        pr.op("scalar", lambda s, hb=hb: s.activation(
                            rv_b[:, hb * TC:hb * TC + TC],
                            rv_b[:, hb * TC:hb * TC + TC], AF.Silu), "sS")
                    gsv = pr.current("sS")
                    pr.wait("vector", "sS", gsv)
                    pr.wait("vector", "sT", tv)

                    def f(v, scc=scc):
                        for hb in range(HB):
                            c = hb * TC
                            v.scalar_tensor_tensor(z_b[:, 0:TC],
                                                   pg[hb][:, scc:scc + TC],
                                                   vec_t["b2"][:, hb:hb + 1],
                                                   ssr[:, c:c + TC],
                                                   ALU.add, ALU.add)
                            v.tensor_mul(ssr[:, c:c + TC], z_b[:, 0:TC],
                                         rv_b[:, c:c + TC])
                        return v.engine_nop()
                    pr.op("vector", f, "sEv")
                    for tt2 in range(TC // 128):
                        row = t0 + tt2 * 128
                        pr.wait("sync", "sO")
                        pr.dma("sync", xtm_b[:], xin[row:row + 128, :], "sX")
                        xv2 = pr.current("sX")
                        for hb in range(HB):
                            pr.wait("tensor", "sEv", pr.current("sEv"))
                            ttv = pr.op("tensor", lambda t, hb=hb, tt2=tt2:
                                        t.transpose(
                                            pt[:, 0:128],
                                            ssr[:, hb * TC + tt2 * 128:
                                                hb * TC + tt2 * 128 + 128],
                                            id_f[:]), "sT")
                            pr.wait("vector", "sT", ttv)
                            pr.op("vector", lambda v, hb=hb: v.tensor_copy(
                                sgt[:, hb * 128:(hb + 1) * 128], pt[:, 0:128]),
                                "sEv")
                        pr.wait("vector", "sX", xv2)
                        pr.wait("vector", "sEv")
                        pr.op("vector", lambda v: v.tensor_add(
                            xtm_b[:], xtm_b[:], sgt[:]), "sV")
                        pr.wait("sync", "sV")
                        pr.dma("sync", sp_x1[row:row + 128, :], xtm_b[:], "sO")

        if checkpoint("B"):
            return nc, outs
        ar.pop()
        ar.push()
        wq_b = ar.bf16(HB * D)
        wk1_b = ar.bf16(HB * D)
        wv1_b = ar.bf16(HB * D)
        wbeta_b = ar.bf16(HB * NH)
        x1tm = ar.f32(D)
        x1tm2 = ar.f32(D)
        xn1tm = ar.f32(D)
        xn1bL = ar.bf16(HB * CD)
        qfm = ar.bf16(NH * CD)
        khat = ar.bf16(NH * DH)
        vtm = ar.bf16(NH * DH)
        kfm = ar.bf16(NH * CD)
        beta_c = ar.f32(NH)
        rstdk = ar.f32(NH)
        negb_c = ar.f32(NH)
        kss = ar.f32(NH)
        eps24 = ar.f32(4)
        M_all = ar.bf16(NH * 128)
        MT_all = ar.bf16(NH * 128)
        M2_all = ar.bf16(NH * 128)
        MT2_all = ar.bf16(NH * 128)
        Y_all = ar.bf16(NH * 128)
        W2T_all = ar.bf16(NH * 128)
        ImG_all = ar.bf16(NH * 64)
        ImGT_all = ar.bf16(NH * 64)
        Mp_run = ar.bf16(NH * DH)
        MpT_run = ar.bf16(NH * DH)
        St_run = ar.f32(NH * DH)
        StB_run = ar.bf16(NH * DH)
        O1ch = ar.f32(NH * CD)
        PWch = ar.bf16(NH * CD)
        agst = ar.bf16(2 * NH * DH)
        csc = ar.f32(512)

        for name, dst, w in [("Wq", wq_b, D), ("Wk1", wk1_b, D),
                             ("Wv1", wv1_b, D), ("Wbeta", wbeta_b, NH)]:
            for kb in range(HB):
                pr.dma("gpsimd", dst[:, kb * w:(kb + 1) * w],
                       Wd[name][kb * 128:(kb + 1) * 128, :], "sW")
        wv_c = pr.current("sW")
        pr.op("vector", lambda v: v.memset(eps24[:], 1e-24))
        pr.op("vector", lambda v: v.memset(St_run[0:64, :], 0.0))
        pr.op("vector", lambda v: v.memset(StB_run[0:64, :], 0.0))

        def f(v):
            for h in range(NH):
                v.tensor_copy(Mp_run[0:64, h * DH:(h + 1) * DH],
                              id_bf[0:64, 0:64])
                v.tensor_copy(MpT_run[0:64, h * DH:(h + 1) * DH],
                              id_bf[0:64, 0:64])
            return v.engine_nop()
        pr.op("vector", f, "sV")
        pr.wait("vector", "sV")

        x1bufs = [x1tm, x1tm2]
        pr.wait("sync", "sV")
        pr.wait("sync", "sEv")
        pr.dma("sync", x1bufs[0][:], sp_x1[0:CD, :], "sX")
        vv_ln = {}
        for dc in range(ndc):
            r0 = dc * CD
            if dc + 1 < ndc:
                pr.wait("sync", "sV", vv_ln.get(dc - 1, 0))
                pr.dma("sync", x1bufs[(dc + 1) % 2][:],
                       sp_x1[r0 + CD:r0 + 2 * CD, :], "sX")
            pr.wait("sync", "sV")
            pr.wait("sync", "sEv")
            pr.dma("sync", sp_Mp[dc], Mp_run[0:64, :], "sSp")
            pr.dma("sync", sp_Stl[dc], St_run[0:64, :], "sSp")
            xv = pr.current("sX")
            pr.wait("vector", "sX", xv)
            pr.wait("scalar", "sX", xv)
            tm_ln(x1bufs[dc % 2][:], 0, 0, xn1tm[:])
            vv_ln[dc] = pr.current("sV")
            vready = pr.current("sV")
            pr.wait("tensor", "sV", vready)
            pr.wait("tensor", "sW", wv_c)
            pr.wait("tensor", "sEv", pr.current("sEv"))
            for g in range(2):
                tbank = pg[5] if g == 0 else pg[6]
                for q in range(4):
                    hb = g * 4 + q
                    pr.op("tensor", lambda t, hb=hb, q=q, tbank=tbank:
                          t.transpose(tbank[:, q * 128:(q + 1) * 128],
                                      xn1tm[:, hb * 128:(hb + 1) * 128],
                                      id_f[:]), "sT" if q == 3 else None)
                pr.wait("scalar", "sT", pr.current("sT"))
                pr.op("scalar", lambda s, g=g, tbank=tbank: s.activation(
                    xn1bL[:, g * 512:(g + 1) * 512], tbank[:, 0:512],
                    AF.Copy), "sEv")
            pr.wait("sync", "sEv")
            pr.dma("sync", sp_xn1b[dc], xn1bL[:], "sSp")
            if dc == 0 and checkpoint("Cln"):
                return nc, outs
            pr.wait("tensor", "sEv", pr.current("sEv"))
            for h in range(NH):
                bank = pg[h // 4]
                qc = (h % 4) * CD
                for kb in range(HB):
                    mm(bank[0:64, qc:qc + CD],
                       wq_b[:, kb * D + h * DH:kb * D + h * DH + DH],
                       xn1bL[:, kb * CD:(kb + 1) * CD],
                       kb == 0, kb == HB - 1,
                       "sT" if (kb == HB - 1 and h % 4 == 3) else None)
                if h % 4 == 3:
                    b = h // 4
                    pr.wait("scalar", "sT", pr.current("sT"))
                    pr.op("scalar", lambda s, b=b: s.activation(
                        qfm[0:64, b * 512:(b + 1) * 512],
                        pg[b][0:64, 0:512], AF.Copy), "sEv")
            if dc == 0 and checkpoint("Cq"):
                return nc, outs
            pr.wait("tensor", "sEv", pr.current("sEv"))
            for half in range(2):
                for kb in range(HB):
                    mm(pg[2 + half][:, 0:512], xn1bL[:, kb * CD:(kb + 1) * CD],
                       wk1_b[:, kb * D + half * 512:kb * D + half * 512 + 512],
                       kb == 0, kb == HB - 1, "sT" if kb == HB - 1 else None)
            for half in range(2):
                pr.wait("tensor", "sEv", pr.current("sEv"))
                for kb in range(HB):
                    mm(pg[4][:, 0:512], xn1bL[:, kb * CD:(kb + 1) * CD],
                       wv1_b[:, kb * D + half * 512:kb * D + half * 512 + 512],
                       kb == 0, kb == HB - 1, "sT" if kb == HB - 1 else None)
                tv = pr.current("sT")
                pr.wait("scalar", "sT", tv)
                pr.op("scalar", lambda s, half=half: s.activation(
                    vtm[:, half * 512:half * 512 + 512], pg[4][:, 0:512],
                    AF.Copy), "sEv")
            ktv = pr.current("sT")
            for half in range(2):
                pr.wait("scalar", "sT", ktv)
                if half > 0:
                    pr.wait("scalar", "sV")
                pr.op("scalar", lambda s, half=half: s.activation(
                    sqs[:, 0:512], pg[2 + half][:, 0:512], AF.Square), "sS")
                ssv = pr.current("sS")
                pr.wait("vector", "sS", ssv)
                pr.op("vector", lambda v, half=half: v.tensor_reduce(
                    kss[:, half * 8:half * 8 + 8],
                    sqs[:, 0:512].rearrange("p (h d) -> p h d", d=DH),
                    axis=AXX, op=ALU.add), "sV")
            pr.wait("scalar", "sV")

            def f(s):
                s.activation(rstdk[:, 0:16], kss[:, 0:16], AF.Ln,
                             bias=eps24[:, 0:1])
                return s.activation(rstdk[:, 0:16], rstdk[:, 0:16], AF.Exp,
                                    scale=-0.5)
            pr.op("scalar", f, "sS")
            pr.wait("vector", "sS")
            for h in range(NH):
                half = h // 8
                pr.op("vector", lambda v, h=h, half=half: v.tensor_scalar(
                    khat[:, h * DH:(h + 1) * DH],
                    pg[2 + half][:, (h % 8) * DH:(h % 8) * DH + DH],
                    rstdk[:, h:h + 1], None, ALU.mult),
                    "sEv")
            pr.wait("tensor", "sEv", pr.current("sEv"))
            for kb in range(HB):
                mm(pg[4][:, 0:NH], xn1bL[:, kb * CD:(kb + 1) * CD],
                   wbeta_b[:, kb * NH:(kb + 1) * NH], kb == 0, kb == HB - 1,
                   "sT" if kb == HB - 1 else None)
            tv = pr.current("sT")
            pr.wait("vector", "sT", tv)
            pr.op("vector", lambda v: v.tensor_add(beta_c[:, 0:NH],
                                                   pg[4][:, 0:NH],
                                                   bbeta_t[:]), "sV")
            pr.wait("scalar", "sV")
            pr.op("scalar", lambda s: s.activation(beta_c[:, 0:NH],
                                                   beta_c[:, 0:NH],
                                                   AF.Sigmoid), "sS")
            pr.wait("vector", "sS")
            pr.op("vector", lambda v: v.tensor_scalar_mul(negb_c[:, 0:NH],
                                                          beta_c[:, 0:NH],
                                                          -1.0), "sV")
            pr.wait("vector", "sV")
            pr.wait("scalar", "sV")
            if dc == 0 and checkpoint("Ckv"):
                return nc, outs
            for h in range(NH):
                pr.wait("tensor", "sEv", pr.current("sEv"))
                ttv = pr.op("tensor", lambda t, h=h: t.transpose(
                    ptb[0:64, 0:128], khat[:, h * DH:(h + 1) * DH], id_bf[:]),
                    "sT")
                pr.wait("vector", "sT", ttv)
                pr.op("vector", lambda v, h=h: v.tensor_copy(
                    kfm[0:64, h * CD:(h + 1) * CD], ptb[0:64, 0:128]), "sEv")

            if dc == 0 and checkpoint("Ckf"):
                return nc, outs
            def hsl_(t, h):
                return t[0:128, h * 128:(h + 1) * 128]

            pr.wait("tensor", "sEv", pr.current("sEv"))
            for h in range(NH):
                bank = pg[h // 4]
                qc = (h % 4) * 128
                mm(bank[:, qc:qc + 128], kfm[0:64, h * CD:(h + 1) * CD],
                   kfm[0:64, h * CD:(h + 1) * CD], True, True,
                   "sT" if h == NH - 1 else None)
            tv = pr.current("sT")
            pr.wait("vector", "sT", tv)
            for b in range(4):
                pr.op("vector", lambda v, b=b: v.tensor_mul(
                    csc[:, 0:512], pg[b][:, 0:512], trilS4[:]), None)

                def f(v, b=b):
                    last = None
                    for hq in range(4):
                        h = b * 4 + hq
                        last = v.tensor_scalar(
                            hsl_(M_all, h), csc[:, hq * 128:(hq + 1) * 128],
                            negb_c[:, h:h + 1], None, ALU.mult)
                    return last
                pr.op("vector", f, "sEv")

            def f(v):
                last = None
                for h in range(NH):
                    last = v.tensor_scalar(Y_all[0:128, h * 128:h * 128 + 64],
                                           vtm[:, h * DH:(h + 1) * DH],
                                           beta_c[:, h:h + 1], None, ALU.mult)
                    last = v.tensor_scalar(Y_all[0:128, h * 128 + 64:
                                                 h * 128 + 128],
                                           khat[:, h * DH:(h + 1) * DH],
                                           beta_c[:, h:h + 1], None, ALU.mult)
                return last
            pr.op("vector", f, "sEv")
            pr.wait("tensor", "sEv", pr.current("sEv"))
            for r4 in range(4):
                for k4 in range(4):
                    h = r4 * 4 + k4
                    pr.op("tensor", lambda t, h=h, k4=k4: t.transpose(
                        ptb[0:128, k4 * 128:(k4 + 1) * 128], hsl_(M_all, h),
                        id_bf[:]), "sT" if k4 == 3 else None)
                pr.wait("vector", "sT", pr.current("sT"))
                pr.op("vector", lambda v, r4=r4: v.tensor_copy(
                    MT_all[0:128, r4 * 512:(r4 + 1) * 512], ptb[0:128, 0:512]),
                    "sEv")
                if r4 < 3:
                    pr.wait("tensor", "sEv", pr.current("sEv"))
            curM, curMT = M_all, MT_all
            nxtM, nxtMT = M2_all, MT2_all
            for lvl in range(7):
                pr.wait("tensor", "sEv", pr.current("sEv"))
                for h in range(NH):
                    bank = pg[h // 4]
                    qc = (h % 4) * 128
                    mm(bank[:, qc:qc + 128], hsl_(curMT, h), hsl_(Y_all, h),
                       True, True, "sT" if h % 4 == 3 else None)
                    if h % 4 == 3:
                        b = h // 4
                        pr.wait("vector", "sT", pr.current("sT"))
                        pr.op("vector", lambda v, b=b: v.tensor_add(
                            Y_all[0:128, b * 512:(b + 1) * 512],
                            Y_all[0:128, b * 512:(b + 1) * 512],
                            pg[b][:, 0:512]), "sEv")
                yev = pr.current("sEv")
                if lvl < 6:
                    for h in range(NH):
                        bank = pg[4 + h // 4]
                        qc = (h % 4) * 128
                        mm(bank[:, qc:qc + 128], hsl_(curMT, h), hsl_(curM, h),
                           True, True, "sT" if h % 4 == 3 else None)
                        if h % 4 == 3:
                            b = h // 4
                            pr.wait("scalar", "sT", pr.current("sT"))
                            pr.op("scalar", lambda s, b=b, nxtM=nxtM:
                                  s.activation(
                                      nxtM[0:128, b * 512:(b + 1) * 512],
                                      pg[4 + b][:, 0:512], AF.Copy), "sEv")
                    pr.wait("tensor", "sEv", yev)
                    for h in range(NH):
                        bank = pg[h // 4]
                        qc = (h % 4) * 128
                        mm(bank[:, qc:qc + 128], hsl_(curM, h), hsl_(curMT, h),
                           True, True, "sT" if h % 4 == 3 else None)
                        if h % 4 == 3:
                            b = h // 4
                            pr.wait("vector", "sT", pr.current("sT"))
                            pr.op("vector", lambda v, b=b, nxtMT=nxtMT:
                                  v.tensor_copy(
                                      nxtMT[0:128, b * 512:(b + 1) * 512],
                                      pg[b][:, 0:512]), "sEv")
                    curM, nxtM = nxtM, curM
                    curMT, nxtMT = nxtMT, curMT
            pr.wait("tensor", "sEv", pr.current("sEv"))
            for h in range(NH):
                bank = pg[h // 8]
                qc = (h % 8) * 64
                mm(bank[0:64, qc:qc + 64],
                   Y_all[0:128, h * 128 + 64:h * 128 + 128],
                   khat[:, h * DH:(h + 1) * DH], True, True,
                   "sT" if h == NH - 1 else None)
            for h in range(NH):
                bank = pg[2 + h // 8]
                qc = (h % 8) * 64
                mm(bank[0:64, qc:qc + 64], khat[:, h * DH:(h + 1) * DH],
                   Y_all[0:128, h * 128 + 64:h * 128 + 128], True, True,
                   "sT" if h == NH - 1 else None)
            tv = pr.current("sT")
            pr.wait("vector", "sT", tv)
            for b in range(2):
                pr.op("vector", lambda v, b=b: v.tensor_sub(
                    ImG_all[0:64, b * 512:(b + 1) * 512], id64s[:],
                    pg[b][0:64, 0:512]), "sEv")
            for b in range(2):
                pr.op("vector", lambda v, b=b: v.tensor_sub(
                    ImGT_all[0:64, b * 512:(b + 1) * 512], id64s[:],
                    pg[2 + b][0:64, 0:512]), "sEv")
            pr.wait("tensor", "sEv", pr.current("sEv"))
            for h in range(NH):
                bank = pg[h // 4]
                qc = (h % 4) * 128
                mm(bank[:, qc:qc + 128], kfm[0:64, h * CD:(h + 1) * CD],
                   qfm[0:64, h * CD:(h + 1) * CD], True, True,
                   "sT" if h == NH - 1 else None)
            tv = pr.current("sT")
            pr.wait("vector", "sT", tv)
            for b in range(4):
                pr.op("vector", lambda v, b=b: v.tensor_mul(
                    csc[:, 0:512], pg[b][:, 0:512], triuI4[:]), None)
                pr.op("vector", lambda v, b=b: v.tensor_copy(
                    W2T_all[0:128, b * 512:(b + 1) * 512], csc[:, 0:512]),
                    "sEv")
            pr.wait("tensor", "sEv", pr.current("sEv"))
            for h in range(NH):
                bank = pg[h // 8]
                qc = (h % 8) * 64
                mm(bank[0:64, qc:qc + 64],
                   MpT_run[0:64, h * DH:(h + 1) * DH],
                   ImG_all[0:64, h * 64:(h + 1) * 64], True, True,
                   "sT" if h == NH - 1 else None)
            for h in range(NH):
                bank = pg[2 + h // 8]
                qc = (h % 8) * 64
                mm(bank[0:64, qc:qc + 64],
                   ImG_all[0:64, h * 64:(h + 1) * 64],
                   MpT_run[0:64, h * DH:(h + 1) * DH], True, True,
                   "sT" if h == NH - 1 else None)
            for h in range(NH):
                bank = pg[4 + h // 8]
                qc = (h % 8) * 64
                mm(bank[0:64, qc:qc + 64], khat[:, h * DH:(h + 1) * DH],
                   Y_all[0:128, h * 128:h * 128 + 64], True, False)
                mm(bank[0:64, qc:qc + 64],
                   ImG_all[0:64, h * 64:(h + 1) * 64],
                   StB_run[0:64, h * DH:(h + 1) * DH], False, True,
                   "sT" if h == NH - 1 else None)
            tv = pr.current("sT")
            pr.wait("vector", "sT", tv)
            for b in range(2):
                pr.op("vector", lambda v, b=b: v.tensor_copy(
                    Mp_run[0:64, b * 512:(b + 1) * 512], pg[b][0:64, 0:512]),
                    None)
                pr.op("vector", lambda v, b=b: v.tensor_copy(
                    MpT_run[0:64, b * 512:(b + 1) * 512],
                    pg[2 + b][0:64, 0:512]), None)
                pr.op("vector", lambda v, b=b: v.tensor_copy(
                    St_run[0:64, b * 512:(b + 1) * 512],
                    pg[4 + b][0:64, 0:512]), None)
                pr.op("vector", lambda v, b=b: v.tensor_copy(
                    StB_run[0:64, b * 512:(b + 1) * 512],
                    pg[4 + b][0:64, 0:512]), "sEv")
            pr.wait("tensor", "sEv", pr.current("sEv"))
            for h in range(NH):
                bank = pg[h // 4]
                qc = (h % 4) * 128
                mm(bank[0:64, qc:qc + 128],
                   Y_all[0:128, h * 128:h * 128 + 64],
                   W2T_all[0:128, h * 128:(h + 1) * 128], True, True,
                   "sT" if h == NH - 1 else None)
            for h in range(NH):
                bank = pg[4 + h // 4]
                qc = (h % 4) * 128
                mm(bank[0:64, qc:qc + 128],
                   Y_all[0:128, h * 128 + 64:h * 128 + 128],
                   W2T_all[0:128, h * 128:(h + 1) * 128], True, True,
                   "sT" if h == NH - 1 else None)
            tv = pr.current("sT")
            pr.wait("vector", "sT", tv)
            for b in range(4):
                pr.op("vector", lambda v, b=b: v.tensor_copy(
                    O1ch[0:64, b * 512:(b + 1) * 512], pg[b][0:64, 0:512]),
                    None)

                def f(v, b=b):
                    v.tensor_scalar_mul(csc[0:64, 0:512],
                                        pg[4 + b][0:64, 0:512], -1.0)
                    return v.tensor_copy(PWch[0:64, b * 512:(b + 1) * 512],
                                         csc[0:64, 0:512])
                pr.op("vector", f, "sEv")
            pr.wait("sync", "sEv")
            pr.dma("sync", sp_O1[dc], O1ch[0:64, :], "sSp")
            pr.dma("sync", sp_Qef[dc], PWch[0:64, :], "sSp")
            pr.dma("sync", sp_Qfm[dc], qfm[0:64, :], "sSp")

        if checkpoint("C"):
            return nc, outs
        pr.wait("gpsimd", "sEv")
        pr.wait("gpsimd", "sV")
        pr.dma("gpsimd", agst[0:64, 0:NH * DH], St_run[0:64, :], "sG")
        pr.dma("gpsimd", agst[0:64, NH * DH:2 * NH * DH], Mp_run[0:64, :],
               "sG")
        pr.wait("gpsimd", "sG")
        pr.dma("gpsimd", ag1_in, agst[0:64, :], "sG")
        pr.wait("gpsimd", "sG")
        pr.op("gpsimd", lambda g: g.collective_compute(
            "AllGather", ALU.bypass, replica_groups=[list(range(8))],
            ins=[ag1_in.opt()], outs=[ag1_out.opt()]), "sCC")
        pr.wait("gpsimd", "sCC")
        ag1t = ar.bf16(2 * NH * DH)
        msel_f = ar.f32(64)
        msel_b = ar.bf16(64)
        stf = ar.f32(64)
        pr.op("vector", lambda v: v.memset(StIn[0:64, :], 0.0))
        pr.op("vector", lambda v: v.memset(StInB[0:64, :], 0.0), "sV")
        pr.wait("vector", "sV")
        for s in range(8):
            pr.wait("gpsimd", "sEv")
            pr.dma("gpsimd", ag1t[0:64, :], ag1_out[s], "sG")
            pr.wait("vector", "sG")
            base = 0
            for h in range(NH):
                Cs = ag1t[0:64, base + h * DH:base + (h + 1) * DH]
                Ms = ag1t[0:64, base + NH * DH + h * DH:
                          base + NH * DH + (h + 1) * DH]

                def f(v, Cs=Cs, Ms=Ms, s=s):
                    v.tensor_scalar(msel_f[0:64, :], Ms,
                                    cmask_t[0:64, s:s + 1], None, ALU.mult)
                    v.tensor_add(msel_f[0:64, :], msel_f[0:64, :],
                                 isel_t[0:64, s * 64:(s + 1) * 64])
                    return v.tensor_copy(msel_b[0:64, 0:64], msel_f[0:64, :])
                pr.op("vector", f, "sV")
                pr.wait("vector", "sV")
                pr.wait("tensor", "sV")
                pr.wait("tensor", "sEv", pr.current("sEv"))
                tv = mm(pg[0][0:64, 0:64], msel_b[0:64, 0:64],
                        StInB[0:64, h * DH:(h + 1) * DH], True, True, "sT")
                pr.wait("vector", "sT", tv)

                def f(v, Cs=Cs, s=s, h=h):
                    v.tensor_scalar(stf[0:64, :], Cs, cmask_t[0:64, s:s + 1],
                                    None, ALU.mult)
                    v.tensor_add(StIn[0:64, h * DH:(h + 1) * DH],
                                 pg[0][0:64, 0:64], stf[0:64, :])
                    return v.tensor_copy(StInB[0:64, h * DH:(h + 1) * DH],
                                         StIn[0:64, h * DH:(h + 1) * DH])
                pr.op("vector", f, "sEv")
                pr.wait("vector", "sEv")

        if checkpoint("AG1"):
            return nc, outs
        ar.pop()
        ar.push()
        FP = tok
        NFC = FP // 1024
        _xo = ar.off
        xn1f = ar.bf16(HB * FP)
        wstD = [ar.bf16(1024) for _ in range(8)]
        hbuf = ar.bf16(4 * FP)
        hsl2 = [ar.bf16(1024) for _ in range(6)]
        g1b = ar.b[0:128, _xo:_xo + HB * 1024]
        woutD = ar.bf16(NH * D, parts=64)
        MpD = ar.bf16(NH * DH)
        StlD = ar.f32(NH * DH)
        O1D = ar.f32(NH * CD)
        PWD = ar.bf16(NH * CD)
        QfD = ar.bf16(NH * CD)
        StaD = ar.bf16(NH * DH)
        Ofm = ar.bf16(NH * CD)
        s1b = ar.f32(HB * CD)
        x1tmD = ar.f32(D)
        sgtD = ar.f32(D)
        dsc = ar.f32(512)

        pr.wait("sync", "sSp")
        xn1f3 = xn1f.rearrange("p (hb t) -> p hb t", hb=HB)
        for dc in range(ndc):
            pr.dma("sync", xn1f3[:, :, dc * CD:(dc + 1) * CD], sp_xn1b[dc],
                   "sX")
        pr.wait("tensor", "sX")

        w_last = {}
        h_last = {}

        def stage(w_name, scales, act, nk, nob, src_sbuf, src_dram, dst_dram,
                  second=None):
            for ogr in range(nob // 4):
                for fc in range(NFC):
                    f0 = fc * 1024
                    pr.wait("tensor", "sEv", pr.current("sEv"))
                    kb_tv = {}
                    for kb in range(nk):
                        wslot = wstD[kb % 8]
                        pr.wait("gpsimd", "sT",
                                w_last.get(kb % 8, pr.current("sT")))
                        pr.dma("gpsimd", wslot[:, 0:512],
                               Wd[w_name][kb * 128:(kb + 1) * 128,
                                          ogr * 512:(ogr + 1) * 512], "sW")
                        if src_dram is not None:
                            hslot = hsl2[kb % 6]
                            pr.wait("sync", "sT",
                                    h_last.get(kb % 6, pr.current("sT")))
                            pr.dma("sync", hslot[:, 0:1024],
                                   src_dram[kb][:, f0:f0 + 1024], "sX")
                            def rr(fi, hslot=hslot):
                                return hslot[:, fi * 512:fi * 512 + 512]
                        else:
                            def rr(fi, kb=kb, f0=f0):
                                return src_sbuf[:, kb * FP + f0 + fi * 512:
                                                kb * FP + f0 + fi * 512 + 512]
                        pr.wait("tensor", "sW", pr.current("sW"))
                        pr.wait("tensor", "sX", pr.current("sX"))
                        for ob in range(4):
                            for fi in range(2):
                                kb_tv[kb] = mm(
                                    pg[2 * ob + fi][:, 0:512],
                                    wslot[:, ob * 128:(ob + 1) * 128],
                                    rr(fi),
                                    kb == 0, kb == nk - 1,
                                    "sT" if (ob == 3 and fi == 1) else None)
                        w_last[kb % 8] = kb_tv[kb]
                        if src_dram is not None:
                            h_last[kb % 6] = kb_tv[kb]
                    tv = pr.current("sT")
                    pr.wait("scalar", "sT", tv)
                    if dst_dram is not None:
                        pr.wait("scalar", "sSp", pr.current("sSp"))
                    for ob in range(4):
                        o = ogr * 4 + ob
                        for fi in range(2):
                            pr.op("scalar", lambda s, o=o, ob=ob, fi=fi:
                                  s.activation(
                                      hbuf[:, ob * FP + fi * 512:
                                           ob * FP + fi * 512 + 512],
                                      pg[2 * ob + fi][:, 0:512], act,
                                      scale=scales[:, o:o + 1]), "sEv")
                    if second is not None:
                        w2n, s2 = second
                        pr.wait("tensor", "sEv", pr.current("sEv"))
                        kb_tv2 = {}
                        for kb in range(nk):
                            wslot = wstD[kb % 8]
                            pr.wait("gpsimd", "sT",
                                    w_last.get(kb % 8, pr.current("sT")))
                            pr.dma("gpsimd", wslot[:, 0:512],
                                   Wd[w2n][kb * 128:(kb + 1) * 128,
                                           ogr * 512:(ogr + 1) * 512], "sW")
                            pr.wait("tensor", "sW", pr.current("sW"))
                            for ob in range(4):
                                for fi in range(2):
                                    kb_tv2[kb] = mm(
                                        pg[2 * ob + fi][:, 0:512],
                                        wslot[:, ob * 128:(ob + 1) * 128],
                                        src_sbuf[:, kb * FP + f0 + fi * 512:
                                                 kb * FP + f0 + fi * 512 + 512],
                                        kb == 0, kb == nk - 1,
                                        "sT" if (ob == 3 and fi == 1) else None)
                            w_last[kb % 8] = kb_tv2[kb]
                        tv = pr.current("sT")
                        pr.wait("vector", "sT", tv)
                        for ob in range(4):
                            o = ogr * 4 + ob
                            for fi in range(2):
                                pr.op("vector", lambda v, o=o, ob=ob, fi=fi:
                                      v.scalar_tensor_tensor(
                                          hbuf[:, ob * FP + fi * 512:
                                               ob * FP + fi * 512 + 512],
                                          pg[2 * ob + fi][:, 0:512],
                                          s2[:, o:o + 1],
                                          hbuf[:, ob * FP + fi * 512:
                                               ob * FP + fi * 512 + 512],
                                          ALU.mult, ALU.mult), "sEv")
                    if dst_dram is not None:
                        pr.wait("sync", "sEv")
                        for ob in range(4):
                            o = ogr * 4 + ob
                            pr.dma("sync", dst_dram[o][:, f0:f0 + 1024],
                                   hbuf[:, ob * FP:ob * FP + 1024], "sSp")
            return

        stage("w1T", vecf_t["w1s"], AF.Silu, HB, FB, xn1f, None, sp_ha,
              second=("w2T", vecf_t["w2s"]))
        stage("h1T", vecf_t["h1s"], AF.Gelu, FB, FB, None, sp_ha, sp_hb)
        stage("h2T", vecf_t["h2s"], AF.Gelu, FB, FB, None, sp_hb, sp_hc)

        pr.wait("gpsimd", "sSp")
        for kb in range(HB):
            pr.dma("gpsimd", woutD[0:64, kb * 2048:(kb + 1) * 2048],
                   Wd["WoutH"][:, kb * 2048:(kb + 1) * 2048], "sW")
        wv_d = pr.current("sW")
        for fc in range(NFC):
            f0 = fc * 1024
            for ogr in range(2):
                pr.wait("tensor", "sEv", pr.current("sEv"))
                kb_tv = {}
                for kb in range(FB):
                    wslot = wstD[kb % 8]
                    pr.wait("gpsimd", "sT",
                            w_last.get(kb % 8, pr.current("sT")))
                    pr.dma("gpsimd", wslot[:, 0:512],
                           Wd["w3T"][kb * 128:(kb + 1) * 128,
                                     ogr * 512:(ogr + 1) * 512], "sW")
                    hslot = hsl2[kb % 6]
                    pr.wait("sync", "sT",
                            h_last.get(kb % 6, pr.current("sT")))
                    pr.dma("sync", hslot[:, 0:1024], sp_hc[kb][:, f0:f0 + 1024],
                           "sX")
                    pr.wait("tensor", "sW", pr.current("sW"))
                    pr.wait("tensor", "sX", pr.current("sX"))
                    for ob in range(4):
                        for fi in range(2):
                            kb_tv[kb] = mm(
                                pg[2 * ob + fi][:, 0:512],
                                wslot[:, ob * 128:(ob + 1) * 128],
                                hslot[:, fi * 512:fi * 512 + 512],
                                kb == 0, kb == FB - 1,
                                "sT" if (ob == 3 and fi == 1) else None)
                    w_last[kb % 8] = kb_tv[kb]
                    h_last[kb % 6] = kb_tv[kb]
                tv = pr.current("sT")
                pr.wait("scalar", "sT", tv)
                for ob in range(4):
                    o = ogr * 4 + ob
                    for fi in range(2):
                        pr.op("scalar", lambda s, o=o, ob=ob, fi=fi:
                              s.activation(
                                  g1b[:, o * 1024 + fi * 512:
                                      o * 1024 + fi * 512 + 512],
                                  pg[2 * ob + fi][:, 0:512], AF.Silu,
                                  scale=vec_t["w3s"][:, o:o + 1]), "sEv")
            if "dbgD" in taps:
                if fc == 0:
                    d_g1 = dout("d_g1", [NFC, 128, HB * 1024])
                    d_s1 = dout("d_s1", [ndc, 128, HB * CD])
                pr.wait("sync", "sEv")
                pr.dma("sync", d_g1[fc], g1b[:], "sO")
            pr.wait("tensor", "sW", wv_d)
            for dci in range(1024 // CD):
                dc = fc * (1024 // CD) + dci
                r0 = dc * CD
                gco = dci * CD
                pr.wait("sync", "sV")
                pr.dma("sync", MpD[0:64, :], sp_Mp[dc], "sX")
                pr.dma("sync", StlD[0:64, :], sp_Stl[dc], "sX")
                pr.dma("sync", O1D[0:64, :], sp_O1[dc], "sX")
                pr.dma("sync", PWD[0:64, :], sp_Qef[dc], "sX")
                pr.dma("sync", QfD[0:64, :], sp_Qfm[dc], "sX")
                pr.dma("sync", x1tmD[:], sp_x1[r0:r0 + CD, :], "sX")
                xv = pr.current("sX")
                pr.wait("tensor", "sX", xv)
                pr.wait("vector", "sX", xv)
                for h in range(NH):
                    pr.wait("tensor", "sEv", pr.current("sEv"))
                    tv = mm(pg[0][0:64, 0:64], MpD[0:64, h * DH:(h + 1) * DH],
                            StInB[0:64, h * DH:(h + 1) * DH], True, True, "sT")
                    pr.wait("vector", "sT", tv)
                    pr.op("vector", lambda v, h=h: v.tensor_add(
                        dsc[0:64, 0:64], pg[0][0:64, 0:64],
                        StlD[0:64, h * DH:(h + 1) * DH]), None)
                    pr.op("vector", lambda v, h=h: v.tensor_copy(
                        StaD[0:64, h * DH:(h + 1) * DH], dsc[0:64, 0:64]),
                        "sEv")
                    pr.wait("tensor", "sEv", pr.current("sEv"))
                    mm(pg[1][0:64, 0:128], StaD[0:64, h * DH:(h + 1) * DH],
                       QfD[0:64, h * CD:(h + 1) * CD], True, False)
                    tv = mm(pg[1][0:64, 0:128], StaD[0:64, h * DH:(h + 1) * DH],
                            PWD[0:64, h * CD:(h + 1) * CD], False, True, "sT")
                    pr.wait("vector", "sT", tv)

                    def f(v, h=h):
                        v.tensor_add(dsc[0:64, 0:128], pg[1][0:64, 0:128],
                                     O1D[0:64, h * CD:(h + 1) * CD])
                        return v.tensor_copy(Ofm[0:64, h * CD:(h + 1) * CD],
                                             dsc[0:64, 0:128])
                    pr.op("vector", f, "sEv")
                for o in range(HB):
                    bank = pg[2 + (o % 2)]
                    pr.wait("tensor", "sEv", pr.current("sEv"))
                    for h in range(NH):
                        mm(bank[:, 0:CD],
                           woutD[0:64, h * D + o * 128:h * D + o * 128 + 128],
                           Ofm[0:64, h * CD:(h + 1) * CD],
                           h == 0, h == NH - 1, "sT" if h == NH - 1 else None)
                    tv = pr.current("sT")
                    pr.wait("vector", "sT", tv)
                    pr.op("vector", lambda v, o=o, bank=bank: v.tensor_copy(
                        s1b[:, o * CD:(o + 1) * CD], bank[:, 0:CD]), "sEv")
                if "dbgD" in taps:
                    pr.wait("sync", "sEv")
                    pr.dma("sync", d_s1[dc], s1b[:], "sO")
                pr.wait("vector", "sEv")

                def f(v, gco=gco):
                    for hb in range(HB):
                        v.tensor_mul(s1b[:, hb * CD:(hb + 1) * CD],
                                     s1b[:, hb * CD:(hb + 1) * CD],
                                     g1b[:, hb * 1024 + gco:
                                         hb * 1024 + gco + CD])
                    return v.engine_nop()
                pr.op("vector", f, "sV")
                pr.wait("tensor", "sV")
                for hb in range(HB):
                    pr.wait("tensor", "sEv", pr.current("sEv"))
                    ttv = pr.op("tensor", lambda t, hb=hb: t.transpose(
                        pt[:, 0:128], s1b[:, hb * CD:(hb + 1) * CD], id_f[:]),
                        "sT")
                    pr.wait("vector", "sT", ttv)
                    pr.op("vector", lambda v, hb=hb: v.tensor_copy(
                        sgtD[:, hb * 128:(hb + 1) * 128], pt[:, 0:128]), "sEv")
                pr.wait("vector", "sEv")
                pr.op("vector", lambda v: v.tensor_add(x1tmD[:], x1tmD[:],
                                                       sgtD[:]), "sV")
                pr.wait("sync", "sV")
                pr.dma("sync", out_d[r0:r0 + CD, :], x1tmD[:], "sO")
        pr.wait("sync", "sO")

        pr.emit(es.enter_context(nc.Block()))
    return nc, outs


def kernel(**inputs):
    ims = host_prep(inputs, TOK)
    nc, outs = build(TOK)
    res = run_bass_kernel_spmd(nc, ims, core_ids=list(range(8)))
    shards = [np.asarray(res.results[c]["out"]) for c in range(8)]
    full = np.stack([np.concatenate(shards[0:4], 0),
                     np.concatenate(shards[4:8], 0)], 0)
    return np.ascontiguousarray(full, np.float32)


# version 11
# speedup vs baseline: 1.5615x; 1.0013x over previous
import numpy as np
from contextlib import ExitStack
import concourse.bass as bass
import concourse.mybir as mybir
from concourse.bass_utils import run_bass_kernel_spmd

F32 = mybir.dt.float32
BF16 = mybir.dt.bfloat16
AF = mybir.ActivationFunctionType
ALU = mybir.AluOpType
AXX = mybir.AxisListType.X

D = 1024
HB = 8
NST = 16
FF = 4096
FB = 32
NH = 16
HP = NH // 2
DH = 64
EPS_LN = 1e-5
TOK = 2048
TC = 256
CD = 128
PASS = 1024
ENGS = ("sync", "tensor", "vector", "scalar", "gpsimd")


class Prog:
    def __init__(self, nc, es):
        self.nc, self.es = nc, es
        self.q = {e: [] for e in ENGS}
        self.sems, self.cnt = {}, {}

    def sem(self, name):
        if name not in self.sems:
            self.sems[name] = self.es.enter_context(self.nc.semaphore(name))
            self.cnt[name] = 0
        return self.sems[name]

    def op(self, eng, fn, sem=None, k=1):
        if sem is None:
            self.q[eng].append(fn)
            return None
        s = self.sem(sem)
        self.cnt[sem] += k

        def run(e, fn=fn, s=s, k=k):
            fn(e).then_inc(s, k)
        self.q[eng].append(run)
        return self.cnt[sem]

    def dma(self, eng, out, in_, sem):
        return self.op(eng, lambda e, o=out, i=in_: e.dma_start(out=o, in_=i),
                       sem, 16)

    def wait(self, eng, sem, val=None):
        s = self.sem(sem)
        v = self.cnt[sem] if val is None else val
        if v > 0:
            self.q[eng].append(lambda e, s=s, v=v: e.wait_ge(s, v))

    def current(self, sem):
        self.sem(sem)
        return self.cnt[sem]

    def emit(self, block):
        for e in ENGS:
            fns = self.q[e]
            if fns:
                def body(h, fns=fns):
                    for f in fns:
                        f(h)
                getattr(block, e)(body)


class Arena:

    def __init__(self, backing, cols):
        self.b = backing
        self.cols = cols
        self.off = 0
        self.marks = []

    def push(self):
        self.marks.append(self.off)

    def pop(self):
        self.off = self.marks.pop()

    def alloc(self, cols_f32, dt=F32, parts=128):
        cols_f32 = (cols_f32 + 3) & ~3
        assert self.off + cols_f32 <= self.cols, \
            f"arena overflow: {self.off}+{cols_f32} > {self.cols}"
        ap = self.b[0:parts, self.off:self.off + cols_f32]
        self.off += cols_f32
        if dt == BF16:
            ap = ap.bitcast(BF16)
        return ap

    def f32(self, n_elems_per_part, parts=128):
        return self.alloc(n_elems_per_part, F32, parts)

    def bf16(self, n_elems_per_part, parts=128):
        return self.alloc((n_elems_per_part + 1) // 2, BF16, parts)


def _f32(a):
    return np.ascontiguousarray(np.asarray(a), dtype=np.float32)


def host_prep(inputs, tok=TOK):
    x = np.asarray(inputs["L0_attn_x"])
    wT = {
        "WdT": _f32(np.asarray(inputs["L0_attn_Wd"]).T),
        "WbcT": _f32(np.asarray(inputs["L0_attn_Wbc"]).T),
        "W1T": _f32(np.asarray(inputs["L0_attn_W1"]).T),
        "W2T": _f32(np.asarray(inputs["L0_attn_W2"]).T),
        "WrT": _f32(np.asarray(inputs["L0_ffn_W_r"]).T),
        "WkT": _f32(np.asarray(inputs["L0_ffn_W_k"]).T),
        "WvT": _f32(np.asarray(inputs["L0_ffn_W_v"]).T),
        "Wq": _f32(inputs["L1_attn_Wq"]), "Wk1": _f32(inputs["L1_attn_Wk"]),
        "Wv1": _f32(inputs["L1_attn_Wv"]), "Wbeta": _f32(inputs["L1_attn_Wbeta"]),
        "Wout": _f32(inputs["L1_attn_Wout"]),
        "w1T": _f32(np.asarray(inputs["L1_ffn_w1_w"]).T),
        "w2T": _f32(np.asarray(inputs["L1_ffn_w2_w"]).T),
        "h1T": _f32(np.asarray(inputs["L1_ffn_h1_w"]).T),
        "h2T": _f32(np.asarray(inputs["L1_ffn_h2_w"]).T),
        "w3T": _f32(np.asarray(inputs["L1_ffn_w3_w"]).T),
    }
    fvec = lambda v: _f32(np.asarray(v).reshape(-1, 128).T)
    vecs = {"ln1_g": "L0_attn_ln1_g", "ln1_b": "L0_attn_ln1_b",
            "bd": "L0_attn_bd", "D0": "L0_attn_D", "ln2_g": "L0_attn_ln2_g",
            "ln2_b": "L0_attn_ln2_b", "b1": "L0_attn_b1", "b2": "L0_attn_b2",
            "mu_r": "L0_ffn_mu_r", "mu_k": "L0_ffn_mu_k",
            "w1s": "L1_ffn_w1_s", "w2s": "L1_ffn_w2_s", "h1s": "L1_ffn_h1_s",
            "h2s": "L1_ffn_h2_s", "w3s": "L1_ffn_w3_s"}
    vec_sb = {k: fvec(inputs[v]) for k, v in vecs.items()}
    A_log = np.asarray(inputs["L0_attn_A_log"])
    A_sb = _f32(A_log.reshape(HB, 128, NST).transpose(1, 0, 2).reshape(128, HB * NST))
    bbc_col = np.zeros((128, 1), np.float32)
    bbc_col[:32, 0] = np.asarray(inputs["L0_attn_bbc"])
    bbeta_r = _f32(np.broadcast_to(np.asarray(inputs["L1_attn_b_beta"])[None, :],
                                   (128, NH)))
    import ml_dtypes
    consts = {
        "A_sb": A_sb, "bbc_col": _f32(bbc_col), "bbeta_r": bbeta_r,
        "ident_bf": np.eye(128).astype(ml_dtypes.bfloat16),
        "ident_f32": _f32(np.eye(128)),
        "trilS": _f32(np.tril(np.ones((128, 128)), -1)),
        "triuI": _f32(np.triu(np.ones((128, 128)), 0)),
    }
    nsel = np.zeros((32, 32 * 128), np.float32)
    for n in range(32):
        nsel[n, n * 128:(n + 1) * 128] = 1.0
    consts["nsel"] = _f32(nsel)
    consts["trilS4"] = _f32(np.tile(np.tril(np.ones((128, 128)), -1), (1, 4)))
    consts["triuI4"] = _f32(np.tile(np.triu(np.ones((128, 128)), 0), (1, 4)))
    consts["id64s"] = _f32(np.tile(np.eye(64), (1, 8)))
    ims = []
    for c in range(8):
        b, q = divmod(c, 4)
        t0 = q * tok
        im = dict(wT)
        im.update(vec_sb)
        im.update(consts)
        im["xin"] = _f32(x[b, t0:t0 + tok, :])
        xp = np.zeros(D, np.float32) if q == 0 else np.asarray(x[b, t0 - 1, :])
        im["xprev"] = _f32(np.asarray(xp).reshape(HB, 128).T)
        m = np.array([1.0 if (s // 4 == b and s % 4 < q) else 0.0
                      for s in range(8)], np.float32)
        im["cmask"] = _f32(np.broadcast_to(m[None, :], (128, 8)))
        isel = np.zeros((64, 8 * 64), np.float32)
        for s in range(8):
            isel[:, s * 64:(s + 1) * 64] = (1.0 - m[s]) * np.eye(64)
        im["isel"] = _f32(isel)
        wout = np.asarray(inputs["L1_attn_Wout"])
        im["WoutH"] = _f32(wout.reshape(NH, DH, D).transpose(1, 0, 2)
                           .reshape(DH, NH * D))
        ims.append(im)
    return ims


def build(tok=TOK, taps=(), stop_after=None):
    assert tok % 1024 == 0
    nch = tok // TC
    ntt = TC // 128
    ndc = tok // CD
    npass = tok // PASS
    taps = set(taps)
    nc = bass.Bass("TRN2", target_bir_lowering=False, debug=False)
    es = ExitStack()

    def din(name, shape, dt=F32):
        return nc.dram_tensor(name, shape, dt, kind="ExternalInput").ap()

    outs = {}

    def dout(name, shape, dt=F32):
        outs[name] = nc.dram_tensor(name, shape, dt, kind="ExternalOutput").ap()
        return outs[name]

    xin = din("xin", [tok, D])
    xprev_d = din("xprev", [128, HB])
    Wd = {k: din(k, list(s)) for k, s in {
        "WdT": (D, D), "WbcT": (D, 2 * NST), "W1T": (D, FF), "W2T": (FF, D),
        "WrT": (D, D), "WkT": (D, D), "WvT": (D, D),
        "Wq": (D, D), "Wk1": (D, D), "Wv1": (D, D), "Wbeta": (D, NH),
        "Wout": (D, D), "w1T": (D, FF), "w2T": (D, FF),
        "h1T": (FF, FF), "h2T": (FF, FF), "w3T": (FF, D)}.items()}
    VEC8 = {k: din(k, [128, HB]) for k in ["ln1_g", "ln1_b", "bd", "D0",
                                           "ln2_g", "ln2_b", "b2", "mu_r",
                                           "mu_k", "w3s"]}
    VECF = {k: din(k, [128, FB]) for k in ["b1", "w1s", "w2s", "h1s", "h2s"]}
    A_sb_d = din("A_sb", [128, HB * NST])
    bbc_col_d = din("bbc_col", [128, 1])
    bbeta_d = din("bbeta_r", [128, NH])
    id_bf_d = din("ident_bf", [128, 128], BF16)
    id_f_d = din("ident_f32", [128, 128])
    trilS_d = din("trilS", [128, 128])
    triuI_d = din("triuI", [128, 128])
    nsel_d = din("nsel", [32, 32 * 128])
    trilS4_d = din("trilS4", [128, 512])
    triuI4_d = din("triuI4", [128, 512])
    id64s_d = din("id64s", [64, 512])
    cmask_d = din("cmask", [128, 8])
    isel_d = din("isel", [64, 8 * 64])
    Wd["WoutH"] = din("WoutH", [64, NH * D])
    out_d = dout("out", [tok, D])

    def scratch(name, shape, dt=F32):
        return nc.dram_tensor(name, shape, dt).ap()

    sp_cumd = scratch("sp_cumd", [nch, 128, HB * TC])
    sp_bc = scratch("sp_bc", [nch, 32, TC])
    mk_sp = dout if "dbg0" in taps else scratch
    sp_u = mk_sp("sp_u", [nch, 128, HB * TC])
    sp_y = mk_sp("sp_y", [nch, 128, HB * TC])
    sp_rp = mk_sp("sp_rp", [nch, 128, HB * TC])
    sp_v = mk_sp("sp_v", [nch, 128, HB * TC])
    if "x1" in taps:
        sp_x1 = dout("sp_x1", [tok, D])
    else:
        sp_x1 = scratch("sp_x1", [tok, D])
    sp_xn1b = scratch("sp_xn1b", [ndc, 128, HB * CD], BF16)
    sp_O1 = (dout if "dbgC" in taps else scratch)("sp_O1", [ndc, 64, NH * CD])
    sp_Qef = (dout if "dbgC" in taps else scratch)("sp_Qef", [ndc, 64, NH * CD], BF16)
    sp_Mp = (dout if "dbgC" in taps else scratch)("sp_Mp", [ndc, 64, NH * DH], BF16)
    sp_Stl = (dout if "dbgC" in taps else scratch)("sp_Stl", [ndc, 64, NH * DH])
    mk_sp2 = dout if "dbgC" in taps else scratch
    sp_Qfm = mk_sp2("sp_Qfm", [ndc, 64, NH * CD], BF16)
    sp_ha = scratch("sp_ha", [FB, 128, tok], BF16)
    sp_hb = scratch("sp_hb", [FB, 128, tok], BF16)
    sp_hc = scratch("sp_hc", [FB, 128, tok], BF16)
    WB = {k: scratch("wb_" + k, list(s), BF16) for k, s in {
        "W1T": (D, FF), "W2T": (FF, D), "w1T": (D, FF), "w2T": (D, FF),
        "h1T": (FF, FF), "h2T": (FF, FF), "w3T": (FF, D)}.items()}
    ag0_in = scratch("ag0_in", [128, 256])
    ag0_out = scratch("ag0_out", [8, 128, 256])
    ag1_in = scratch("ag1_in", [64, 2 * NH * DH], BF16)
    ag1_out = scratch("ag1_out", [8, 64, 2 * NH * DH], BF16)

    def sb(name, shape, dt=F32):
        return es.enter_context(nc.sbuf_tensor(name, shape, dt))

    ARENA_COLS = 44544

    with es:
        pr = Prog(nc, es)
        vec_t = {k: sb("v_" + k, [128, HB]) for k in VEC8}
        vecf_t = {k: sb("vf_" + k, [128, FB]) for k in VECF}
        A_sb = sb("A_sbuf", [128, HB * NST])
        bbc_col = sb("bbc_colt", [128, 1])
        bbeta_t = sb("bbeta_t", [128, NH])
        id_bf = sb("id_bf", [128, 128], BF16)
        id_f = sb("id_f", [128, 128])
        trilS = sb("trilSt", [128, 128])
        triuI = sb("triuIt", [128, 128])
        cmask_t = sb("cmask_t", [128, 8])
        trilS4 = sb("trilS4t", [128, 512])
        triuI4 = sb("triuI4t", [128, 512])
        id64s = sb("id64st", [64, 512])
        isel_t = sb("isel_t", [64, 8 * 64])
        epsv = sb("epsv", [128, 1])
        xprev_t = sb("xprev_t", [128, HB])
        carry8 = sb("carry8", [128, 8])
        hcarry = sb("hcarry", [128, 128])
        dloc = sb("dloc", [128, 128])
        hin = sb("hin", [128, 128])
        stat = sb("stat", [128, 16])
        sstat = sb("sstat", [128, 8])
        sqs = sb("sqs", [128, D])
        StIn = sb("StIn", [64, NH * DH])
        StInB = sb("StInB", [64, NH * DH], BF16)
        arena_t = sb("arena", [128, ARENA_COLS])
        ar = Arena(arena_t, ARENA_COLS)

        pg = [es.enter_context(nc.psum_tensor(f"pg{i}", [128, 512], F32))
              for i in range(8)]
        pb = [pg[6], pg[7]]
        pt = pg[5]
        ptb = pg[5][:, 256:512].bitcast(BF16)

        nsel_t = ar.alloc(32 * 128)[0:32, :]
        for dst, src in [(A_sb[:], A_sb_d), (bbc_col[:], bbc_col_d),
                         (bbeta_t[:], bbeta_d), (id_bf[:], id_bf_d),
                         (id_f[:], id_f_d), (trilS[:], trilS_d),
                         (triuI[:], triuI_d), (nsel_t, nsel_d),
                         (trilS4[:], trilS4_d), (triuI4[:], triuI4_d),
                         (id64s[:], id64s_d),
                         (cmask_t[:], cmask_d), (isel_t[:], isel_d),
                         (xprev_t[:], xprev_d)]:
            pr.dma("sync", dst, src, "sX")
        for k in VEC8:
            pr.dma("sync", vec_t[k][:], VEC8[k], "sX")
        for k in VECF:
            pr.dma("sync", vecf_t[k][:], VECF[k], "sX")

        ar.push()
        wd_b = ar.bf16(HB * D)
        wbc_b = ar.bf16(HB * 2 * NST)
        wr_b = ar.bf16(HB * D)
        wk_b = ar.bf16(HB * D)
        wv_b = ar.bf16(HB * D)
        xtm = ar.f32(D)
        xntm = ar.f32(D)
        xn_fm = ar.f32(HB * (TC + 1))
        xn1_fm = ar.f32(HB * TC)
        xn1b_fm = ar.bf16(HB * TC)
        delta0 = ar.f32(HB * TC)
        dx0 = ar.f32(HB * TC)
        cumd = ar.f32(HB * TC)
        u_fm = ar.f32(HB * TC)
        y_fm = ar.f32(HB * TC)
        xr_b = ar.bf16(HB * TC)
        xk_b = ar.bf16(HB * TC)
        kact_b = ar.bf16(HB * TC)
        rpv_st = ar.f32(2 * TC)
        bc_ch = ar.alloc(TC)[0:32, :]
        a_scr = u_fm
        zring_b = ar.f32(4 * TC)
        zring = [zring_b[:, _zi * TC:(_zi + 1) * TC] for _zi in range(4)]
        b_scr = ar.f32(TC)
        h_scr = ar.f32(2 * TC)
        z_scr = b_scr

        for name, dst, w in [("WdT", wd_b, D), ("WbcT", wbc_b, 2 * NST),
                             ("WrT", wr_b, D), ("WkT", wk_b, D),
                             ("WvT", wv_b, D)]:
            for kb in range(HB):
                pr.dma("gpsimd", dst[:, kb * w:(kb + 1) * w],
                       Wd[name][kb * 128:(kb + 1) * 128, :], "sW")
        p_done = {}
        pr.wait("vector", "sX")
        pr.op("vector", lambda v: v.memset(epsv[:], EPS_LN))
        pr.op("vector", lambda v: v.memset(hcarry[:], 0.0), "sV")
        pr.wait("scalar", "sX")
        pr.op("scalar", lambda s: s.activation(A_sb[:], A_sb[:], AF.Exp), "sS")
        pr.wait("vector", "sS")
        pr.op("vector", lambda v: v.tensor_scalar_mul(A_sb[:], A_sb[:], -1.0), "sV")
        pr.wait("scalar", "sV")

        tap_jobs = []

        def tap(name, tile, shape):
            if name in taps:
                dd = dout("tap_" + name, shape)
                tap_jobs.append((dd, tile))


        def checkpoint(name):
            if stop_after == name:
                pr.wait("sync", "sV")
                pr.wait("sync", "sEv")
                pr.wait("sync", "sS")
                pr.wait("sync", "sSp")
                for dd, tile in tap_jobs:
                    pr.dma("sync", dd, tile[:], "sO")
                pr.dma("sync", out_d[0:128, :], sqs[:], "sO")
                pr.wait("sync", "sO")
                pr.emit(es.enter_context(nc.Block()))
                return True
            return False

        def mm(pj, lhsT, rhs, start, stop, sem=None):
            return pr.op("tensor",
                         lambda t, pj=pj, l=lhsT, r=rhs, a=start, o=stop:
                         t.matmul(pj, l, r, start=a, stop=o), sem)

        def sv(fn):
            pr.op("vector", fn, "sV")
            pr.wait("vector", "sV")

        def tm_ln(xt, c0, sst_col, out, gb=None):
            sv(lambda v: v.reduce_sum(stat[:, c0:c0 + 1], xt, axis=AXX))
            pr.wait("scalar", "sV")
            pr.op("scalar", lambda s: s.activation(sqs[:], xt, AF.Square), "sS")
            pr.wait("vector", "sS")
            sv(lambda v: v.reduce_sum(stat[:, c0 + 1:c0 + 2], sqs[:], axis=AXX))
            sv(lambda v: v.tensor_scalar(stat[:, c0:c0 + 1], stat[:, c0:c0 + 1],
                                         1.0 / D, None, ALU.mult))
            sv(lambda v: v.tensor_scalar(stat[:, c0 + 1:c0 + 2],
                                         stat[:, c0 + 1:c0 + 2], 1.0 / D, None,
                                         ALU.mult))
            sv(lambda v: v.tensor_mul(stat[:, c0 + 2:c0 + 3], stat[:, c0:c0 + 1],
                                      stat[:, c0:c0 + 1]))
            sv(lambda v: v.tensor_sub(stat[:, c0 + 1:c0 + 2],
                                      stat[:, c0 + 1:c0 + 2],
                                      stat[:, c0 + 2:c0 + 3]))
            pr.wait("scalar", "sV")

            def f(s):
                s.activation(sstat[:, sst_col:sst_col + 1],
                             stat[:, c0 + 1:c0 + 2], AF.Ln, bias=epsv[:])
                return s.activation(sstat[:, sst_col:sst_col + 1],
                                    sstat[:, sst_col:sst_col + 1], AF.Exp,
                                    scale=-0.5)
            pr.op("scalar", f, "sS")
            pr.wait("vector", "sS")
            sv(lambda v: v.tensor_mul(stat[:, c0 + 3:c0 + 4], stat[:, c0:c0 + 1],
                                      sstat[:, sst_col:sst_col + 1]))
            sv(lambda v: v.tensor_scalar_mul(stat[:, c0 + 3:c0 + 4],
                                             stat[:, c0 + 3:c0 + 4], -1.0))
            sv(lambda v: v.tensor_scalar(out, xt, sstat[:, sst_col:sst_col + 1],
                                         stat[:, c0 + 3:c0 + 4], ALU.mult,
                                         ALU.add))

        last_transpose = [0]

        for ch in range(nch):
            t0 = ch * TC
            dcur = delta0
            dxc = dx0

            for tt in range(ntt):
                row = t0 + tt * 128
                pr.wait("sync", "sT", last_transpose[0])
                pr.dma("sync", xtm[:], xin[row:row + 128, :], "sX")
                xv = pr.current("sX")
                pr.wait("vector", "sX", xv)
                pr.wait("scalar", "sX", xv)
                tm_ln(xtm[:], 0, 0, xntm[:])
                pr.wait("scalar", "sV")
                tm_ln(xntm[:], 4, 4, xtm[:])
                vready = pr.current("sV")

                pr.wait("tensor", "sV", vready)
                for hb in range(HB):
                    pr.wait("tensor", "sEv", pr.current("sEv"))
                    pr.op("tensor", lambda t, hb=hb: t.transpose(
                        pt[:, 0:128], xntm[:, hb * 128:(hb + 1) * 128], id_f[:]))
                    tv = pr.op("tensor", lambda t, hb=hb: t.transpose(
                        pt[:, 128:256], xtm[:, hb * 128:(hb + 1) * 128],
                        id_f[:]), "sT")
                    pr.wait("vector", "sT", tv)
                    c0 = hb * (TC + 1) + 1 + tt * 128
                    c1 = hb * TC + tt * 128

                    def f(v, c0=c0, c1=c1, hb=hb):
                        v.tensor_copy(xn_fm[:, c0:c0 + 128], pt[:, 0:128])
                        v.tensor_scalar(xn1_fm[:, c1:c1 + 128], pt[:, 128:256],
                                        vec_t["ln1_g"][:, hb:hb + 1],
                                        vec_t["ln1_b"][:, hb:hb + 1],
                                        ALU.mult, ALU.add)
                        return v.tensor_copy(xn1b_fm[:, c1:c1 + 128],
                                             xn1_fm[:, c1:c1 + 128])
                    pr.op("vector", f, "sEv")
                last_transpose[0] = pr.current("sT")

            pr.wait("vector", "sSp")

            def f(v):
                last = None
                for hb in range(HB):
                    cb = hb * (TC + 1)
                    last = v.tensor_copy(xn_fm[:, cb:cb + 1], xprev_t[:, hb:hb + 1])
                return last
            pr.op("vector", f, "sV")
            pr.wait("vector", "sV")

            def f(v):
                for hb in range(HB):
                    cb = hb * (TC + 1)
                    c1 = hb * TC
                    v.scalar_tensor_tensor(u_fm[:, c1:c1 + TC],
                                           xn1_fm[:, c1:c1 + TC],
                                           vec_t["D0"][:, hb:hb + 1],
                                           xn_fm[:, cb + 1:cb + 1 + TC],
                                           ALU.mult, ALU.add)
                    v.tensor_sub(z_scr[:, 0:TC], xn_fm[:, cb + 1:cb + 1 + TC],
                                 xn_fm[:, cb:cb + TC])
                    v.scalar_tensor_tensor(xr_b[:, c1:c1 + TC], z_scr[:, 0:TC],
                                           vec_t["mu_r"][:, hb:hb + 1],
                                           xn_fm[:, cb:cb + TC], ALU.mult, ALU.add)
                    v.scalar_tensor_tensor(xk_b[:, c1:c1 + TC], z_scr[:, 0:TC],
                                           vec_t["mu_k"][:, hb:hb + 1],
                                           xn_fm[:, cb:cb + TC], ALU.mult, ALU.add)
                    v.tensor_copy(xprev_t[:, hb:hb + 1],
                                  xn_fm[:, cb + TC:cb + TC + 1])
                return v.engine_nop()
            pr.op("vector", f, "sV")
            pr.wait("sync", "sV")
            pr.dma("sync", sp_u[ch], u_fm[:], "sSp")

            pr.wait("tensor", "sW")

            def gemm_fm(evac, wsb, width, rhs_sb, bank0=0):
                for o in range(width // 128):
                    bank = pg[(bank0 + o) % 4]
                    pr.wait("tensor", "sEv", pr.current("sEv"))
                    for kb in range(HB):
                        mm(bank[:, 0:TC],
                           wsb[:, kb * width + o * 128: kb * width + o * 128 + 128],
                           rhs_sb[:, kb * TC:(kb + 1) * TC],
                           kb == 0, kb == HB - 1, "sT" if kb == HB - 1 else None)
                    evac(o, bank[:, 0:TC], pr.current("sT"))

            def evac_delta(o, pap, tval):
                pr.wait("scalar", "sT", tval)

                def f(s, o=o, pap=pap):
                    s.activation(sqs[:, 0:TC], pap, AF.Exp,
                                 bias=vec_t["bd"][:, o:o + 1])
                    return s.activation(dcur[:, o * TC:(o + 1) * TC],
                                        sqs[:, 0:TC], AF.Ln, bias=1.0)
                pr.op("scalar", f, "sEv")
            gemm_fm(evac_delta, wd_b, D, xn1b_fm, 0)

            pr.wait("tensor", "sEv", pr.current("sEv"))
            for kb in range(HB):
                mm(pg[0][0:32, 0:TC], wbc_b[:, kb * 32:(kb + 1) * 32],
                   xn1b_fm[:, kb * TC:(kb + 1) * TC], kb == 0, kb == HB - 1,
                   "sT" if kb == HB - 1 else None)
            pr.wait("vector", "sT")
            pr.wait("vector", "sSp")
            pr.op("vector", lambda v: v.tensor_scalar_add(
                bc_ch, pg[0][0:32, 0:TC], bbc_col[0:32, :]), "sEv")
            pr.wait("sync", "sEv")
            pr.dma("sync", sp_bc[ch], bc_ch, "sSp")

            pr.wait("tensor", "sV")

            def evac_spill(dst_dram):
                def evac(o, pap, tval):
                    slot = rpv_st[:, (o % 2) * TC:(o % 2 + 1) * TC]
                    pr.wait("scalar", "sSp", max(0, pr.current("sSp") - 16))
                    pr.wait("scalar", "sT", tval)
                    ev = pr.op("scalar", lambda s, slot=slot, pap=pap:
                               s.activation(slot, pap, AF.Copy), "sEv")
                    pr.wait("sync", "sEv", ev)
                    pr.dma("sync", dst_dram[:, o * TC:(o + 1) * TC], slot, "sSp")
                return evac
            gemm_fm(evac_spill(sp_rp[ch]), wr_b, D, xr_b, 1)

            def evac_kact(o, pap, tval):
                pr.wait("scalar", "sT", tval)

                def f(s, o=o, pap=pap):
                    s.activation(sqs[:, 0:TC], pap, AF.Relu)
                    return s.activation(kact_b[:, o * TC:(o + 1) * TC],
                                        sqs[:, 0:TC], AF.Square)
                pr.op("scalar", f, "sEv")
            gemm_fm(evac_kact, wk_b, D, xk_b, 2)
            gemm_fm(evac_spill(sp_v[ch]), wv_b, D, kact_b, 3)

            pr.wait("vector", "sEv")

            def f(v):
                for hb in range(HB):
                    c = hb * TC
                    v.tensor_mul(dxc[:, c:c + TC], dcur[:, c:c + TC],
                                 xn1_fm[:, c:c + TC])
                return v.engine_nop()
            pr.op("vector", f, "sV")
            pr.wait("vector", "sSp")
            if ch > 0:
                def f(v):
                    last = None
                    for hb in range(HB):
                        last = v.tensor_copy(carry8[:, hb:hb + 1],
                                             cumd[:, hb * TC + TC - 1:hb * TC + TC])
                    return last
                pr.op("vector", f, "sV")
                pr.wait("vector", "sV")

            def f(v, ch=ch):
                for hb in range(HB):
                    c = hb * TC
                    ini = carry8[:, hb:hb + 1] if ch > 0 else 0.0
                    v.tensor_tensor_scan(cumd[:, c:c + TC], dcur[:, c:c + TC],
                                         dcur[:, c:c + TC], ini, ALU.add,
                                         ALU.bypass)
                return v.engine_nop()
            pr.op("vector", f, "sV")
            pr.wait("sync", "sV")
            pr.dma("sync", sp_cumd[ch], cumd[:], "sSp")

            pr.wait("vector", "sSp")
            pr.wait("scalar", "sSp", pr.current("sSp"))
            pr.wait("vector", "sG2")
            sc_base = pr.current("sSc")
            for n in range(NST):
                bank = pb[n % 2]
                if n == 0:
                    gv = sc_base - 8
                elif n == 1:
                    gv = sc_base
                else:
                    gv = sc_base + 8 * (n - 1)
                pr.wait("tensor", "sSc", max(0, gv))
                selB = nsel_t[:, n * 128:(n + 1) * 128]
                selC = nsel_t[:, (NST + n) * 128:(NST + n + 1) * 128]
                mm(bank[:, 0:TC], selB, bc_ch, True, True)
                btv = mm(bank[:, TC:2 * TC], selC, bc_ch, True,
                         True, "sT")
                for hb in range(HB):
                    idx = hb * NST + n
                    col = n * HB + hb
                    asl = a_scr[:, (col % 4) * TC:(col % 4 + 1) * TC]
                    pr.wait("scalar", "sSc", max(0, pr.current("sSc") - 3))
                    svv = pr.op("scalar", lambda s, asl=asl, hb=hb, idx=idx:
                                s.activation(asl, dcur[:, hb * TC:hb * TC + TC],
                                             AF.Exp, scale=A_sb[:, idx:idx + 1]),
                                "sS")
                    pr.wait("vector", "sT", btv)
                    pr.wait("vector", "sS", svv)
                    bsl = b_scr[:, 0:TC]
                    hsl = h_scr[:, (col % 2) * TC:(col % 2 + 1) * TC]

                    zsl = zring[col % 4]
                    if n > 0:
                        pr.wait("vector", "sG2", max(0, pr.current("sG2") - 3))

                    def f(v, bsl=bsl, hsl=hsl, asl=asl, hb=hb, n=n, col=col,
                          bank=bank, zsl=zsl):
                        if col >= 2:
                            pc = col - 2
                            v.tensor_copy(hcarry[:, pc:pc + 1],
                                          h_scr[:, (pc % 2) * TC + TC - 1:
                                                (pc % 2) * TC + TC])
                        v.tensor_mul(bsl, dxc[:, hb * TC:hb * TC + TC],
                                     bank[:, 0:TC])
                        v.tensor_tensor_scan(hsl, asl, bsl,
                                             hcarry[:, col:col + 1],
                                             ALU.mult, ALU.add)
                        if n == 0:
                            return v.tensor_mul(y_fm[:, hb * TC:hb * TC + TC],
                                                hsl, bank[:, TC:2 * TC])
                        return v.tensor_mul(zsl, hsl, bank[:, TC:2 * TC])
                    pr.op("vector", f, "sSc")
                    if n > 0:
                        pr.wait("gpsimd", "sSc")
                        pr.op("gpsimd", lambda g, hb=hb, zsl=zsl: g.tensor_add(
                            y_fm[:, hb * TC:hb * TC + TC],
                            y_fm[:, hb * TC:hb * TC + TC], zsl), "sG2")
            pr.wait("vector", "sSc")

            def f(v):
                last = None
                for pc in (NST * HB - 2, NST * HB - 1):
                    last = v.tensor_copy(hcarry[:, pc:pc + 1],
                                         h_scr[:, (pc % 2) * TC + TC - 1:
                                               (pc % 2) * TC + TC])
                return last
            pr.op("vector", f, "sV")
            pr.wait("sync", "sSc")
            pr.wait("sync", "sG2")
            pr.dma("sync", sp_y[ch], y_fm[:], "sSp")

        pr.wait("scalar", "sV")
        pr.wait("scalar", "sSc")

        def f(s):
            last = None
            for n in range(NST):
                for hb in range(HB):
                    idx = hb * NST + n
                    col = n * HB + hb
                    last = s.activation(dloc[:, col:col + 1],
                                        cumd[:, hb * TC + TC - 1:hb * TC + TC],
                                        AF.Exp, scale=A_sb[:, idx:idx + 1])
            return last
        pr.op("scalar", f, "sS")

        for name in ["W1T", "W2T"]:
            rows = Wd[name].shape[0]
            step = 512
            for r0 in range(0, rows, step):
                pr.dma("gpsimd", WB[name][r0:r0 + step, :],
                       Wd[name][r0:r0 + step, :], "sP")
            p_done[name] = pr.current("sP")

        tap("hcarry", hcarry, [128, 128])
        tap("dloc", dloc, [128, 128])

        if stop_after == "A":
            pr.wait("sync", "sS")
            pr.wait("sync", "sV")
            for dd, tile in tap_jobs:
                pr.dma("sync", dd, tile[:], "sO")
            pr.dma("sync", out_d[0:128, :], xtm[:], "sO")
            pr.wait("sync", "sO")
            pr.emit(es.enter_context(nc.Block()))
            return nc, outs

        if checkpoint("A2"):
            return nc, outs
        pr.wait("gpsimd", "sV")
        pr.wait("gpsimd", "sS")
        pr.wait("gpsimd", "sSp")
        agout = ar.f32(256)
        pr.dma("gpsimd", ag0_in[:, 0:128], hcarry[:], "sG")
        pr.dma("gpsimd", ag0_in[:, 128:256], dloc[:], "sG")
        pr.wait("gpsimd", "sG")
        pr.op("gpsimd", lambda g: g.collective_compute(
            "AllGather", ALU.bypass, replica_groups=[list(range(8))],
            ins=[ag0_in.opt()], outs=[ag0_out.opt()]), "sCC")
        pr.wait("gpsimd", "sCC")
        pr.op("vector", lambda v: v.memset(hin[:], 0.0), "sV")
        for s in range(8):
            pr.wait("gpsimd", "sV")
            pr.dma("gpsimd", agout[:, 0:256], ag0_out[s], "sG")
            gv = pr.current("sG")
            pr.wait("vector", "sG", gv)
            hs = agout[:, 0:128]
            ds = agout[:, 128:256]

            def f(v, hs=hs, ds=ds, s=s):
                v.tensor_scalar(h_scr[:, 0:128], ds, 1.0, None, ALU.subtract)
                v.tensor_scalar(h_scr[:, 0:128], h_scr[:, 0:128],
                                cmask_t[:, s:s + 1], 1.0, ALU.mult, ALU.add)
                v.tensor_mul(hin[:], hin[:], h_scr[:, 0:128])
                v.tensor_scalar(h_scr[:, 128:256], hs, cmask_t[:, s:s + 1],
                                None, ALU.mult)
                return v.tensor_add(hin[:], hin[:], h_scr[:, 128:256])
            pr.op("vector", f, "sV")
        pr.wait("vector", "sV")
        tap("hin", hin, [128, 128])


        if checkpoint("AG0"):
            return nc, outs
        ar.pop()
        ar.push()
        xn2b = ar.bf16(HB * PASS)
        hff = ar.bf16(FB * PASS)
        uch = ar.f32(HB * TC)
        ych = ar.f32(HB * TC)
        cumd_b = ar.f32(HB * TC)
        w_st = [ar.bf16(1024) for _ in range(8)]
        rv_b = ar.f32(2 * HB * TC)
        ssr = ych
        sq_b = ar.f32(TC)
        a_b = ar.f32(3 * TC)
        z_b = ar.f32(2 * TC)
        rows = ar.f32(2 * TC)
        bc_b = ar.alloc(TC)[0:32, :]
        zringB_b = ar.f32(3 * TC)
        zringB = [zringB_b[:, _zi * TC:(_zi + 1) * TC] for _zi in range(3)]
        sgt = ar.f32(D)
        xtm_b = ar.f32(D)
        ones_col = triuI[:, 127:128]
        ones_row = triuI[0:1, 0:128]

        for p in range(npass):
            for ci in range(PASS // TC):
                ch = p * (PASS // TC) + ci
                t0 = ch * TC
                co = ci * TC
                pr.wait("sync", "sV")
                pr.wait("sync", "sSc")
                pr.wait("sync", "sG2")
                pr.wait("sync", "sSp")
                pr.dma("sync", ych[:], sp_y[ch], "sX")
                pr.dma("sync", cumd_b[:], sp_cumd[ch], "sX")
                pr.dma("sync", uch[:], sp_u[ch], "sX")
                pr.wait("sync", "sT", pr.current("sT"))
                pr.dma("sync", bc_b, sp_bc[ch], "sX")
                xv = pr.current("sX")
                pr.wait("vector", "sX", xv)
                pr.wait("scalar", "sX", xv)
                pr.wait("tensor", "sX", xv)
                sc_base = pr.current("sSc")
                for n in range(NST):
                    bank = pb[n % 2]
                    gv = sc_base - 8 if n == 0 else (sc_base if n == 1
                                                     else sc_base + 8 * (n - 1))
                    pr.wait("tensor", "sSc", max(0, gv))
                    selC = nsel_t[:, (NST + n) * 128:(NST + n + 1) * 128]
                    btv = mm(bank[:, 0:TC], selC, bc_b, True, True, "sT")
                    for hb in range(HB):
                        idx = hb * NST + n
                        col = n * HB + hb
                        asl = a_b[:, (col % 3) * TC:(col % 3 + 1) * TC]
                        pr.wait("scalar", "sSc", max(0, pr.current("sSc") - 2))
                        svv = pr.op("scalar", lambda s, asl=asl, hb=hb, idx=idx:
                                    s.activation(asl,
                                                 cumd_b[:, hb * TC:hb * TC + TC],
                                                 AF.Exp,
                                                 scale=A_sb[:, idx:idx + 1]),
                                    "sS")
                        pr.wait("vector", "sT", btv)
                        pr.wait("vector", "sS", svv)

                        zsl = zringB[col % 3]
                        pr.wait("vector", "sG2", max(0, pr.current("sG2") - 2))

                        def f(v, asl=asl, hb=hb, col=col, bank=bank, zsl=zsl):
                            v.tensor_scalar(z_b[:, 0:TC], asl,
                                            hin[:, col:col + 1], None, ALU.mult)
                            return v.tensor_mul(zsl, z_b[:, 0:TC],
                                                bank[:, 0:TC])
                        pr.op("vector", f, "sSc")
                        pr.wait("gpsimd", "sSc")
                        pr.op("gpsimd", lambda g, hb=hb, zsl=zsl: g.tensor_add(
                            ych[:, hb * TC:hb * TC + TC],
                            ych[:, hb * TC:hb * TC + TC], zsl), "sG2")
                pr.wait("vector", "sSc")
                pr.wait("vector", "sG2")

                def f(v):
                    for hb in range(HB):
                        c = hb * TC
                        v.tensor_add(uch[:, c:c + TC], uch[:, c:c + TC],
                                     ych[:, c:c + TC])
                    return v.engine_nop()
                pr.op("vector", f, "sV")
                pr.wait("tensor", "sV")
                pr.wait("tensor", "sEv", pr.current("sEv"))
                for hb in range(HB):
                    mm(pg[0][0:1, 0:TC], ones_col, uch[:, hb * TC:hb * TC + TC],
                       hb == 0, hb == HB - 1, "sT" if hb == HB - 1 else None)
                pr.wait("scalar", "sV")
                for hb in range(HB):
                    if hb > 0:
                        pr.wait("scalar", "sT", pr.current("sT"))
                    stv = pr.op("scalar", lambda s, hb=hb: s.activation(
                        sqs[:, 0:TC], uch[:, hb * TC:hb * TC + TC], AF.Square),
                        "sS")
                    pr.wait("tensor", "sS", stv)
                    mm(pg[1][0:1, 0:TC], ones_col, sqs[:, 0:TC],
                       hb == 0, hb == HB - 1, "sT")
                tv = pr.current("sT")
                pr.wait("vector", "sT", tv)

                def f(v):
                    v.tensor_scalar(rows[0:1, 0:TC], pg[0][0:1, 0:TC],
                                    1.0 / D, None, ALU.mult)
                    v.tensor_scalar(rows[0:1, TC:2 * TC], pg[1][0:1, 0:TC],
                                    1.0 / D, None, ALU.mult)
                    v.tensor_mul(sq_b[0:1, 0:TC], rows[0:1, 0:TC],
                                 rows[0:1, 0:TC])
                    return v.tensor_sub(rows[0:1, TC:2 * TC],
                                        rows[0:1, TC:2 * TC], sq_b[0:1, 0:TC])
                pr.op("vector", f, "sV")
                pr.wait("scalar", "sV")

                def f(s):
                    s.activation(sq_b[0:1, 0:TC], rows[0:1, TC:2 * TC], AF.Ln,
                                 bias=epsv[0:1, :])
                    return s.activation(sq_b[0:1, 0:TC], sq_b[0:1, 0:TC],
                                        AF.Exp, scale=-0.5)
                pr.op("scalar", f, "sS")
                pr.wait("vector", "sS")

                def f(v):
                    v.tensor_mul(rows[0:1, 0:TC], rows[0:1, 0:TC],
                                 sq_b[0:1, 0:TC])
                    return v.tensor_scalar_mul(rows[0:1, 0:TC],
                                               rows[0:1, 0:TC], -1.0)
                pr.op("vector", f, "sV")
                pr.wait("tensor", "sV")
                pr.wait("tensor", "sEv", pr.current("sEv"))
                mm(pg[2][:, 0:TC], ones_row, sq_b[0:1, 0:TC], True, True)
                tv = mm(pg[3][:, 0:TC], ones_row, rows[0:1, 0:TC], True, True,
                        "sT")
                pr.wait("vector", "sT", tv)

                def f(v, co=co):
                    for hb in range(HB):
                        v.tensor_mul(z_b[:, 0:TC], uch[:, hb * TC:hb * TC + TC],
                                     pg[2][:, 0:TC])
                        v.tensor_add(z_b[:, 0:TC], z_b[:, 0:TC],
                                     pg[3][:, 0:TC])
                        v.tensor_scalar(
                            xn2b[:, hb * PASS + co:hb * PASS + co + TC],
                            z_b[:, 0:TC], vec_t["ln2_g"][:, hb:hb + 1],
                            vec_t["ln2_b"][:, hb:hb + 1], ALU.mult, ALU.add)
                    return v.engine_nop()
                pr.op("vector", f, "sEv")
                pr.wait("sync", "sEv")
                pr.dma("sync", sp_u[ch], uch[:], "sSp")

            if "dbgB" in taps and p == 0:
                d_rows = dout("d_rows", [1, 2 * TC])
                pr.wait("sync", "sV")
                pr.wait("sync", "sS")
                pr.dma("sync", d_rows, rows[0:1, :], "sO")
                d_sqb = dout("d_sqb", [1, TC])
                pr.dma("sync", d_sqb, sq_b[0:1, 0:TC], "sO")
                d_xn2b = dout("d_xn2b", [128, HB * PASS], BF16)
                pr.wait("sync", "sEv")
                pr.dma("sync", d_xn2b, xn2b[:], "sO")
            for og in range(4):
                pr.wait("gpsimd", "sT", pr.current("sT"))
                pr.wait("gpsimd", "sP", p_done["W1T"])
                for kb in range(HB):
                    pr.dma("gpsimd", w_st[kb],
                           WB["W1T"][kb * 128:(kb + 1) * 128,
                                     og * 1024:(og + 1) * 1024], "sW")
                wv = pr.current("sW")
                pr.wait("tensor", "sW", wv)
                for fi in range(PASS // 512):
                    fc = fi * 512
                    pr.wait("tensor", "sEv", pr.current("sEv"))
                    for kb in range(HB):
                        for ob in range(8):
                            mm(pg[ob][:, 0:512],
                               w_st[kb][:, ob * 128:(ob + 1) * 128],
                               xn2b[:, kb * PASS + fc:kb * PASS + fc + 512],
                               kb == 0, kb == HB - 1,
                               "sT" if (kb == HB - 1 and ob == 7) else None)
                    tv = pr.current("sT")
                    pr.wait("scalar", "sT", tv)
                    for ob in range(8):
                        o = og * 8 + ob
                        pr.op("scalar", lambda s, o=o, ob=ob, fc=fc:
                              s.activation(hff[:, o * PASS + fc:
                                               o * PASS + fc + 512],
                                           pg[ob][:, 0:512], AF.Gelu,
                                           bias=vecf_t["b1"][:, o:o + 1]),
                              "sEv")

            if "dbgB" in taps and p == 0:
                d_hff = dout("d_hff", [128, FB * PASS], BF16)
                pr.wait("sync", "sEv")
                pr.dma("sync", d_hff, hff[:], "sO")
            for fi in range(PASS // 512):
                fc = fi * 512
                pr.wait("tensor", "sEv", pr.current("sEv"))
                pr.wait("gpsimd", "sT", pr.current("sT"))
                pr.wait("gpsimd", "sP", p_done["W2T"])
                kb_tv = {}
                for kb in range(FB):
                    slot = w_st[kb % 8]
                    if kb >= 8:
                        pr.wait("gpsimd", "sT", kb_tv[kb - 8])
                    pr.dma("gpsimd", slot,
                           WB["W2T"][kb * 128:(kb + 1) * 128, :], "sW")
                    wv = pr.current("sW")
                    pr.wait("tensor", "sW", wv)
                    for ob in range(8):
                        kb_tv[kb] = mm(pg[ob][:, 0:512],
                                       slot[:, ob * 128:(ob + 1) * 128],
                                       hff[:, kb * PASS + fc:
                                           kb * PASS + fc + 512],
                                       kb == 0, kb == FB - 1,
                                       "sT" if ob == 7 else None)
                tv = pr.current("sT")
                for sci in range(512 // TC):
                    ch = p * (PASS // TC) + fi * (512 // TC) + sci
                    t0 = ch * TC
                    scc = sci * TC
                    pr.wait("sync", "sV")
                    pr.dma("sync", rv_b[:, 0:HB * TC], sp_rp[ch], "sX")
                    pr.dma("sync", rv_b[:, HB * TC:2 * HB * TC], sp_v[ch], "sX")
                    pr.dma("sync", ssr[:], sp_u[ch], "sX")
                    xv = pr.current("sX")
                    pr.wait("scalar", "sX", xv)
                    for hb in range(HB):
                        pr.op("scalar", lambda s, hb=hb: s.activation(
                            rv_b[:, hb * TC:hb * TC + TC],
                            rv_b[:, hb * TC:hb * TC + TC], AF.Sigmoid), "sS")
                    pr.wait("vector", "sS")
                    pr.wait("vector", "sX", xv)

                    def f(v):
                        for hb in range(HB):
                            c = hb * TC
                            v.tensor_mul(rv_b[:, c:c + TC], rv_b[:, c:c + TC],
                                         rv_b[:, HB * TC + c:HB * TC + c + TC])
                        return v.engine_nop()
                    pr.op("vector", f, "sV")
                    pr.wait("scalar", "sV")
                    for hb in range(HB):
                        pr.op("scalar", lambda s, hb=hb: s.activation(
                            rv_b[:, hb * TC:hb * TC + TC],
                            rv_b[:, hb * TC:hb * TC + TC], AF.Silu), "sS")
                    gsv = pr.current("sS")
                    pr.wait("vector", "sS", gsv)
                    pr.wait("vector", "sT", tv)

                    def f(v, scc=scc):
                        for hb in range(HB):
                            c = hb * TC
                            v.scalar_tensor_tensor(z_b[:, 0:TC],
                                                   pg[hb][:, scc:scc + TC],
                                                   vec_t["b2"][:, hb:hb + 1],
                                                   ssr[:, c:c + TC],
                                                   ALU.add, ALU.add)
                            v.tensor_mul(ssr[:, c:c + TC], z_b[:, 0:TC],
                                         rv_b[:, c:c + TC])
                        return v.engine_nop()
                    pr.op("vector", f, "sEv")
                    for tt2 in range(TC // 128):
                        row = t0 + tt2 * 128
                        pr.wait("sync", "sO")
                        pr.dma("sync", xtm_b[:], xin[row:row + 128, :], "sX")
                        xv2 = pr.current("sX")
                        for hb in range(HB):
                            pr.wait("tensor", "sEv", pr.current("sEv"))
                            ttv = pr.op("tensor", lambda t, hb=hb, tt2=tt2:
                                        t.transpose(
                                            pt[:, 0:128],
                                            ssr[:, hb * TC + tt2 * 128:
                                                hb * TC + tt2 * 128 + 128],
                                            id_f[:]), "sT")
                            pr.wait("vector", "sT", ttv)
                            pr.op("vector", lambda v, hb=hb: v.tensor_copy(
                                sgt[:, hb * 128:(hb + 1) * 128], pt[:, 0:128]),
                                "sEv")
                        pr.wait("vector", "sX", xv2)
                        pr.wait("vector", "sEv")
                        pr.op("vector", lambda v: v.tensor_add(
                            xtm_b[:], xtm_b[:], sgt[:]), "sV")
                        pr.wait("sync", "sV")
                        pr.dma("sync", sp_x1[row:row + 128, :], xtm_b[:], "sO")

        if checkpoint("B"):
            return nc, outs
        ar.pop()
        ar.push()
        wq_b = ar.bf16(HB * D)
        wk1_b = ar.bf16(HB * D)
        wv1_b = ar.bf16(HB * D)
        wbeta_b = ar.bf16(HB * NH)
        x1tm = ar.f32(D)
        x1tm2 = ar.f32(D)
        xn1tm = ar.f32(D)
        xn1bL = ar.bf16(HB * CD)
        qfm = ar.bf16(NH * CD)
        khat = ar.bf16(NH * DH)
        vtm = ar.bf16(NH * DH)
        kfm = ar.bf16(NH * CD)
        beta_c = ar.f32(NH)
        rstdk = ar.f32(NH)
        negb_c = ar.f32(NH)
        kss = ar.f32(NH)
        eps24 = ar.f32(4)
        M_all = ar.bf16(NH * 128)
        MT_all = ar.bf16(NH * 128)
        M2_all = ar.bf16(NH * 128)
        MT2_all = ar.bf16(NH * 128)
        Y_all = ar.bf16(NH * 128)
        W2T_all = ar.bf16(NH * 128)
        ImG_all = ar.bf16(NH * 64)
        ImGT_all = ar.bf16(NH * 64)
        Mp_run = ar.bf16(NH * DH)
        MpT_run = ar.bf16(NH * DH)
        St_run = ar.f32(NH * DH)
        StB_run = ar.bf16(NH * DH)
        O1ch = ar.f32(NH * CD)
        PWch = ar.bf16(NH * CD)
        agst = ar.bf16(2 * NH * DH)
        csc = ar.f32(512)

        for name, dst, w in [("Wq", wq_b, D), ("Wk1", wk1_b, D),
                             ("Wv1", wv1_b, D), ("Wbeta", wbeta_b, NH)]:
            for kb in range(HB):
                pr.dma("gpsimd", dst[:, kb * w:(kb + 1) * w],
                       Wd[name][kb * 128:(kb + 1) * 128, :], "sW")
        wv_c = pr.current("sW")
        pr.op("vector", lambda v: v.memset(eps24[:], 1e-24))
        pr.op("vector", lambda v: v.memset(St_run[0:64, :], 0.0))
        pr.op("vector", lambda v: v.memset(StB_run[0:64, :], 0.0))

        def f(v):
            for h in range(NH):
                v.tensor_copy(Mp_run[0:64, h * DH:(h + 1) * DH],
                              id_bf[0:64, 0:64])
                v.tensor_copy(MpT_run[0:64, h * DH:(h + 1) * DH],
                              id_bf[0:64, 0:64])
            return v.engine_nop()
        pr.op("vector", f, "sV")
        pr.wait("vector", "sV")

        x1bufs = [x1tm, x1tm2]
        pr.wait("sync", "sV")
        pr.wait("sync", "sEv")
        pr.dma("sync", x1bufs[0][:], sp_x1[0:CD, :], "sX")
        vv_ln = {}
        for dc in range(ndc):
            r0 = dc * CD
            if dc + 1 < ndc:
                pr.wait("sync", "sV", vv_ln.get(dc - 1, 0))
                pr.dma("sync", x1bufs[(dc + 1) % 2][:],
                       sp_x1[r0 + CD:r0 + 2 * CD, :], "sX")
            pr.wait("sync", "sV")
            pr.wait("sync", "sEv")
            pr.dma("sync", sp_Mp[dc], Mp_run[0:64, :], "sSp")
            pr.dma("sync", sp_Stl[dc], St_run[0:64, :], "sSp")
            xv = pr.current("sX")
            pr.wait("vector", "sX", xv)
            pr.wait("scalar", "sX", xv)
            tm_ln(x1bufs[dc % 2][:], 0, 0, xn1tm[:])
            vv_ln[dc] = pr.current("sV")
            vready = pr.current("sV")
            pr.wait("tensor", "sV", vready)
            pr.wait("tensor", "sW", wv_c)
            pr.wait("tensor", "sEv", pr.current("sEv"))
            for g in range(2):
                tbank = pg[5] if g == 0 else pg[6]
                for q in range(4):
                    hb = g * 4 + q
                    pr.op("tensor", lambda t, hb=hb, q=q, tbank=tbank:
                          t.transpose(tbank[:, q * 128:(q + 1) * 128],
                                      xn1tm[:, hb * 128:(hb + 1) * 128],
                                      id_f[:]), "sT" if q == 3 else None)
                pr.wait("scalar", "sT", pr.current("sT"))
                pr.op("scalar", lambda s, g=g, tbank=tbank: s.activation(
                    xn1bL[:, g * 512:(g + 1) * 512], tbank[:, 0:512],
                    AF.Copy), "sEv")
            pr.wait("sync", "sEv")
            pr.dma("sync", sp_xn1b[dc], xn1bL[:], "sSp")
            if dc == 0 and checkpoint("Cln"):
                return nc, outs
            pr.wait("tensor", "sEv", pr.current("sEv"))
            for h in range(NH):
                bank = pg[h // 4]
                qc = (h % 4) * CD
                for kb in range(HB):
                    mm(bank[0:64, qc:qc + CD],
                       wq_b[:, kb * D + h * DH:kb * D + h * DH + DH],
                       xn1bL[:, kb * CD:(kb + 1) * CD],
                       kb == 0, kb == HB - 1,
                       "sT" if (kb == HB - 1 and h % 4 == 3) else None)
                if h % 4 == 3:
                    b = h // 4
                    pr.wait("scalar", "sT", pr.current("sT"))
                    pr.op("scalar", lambda s, b=b: s.activation(
                        qfm[0:64, b * 512:(b + 1) * 512],
                        pg[b][0:64, 0:512], AF.Copy), "sEv")
            if dc == 0 and checkpoint("Cq"):
                return nc, outs
            pr.wait("tensor", "sEv", pr.current("sEv"))
            for half in range(2):
                for kb in range(HB):
                    mm(pg[2 + half][:, 0:512], xn1bL[:, kb * CD:(kb + 1) * CD],
                       wk1_b[:, kb * D + half * 512:kb * D + half * 512 + 512],
                       kb == 0, kb == HB - 1, "sT" if kb == HB - 1 else None)
            for half in range(2):
                pr.wait("tensor", "sEv", pr.current("sEv"))
                for kb in range(HB):
                    mm(pg[4][:, 0:512], xn1bL[:, kb * CD:(kb + 1) * CD],
                       wv1_b[:, kb * D + half * 512:kb * D + half * 512 + 512],
                       kb == 0, kb == HB - 1, "sT" if kb == HB - 1 else None)
                tv = pr.current("sT")
                pr.wait("scalar", "sT", tv)
                pr.op("scalar", lambda s, half=half: s.activation(
                    vtm[:, half * 512:half * 512 + 512], pg[4][:, 0:512],
                    AF.Copy), "sEv")
            ktv = pr.current("sT")
            for half in range(2):
                pr.wait("scalar", "sT", ktv)
                if half > 0:
                    pr.wait("scalar", "sV")
                pr.op("scalar", lambda s, half=half: s.activation(
                    sqs[:, 0:512], pg[2 + half][:, 0:512], AF.Square), "sS")
                ssv = pr.current("sS")
                pr.wait("vector", "sS", ssv)
                pr.op("vector", lambda v, half=half: v.tensor_reduce(
                    kss[:, half * 8:half * 8 + 8],
                    sqs[:, 0:512].rearrange("p (h d) -> p h d", d=DH),
                    axis=AXX, op=ALU.add), "sV")
            pr.wait("scalar", "sV")

            def f(s):
                s.activation(rstdk[:, 0:16], kss[:, 0:16], AF.Ln,
                             bias=eps24[:, 0:1])
                return s.activation(rstdk[:, 0:16], rstdk[:, 0:16], AF.Exp,
                                    scale=-0.5)
            pr.op("scalar", f, "sS")
            pr.wait("vector", "sS")
            for h in range(NH):
                half = h // 8
                pr.op("vector", lambda v, h=h, half=half: v.tensor_scalar(
                    khat[:, h * DH:(h + 1) * DH],
                    pg[2 + half][:, (h % 8) * DH:(h % 8) * DH + DH],
                    rstdk[:, h:h + 1], None, ALU.mult),
                    "sEv")
            pr.wait("tensor", "sEv", pr.current("sEv"))
            for kb in range(HB):
                mm(pg[4][:, 0:NH], xn1bL[:, kb * CD:(kb + 1) * CD],
                   wbeta_b[:, kb * NH:(kb + 1) * NH], kb == 0, kb == HB - 1,
                   "sT" if kb == HB - 1 else None)
            tv = pr.current("sT")
            pr.wait("vector", "sT", tv)
            pr.op("vector", lambda v: v.tensor_add(beta_c[:, 0:NH],
                                                   pg[4][:, 0:NH],
                                                   bbeta_t[:]), "sV")
            pr.wait("scalar", "sV")
            pr.op("scalar", lambda s: s.activation(beta_c[:, 0:NH],
                                                   beta_c[:, 0:NH],
                                                   AF.Sigmoid), "sS")
            pr.wait("vector", "sS")
            pr.op("vector", lambda v: v.tensor_scalar_mul(negb_c[:, 0:NH],
                                                          beta_c[:, 0:NH],
                                                          -1.0), "sV")
            pr.wait("vector", "sV")
            pr.wait("scalar", "sV")
            if dc == 0 and checkpoint("Ckv"):
                return nc, outs
            for h in range(NH):
                pr.wait("tensor", "sEv", pr.current("sEv"))
                ttv = pr.op("tensor", lambda t, h=h: t.transpose(
                    ptb[0:64, 0:128], khat[:, h * DH:(h + 1) * DH], id_bf[:]),
                    "sT")
                pr.wait("vector", "sT", ttv)
                pr.op("vector", lambda v, h=h: v.tensor_copy(
                    kfm[0:64, h * CD:(h + 1) * CD], ptb[0:64, 0:128]), "sEv")

            if dc == 0 and checkpoint("Ckf"):
                return nc, outs
            def hsl_(t, h):
                return t[0:128, h * 128:(h + 1) * 128]

            pr.wait("tensor", "sEv", pr.current("sEv"))
            for h in range(NH):
                bank = pg[h // 4]
                qc = (h % 4) * 128
                mm(bank[:, qc:qc + 128], kfm[0:64, h * CD:(h + 1) * CD],
                   kfm[0:64, h * CD:(h + 1) * CD], True, True,
                   "sT" if h == NH - 1 else None)
            tv = pr.current("sT")
            pr.wait("vector", "sT", tv)
            for b in range(4):
                pr.op("vector", lambda v, b=b: v.tensor_mul(
                    csc[:, 0:512], pg[b][:, 0:512], trilS4[:]), None)

                def f(v, b=b):
                    last = None
                    for hq in range(4):
                        h = b * 4 + hq
                        last = v.tensor_scalar(
                            hsl_(M_all, h), csc[:, hq * 128:(hq + 1) * 128],
                            negb_c[:, h:h + 1], None, ALU.mult)
                    return last
                pr.op("vector", f, "sEv")

            def f(v):
                last = None
                for h in range(NH):
                    last = v.tensor_scalar(Y_all[0:128, h * 128:h * 128 + 64],
                                           vtm[:, h * DH:(h + 1) * DH],
                                           beta_c[:, h:h + 1], None, ALU.mult)
                    last = v.tensor_scalar(Y_all[0:128, h * 128 + 64:
                                                 h * 128 + 128],
                                           khat[:, h * DH:(h + 1) * DH],
                                           beta_c[:, h:h + 1], None, ALU.mult)
                return last
            pr.op("vector", f, "sEv")
            pr.wait("tensor", "sEv", pr.current("sEv"))
            for r4 in range(4):
                for k4 in range(4):
                    h = r4 * 4 + k4
                    pr.op("tensor", lambda t, h=h, k4=k4: t.transpose(
                        ptb[0:128, k4 * 128:(k4 + 1) * 128], hsl_(M_all, h),
                        id_bf[:]), "sT" if k4 == 3 else None)
                pr.wait("vector", "sT", pr.current("sT"))
                pr.op("vector", lambda v, r4=r4: v.tensor_copy(
                    MT_all[0:128, r4 * 512:(r4 + 1) * 512], ptb[0:128, 0:512]),
                    "sEv")
                if r4 < 3:
                    pr.wait("tensor", "sEv", pr.current("sEv"))
            curM, curMT = M_all, MT_all
            nxtM, nxtMT = M2_all, MT2_all
            for lvl in range(7):
                pr.wait("tensor", "sEv", pr.current("sEv"))
                for h in range(NH):
                    bank = pg[h // 4]
                    qc = (h % 4) * 128
                    mm(bank[:, qc:qc + 128], hsl_(curMT, h), hsl_(Y_all, h),
                       True, True, "sT" if h % 4 == 3 else None)
                    if h % 4 == 3:
                        b = h // 4
                        pr.wait("vector", "sT", pr.current("sT"))
                        pr.op("vector", lambda v, b=b: v.tensor_add(
                            Y_all[0:128, b * 512:(b + 1) * 512],
                            Y_all[0:128, b * 512:(b + 1) * 512],
                            pg[b][:, 0:512]), "sEv")
                yev = pr.current("sEv")
                if lvl < 6:
                    for h in range(NH):
                        bank = pg[4 + h // 4]
                        qc = (h % 4) * 128
                        mm(bank[:, qc:qc + 128], hsl_(curMT, h), hsl_(curM, h),
                           True, True, "sT" if h % 4 == 3 else None)
                        if h % 4 == 3:
                            b = h // 4
                            pr.wait("scalar", "sT", pr.current("sT"))
                            pr.op("scalar", lambda s, b=b, nxtM=nxtM:
                                  s.activation(
                                      nxtM[0:128, b * 512:(b + 1) * 512],
                                      pg[4 + b][:, 0:512], AF.Copy), "sEv")
                    pr.wait("tensor", "sEv", yev)
                    for h in range(NH):
                        bank = pg[h // 4]
                        qc = (h % 4) * 128
                        mm(bank[:, qc:qc + 128], hsl_(curM, h), hsl_(curMT, h),
                           True, True, "sT" if h % 4 == 3 else None)
                        if h % 4 == 3:
                            b = h // 4
                            pr.wait("vector", "sT", pr.current("sT"))
                            pr.op("vector", lambda v, b=b, nxtMT=nxtMT:
                                  v.tensor_copy(
                                      nxtMT[0:128, b * 512:(b + 1) * 512],
                                      pg[b][:, 0:512]), "sEv")
                    curM, nxtM = nxtM, curM
                    curMT, nxtMT = nxtMT, curMT
            pr.wait("tensor", "sEv", pr.current("sEv"))
            for h in range(NH):
                bank = pg[h // 8]
                qc = (h % 8) * 64
                mm(bank[0:64, qc:qc + 64],
                   Y_all[0:128, h * 128 + 64:h * 128 + 128],
                   khat[:, h * DH:(h + 1) * DH], True, True,
                   "sT" if h == NH - 1 else None)
            for h in range(NH):
                bank = pg[2 + h // 8]
                qc = (h % 8) * 64
                mm(bank[0:64, qc:qc + 64], khat[:, h * DH:(h + 1) * DH],
                   Y_all[0:128, h * 128 + 64:h * 128 + 128], True, True,
                   "sT" if h == NH - 1 else None)
            tv = pr.current("sT")
            pr.wait("vector", "sT", tv)
            for b in range(2):
                pr.op("vector", lambda v, b=b: v.tensor_sub(
                    ImG_all[0:64, b * 512:(b + 1) * 512], id64s[:],
                    pg[b][0:64, 0:512]), "sEv")
            for b in range(2):
                pr.op("vector", lambda v, b=b: v.tensor_sub(
                    ImGT_all[0:64, b * 512:(b + 1) * 512], id64s[:],
                    pg[2 + b][0:64, 0:512]), "sEv")
            pr.wait("tensor", "sEv", pr.current("sEv"))
            for h in range(NH):
                bank = pg[h // 4]
                qc = (h % 4) * 128
                mm(bank[:, qc:qc + 128], kfm[0:64, h * CD:(h + 1) * CD],
                   qfm[0:64, h * CD:(h + 1) * CD], True, True,
                   "sT" if h == NH - 1 else None)
            tv = pr.current("sT")
            pr.wait("vector", "sT", tv)
            for b in range(4):
                pr.op("vector", lambda v, b=b: v.tensor_mul(
                    csc[:, 0:512], pg[b][:, 0:512], triuI4[:]), None)
                pr.op("vector", lambda v, b=b: v.tensor_copy(
                    W2T_all[0:128, b * 512:(b + 1) * 512], csc[:, 0:512]),
                    "sEv")
            pr.wait("tensor", "sEv", pr.current("sEv"))
            for h in range(NH):
                bank = pg[h // 8]
                qc = (h % 8) * 64
                mm(bank[0:64, qc:qc + 64],
                   MpT_run[0:64, h * DH:(h + 1) * DH],
                   ImG_all[0:64, h * 64:(h + 1) * 64], True, True,
                   "sT" if h == NH - 1 else None)
            for h in range(NH):
                bank = pg[2 + h // 8]
                qc = (h % 8) * 64
                mm(bank[0:64, qc:qc + 64],
                   ImG_all[0:64, h * 64:(h + 1) * 64],
                   MpT_run[0:64, h * DH:(h + 1) * DH], True, True,
                   "sT" if h == NH - 1 else None)
            for h in range(NH):
                bank = pg[4 + h // 8]
                qc = (h % 8) * 64
                mm(bank[0:64, qc:qc + 64], khat[:, h * DH:(h + 1) * DH],
                   Y_all[0:128, h * 128:h * 128 + 64], True, False)
                mm(bank[0:64, qc:qc + 64],
                   ImG_all[0:64, h * 64:(h + 1) * 64],
                   StB_run[0:64, h * DH:(h + 1) * DH], False, True,
                   "sT" if h == NH - 1 else None)
            tv = pr.current("sT")
            pr.wait("vector", "sT", tv)
            for b in range(2):
                pr.op("vector", lambda v, b=b: v.tensor_copy(
                    Mp_run[0:64, b * 512:(b + 1) * 512], pg[b][0:64, 0:512]),
                    None)
                pr.op("vector", lambda v, b=b: v.tensor_copy(
                    MpT_run[0:64, b * 512:(b + 1) * 512],
                    pg[2 + b][0:64, 0:512]), None)
                pr.op("vector", lambda v, b=b: v.tensor_copy(
                    St_run[0:64, b * 512:(b + 1) * 512],
                    pg[4 + b][0:64, 0:512]), None)
                pr.op("vector", lambda v, b=b: v.tensor_copy(
                    StB_run[0:64, b * 512:(b + 1) * 512],
                    pg[4 + b][0:64, 0:512]), "sEv")
            pr.wait("tensor", "sEv", pr.current("sEv"))
            for h in range(NH):
                bank = pg[h // 4]
                qc = (h % 4) * 128
                mm(bank[0:64, qc:qc + 128],
                   Y_all[0:128, h * 128:h * 128 + 64],
                   W2T_all[0:128, h * 128:(h + 1) * 128], True, True,
                   "sT" if h == NH - 1 else None)
            for h in range(NH):
                bank = pg[4 + h // 4]
                qc = (h % 4) * 128
                mm(bank[0:64, qc:qc + 128],
                   Y_all[0:128, h * 128 + 64:h * 128 + 128],
                   W2T_all[0:128, h * 128:(h + 1) * 128], True, True,
                   "sT" if h == NH - 1 else None)
            tv = pr.current("sT")
            pr.wait("vector", "sT", tv)
            for b in range(4):
                pr.op("vector", lambda v, b=b: v.tensor_copy(
                    O1ch[0:64, b * 512:(b + 1) * 512], pg[b][0:64, 0:512]),
                    None)

                def f(v, b=b):
                    v.tensor_scalar_mul(csc[0:64, 0:512],
                                        pg[4 + b][0:64, 0:512], -1.0)
                    return v.tensor_copy(PWch[0:64, b * 512:(b + 1) * 512],
                                         csc[0:64, 0:512])
                pr.op("vector", f, "sEv")
            pr.wait("sync", "sEv")
            pr.dma("sync", sp_O1[dc], O1ch[0:64, :], "sSp")
            pr.dma("sync", sp_Qef[dc], PWch[0:64, :], "sSp")
            pr.dma("sync", sp_Qfm[dc], qfm[0:64, :], "sSp")

        if checkpoint("C"):
            return nc, outs
        pr.wait("gpsimd", "sEv")
        pr.wait("gpsimd", "sV")
        pr.dma("gpsimd", agst[0:64, 0:NH * DH], St_run[0:64, :], "sG")
        pr.dma("gpsimd", agst[0:64, NH * DH:2 * NH * DH], Mp_run[0:64, :],
               "sG")
        pr.wait("gpsimd", "sG")
        pr.dma("gpsimd", ag1_in, agst[0:64, :], "sG")
        pr.wait("gpsimd", "sG")
        pr.op("gpsimd", lambda g: g.collective_compute(
            "AllGather", ALU.bypass, replica_groups=[list(range(8))],
            ins=[ag1_in.opt()], outs=[ag1_out.opt()]), "sCC")
        pr.wait("gpsimd", "sCC")
        ag1t = ar.bf16(2 * NH * DH)
        msel_f = ar.f32(64)
        msel_b = ar.bf16(64)
        stf = ar.f32(64)
        pr.op("vector", lambda v: v.memset(StIn[0:64, :], 0.0))
        pr.op("vector", lambda v: v.memset(StInB[0:64, :], 0.0), "sV")
        pr.wait("vector", "sV")
        for s in range(8):
            pr.wait("gpsimd", "sEv")
            pr.dma("gpsimd", ag1t[0:64, :], ag1_out[s], "sG")
            pr.wait("vector", "sG")
            base = 0
            for h in range(NH):
                Cs = ag1t[0:64, base + h * DH:base + (h + 1) * DH]
                Ms = ag1t[0:64, base + NH * DH + h * DH:
                          base + NH * DH + (h + 1) * DH]

                def f(v, Cs=Cs, Ms=Ms, s=s):
                    v.tensor_scalar(msel_f[0:64, :], Ms,
                                    cmask_t[0:64, s:s + 1], None, ALU.mult)
                    v.tensor_add(msel_f[0:64, :], msel_f[0:64, :],
                                 isel_t[0:64, s * 64:(s + 1) * 64])
                    return v.tensor_copy(msel_b[0:64, 0:64], msel_f[0:64, :])
                pr.op("vector", f, "sV")
                pr.wait("vector", "sV")
                pr.wait("tensor", "sV")
                pr.wait("tensor", "sEv", pr.current("sEv"))
                tv = mm(pg[0][0:64, 0:64], msel_b[0:64, 0:64],
                        StInB[0:64, h * DH:(h + 1) * DH], True, True, "sT")
                pr.wait("vector", "sT", tv)

                def f(v, Cs=Cs, s=s, h=h):
                    v.tensor_scalar(stf[0:64, :], Cs, cmask_t[0:64, s:s + 1],
                                    None, ALU.mult)
                    v.tensor_add(StIn[0:64, h * DH:(h + 1) * DH],
                                 pg[0][0:64, 0:64], stf[0:64, :])
                    return v.tensor_copy(StInB[0:64, h * DH:(h + 1) * DH],
                                         StIn[0:64, h * DH:(h + 1) * DH])
                pr.op("vector", f, "sEv")
                pr.wait("vector", "sEv")

        if checkpoint("AG1"):
            return nc, outs
        ar.pop()
        ar.push()
        FP = tok
        NFC = FP // 1024
        _xo = ar.off
        xn1f = ar.bf16(HB * FP)
        wstD = [ar.bf16(1024) for _ in range(8)]
        hbuf = ar.bf16(4 * FP)
        hsl2 = [ar.bf16(1024) for _ in range(6)]
        g1b = ar.b[0:128, _xo:_xo + HB * 1024]
        woutD = ar.bf16(NH * D, parts=64)
        MpD = ar.bf16(NH * DH)
        StlD = ar.f32(NH * DH)
        O1D = ar.f32(NH * CD)
        PWD = ar.bf16(NH * CD)
        QfD = ar.bf16(NH * CD)
        StaD = ar.bf16(NH * DH)
        Ofm = ar.bf16(NH * CD)
        s1b = ar.f32(HB * CD)
        x1tmD = ar.f32(D)
        sgtD = ar.f32(D)
        dsc = ar.f32(512)

        pr.wait("sync", "sSp")
        xn1f3 = xn1f.rearrange("p (hb t) -> p hb t", hb=HB)
        for dc in range(ndc):
            pr.dma("sync", xn1f3[:, :, dc * CD:(dc + 1) * CD], sp_xn1b[dc],
                   "sX")
        pr.wait("tensor", "sX")

        w_last = {}
        h_last = {}

        def stage(w_name, scales, act, nk, nob, src_sbuf, src_dram, dst_dram,
                  second=None):
            for ogr in range(nob // 4):
                for fc in range(NFC):
                    f0 = fc * 1024
                    pr.wait("tensor", "sEv", pr.current("sEv"))
                    kb_tv = {}
                    for kb in range(nk):
                        wslot = wstD[kb % 8]
                        pr.wait("gpsimd", "sT",
                                w_last.get(kb % 8, pr.current("sT")))
                        pr.dma("gpsimd", wslot[:, 0:512],
                               Wd[w_name][kb * 128:(kb + 1) * 128,
                                          ogr * 512:(ogr + 1) * 512], "sW")
                        if src_dram is not None:
                            hslot = hsl2[kb % 6]
                            pr.wait("sync", "sT",
                                    h_last.get(kb % 6, pr.current("sT")))
                            pr.dma("sync", hslot[:, 0:1024],
                                   src_dram[kb][:, f0:f0 + 1024], "sX")
                            def rr(fi, hslot=hslot):
                                return hslot[:, fi * 512:fi * 512 + 512]
                        else:
                            def rr(fi, kb=kb, f0=f0):
                                return src_sbuf[:, kb * FP + f0 + fi * 512:
                                                kb * FP + f0 + fi * 512 + 512]
                        pr.wait("tensor", "sW", pr.current("sW"))
                        pr.wait("tensor", "sX", pr.current("sX"))
                        for ob in range(4):
                            for fi in range(2):
                                kb_tv[kb] = mm(
                                    pg[2 * ob + fi][:, 0:512],
                                    wslot[:, ob * 128:(ob + 1) * 128],
                                    rr(fi),
                                    kb == 0, kb == nk - 1,
                                    "sT" if (ob == 3 and fi == 1) else None)
                        w_last[kb % 8] = kb_tv[kb]
                        if src_dram is not None:
                            h_last[kb % 6] = kb_tv[kb]
                    tv = pr.current("sT")
                    pr.wait("scalar", "sT", tv)
                    if dst_dram is not None:
                        pr.wait("scalar", "sSp", pr.current("sSp"))
                    for ob in range(4):
                        o = ogr * 4 + ob
                        for fi in range(2):
                            pr.op("scalar", lambda s, o=o, ob=ob, fi=fi:
                                  s.activation(
                                      hbuf[:, ob * FP + fi * 512:
                                           ob * FP + fi * 512 + 512],
                                      pg[2 * ob + fi][:, 0:512], act,
                                      scale=scales[:, o:o + 1]), "sEv")
                    if second is not None:
                        w2n, s2 = second
                        pr.wait("tensor", "sEv", pr.current("sEv"))
                        kb_tv2 = {}
                        for kb in range(nk):
                            wslot = wstD[kb % 8]
                            pr.wait("gpsimd", "sT",
                                    w_last.get(kb % 8, pr.current("sT")))
                            pr.dma("gpsimd", wslot[:, 0:512],
                                   Wd[w2n][kb * 128:(kb + 1) * 128,
                                           ogr * 512:(ogr + 1) * 512], "sW")
                            pr.wait("tensor", "sW", pr.current("sW"))
                            for ob in range(4):
                                for fi in range(2):
                                    kb_tv2[kb] = mm(
                                        pg[2 * ob + fi][:, 0:512],
                                        wslot[:, ob * 128:(ob + 1) * 128],
                                        src_sbuf[:, kb * FP + f0 + fi * 512:
                                                 kb * FP + f0 + fi * 512 + 512],
                                        kb == 0, kb == nk - 1,
                                        "sT" if (ob == 3 and fi == 1) else None)
                            w_last[kb % 8] = kb_tv2[kb]
                        tv = pr.current("sT")
                        pr.wait("vector", "sT", tv)
                        for ob in range(4):
                            o = ogr * 4 + ob
                            for fi in range(2):
                                pr.op("vector", lambda v, o=o, ob=ob, fi=fi:
                                      v.scalar_tensor_tensor(
                                          hbuf[:, ob * FP + fi * 512:
                                               ob * FP + fi * 512 + 512],
                                          pg[2 * ob + fi][:, 0:512],
                                          s2[:, o:o + 1],
                                          hbuf[:, ob * FP + fi * 512:
                                               ob * FP + fi * 512 + 512],
                                          ALU.mult, ALU.mult), "sEv")
                    if dst_dram is not None:
                        pr.wait("sync", "sEv")
                        for ob in range(4):
                            o = ogr * 4 + ob
                            pr.dma("sync", dst_dram[o][:, f0:f0 + 1024],
                                   hbuf[:, ob * FP:ob * FP + 1024], "sSp")
            return

        stage("w1T", vecf_t["w1s"], AF.Silu, HB, FB, xn1f, None, sp_ha,
              second=("w2T", vecf_t["w2s"]))
        stage("h1T", vecf_t["h1s"], AF.Gelu, FB, FB, None, sp_ha, sp_hb)
        stage("h2T", vecf_t["h2s"], AF.Gelu, FB, FB, None, sp_hb, sp_hc)

        pr.wait("gpsimd", "sSp")
        for kb in range(HB):
            pr.dma("gpsimd", woutD[0:64, kb * 2048:(kb + 1) * 2048],
                   Wd["WoutH"][:, kb * 2048:(kb + 1) * 2048], "sW")
        wv_d = pr.current("sW")
        for fc in range(NFC):
            f0 = fc * 1024
            for ogr in range(2):
                pr.wait("tensor", "sEv", pr.current("sEv"))
                kb_tv = {}
                for kb in range(FB):
                    wslot = wstD[kb % 8]
                    pr.wait("gpsimd", "sT",
                            w_last.get(kb % 8, pr.current("sT")))
                    pr.dma("gpsimd", wslot[:, 0:512],
                           Wd["w3T"][kb * 128:(kb + 1) * 128,
                                     ogr * 512:(ogr + 1) * 512], "sW")
                    hslot = hsl2[kb % 6]
                    pr.wait("sync", "sT",
                            h_last.get(kb % 6, pr.current("sT")))
                    pr.dma("sync", hslot[:, 0:1024], sp_hc[kb][:, f0:f0 + 1024],
                           "sX")
                    pr.wait("tensor", "sW", pr.current("sW"))
                    pr.wait("tensor", "sX", pr.current("sX"))
                    for ob in range(4):
                        for fi in range(2):
                            kb_tv[kb] = mm(
                                pg[2 * ob + fi][:, 0:512],
                                wslot[:, ob * 128:(ob + 1) * 128],
                                hslot[:, fi * 512:fi * 512 + 512],
                                kb == 0, kb == FB - 1,
                                "sT" if (ob == 3 and fi == 1) else None)
                    w_last[kb % 8] = kb_tv[kb]
                    h_last[kb % 6] = kb_tv[kb]
                tv = pr.current("sT")
                pr.wait("scalar", "sT", tv)
                for ob in range(4):
                    o = ogr * 4 + ob
                    for fi in range(2):
                        pr.op("scalar", lambda s, o=o, ob=ob, fi=fi:
                              s.activation(
                                  g1b[:, o * 1024 + fi * 512:
                                      o * 1024 + fi * 512 + 512],
                                  pg[2 * ob + fi][:, 0:512], AF.Silu,
                                  scale=vec_t["w3s"][:, o:o + 1]), "sEv")
            if "dbgD" in taps:
                if fc == 0:
                    d_g1 = dout("d_g1", [NFC, 128, HB * 1024])
                    d_s1 = dout("d_s1", [ndc, 128, HB * CD])
                pr.wait("sync", "sEv")
                pr.dma("sync", d_g1[fc], g1b[:], "sO")
            pr.wait("tensor", "sW", wv_d)
            for dci in range(1024 // CD):
                dc = fc * (1024 // CD) + dci
                r0 = dc * CD
                gco = dci * CD
                pr.wait("sync", "sV")
                pr.dma("sync", MpD[0:64, :], sp_Mp[dc], "sX")
                pr.dma("sync", StlD[0:64, :], sp_Stl[dc], "sX")
                pr.dma("sync", O1D[0:64, :], sp_O1[dc], "sX")
                pr.dma("sync", PWD[0:64, :], sp_Qef[dc], "sX")
                pr.dma("sync", QfD[0:64, :], sp_Qfm[dc], "sX")
                pr.dma("sync", x1tmD[:], sp_x1[r0:r0 + CD, :], "sX")
                xv = pr.current("sX")
                pr.wait("tensor", "sX", xv)
                pr.wait("vector", "sX", xv)
                for h in range(NH):
                    pr.wait("tensor", "sEv", pr.current("sEv"))
                    tv = mm(pg[0][0:64, 0:64], MpD[0:64, h * DH:(h + 1) * DH],
                            StInB[0:64, h * DH:(h + 1) * DH], True, True, "sT")
                    pr.wait("vector", "sT", tv)
                    pr.op("vector", lambda v, h=h: v.tensor_add(
                        dsc[0:64, 0:64], pg[0][0:64, 0:64],
                        StlD[0:64, h * DH:(h + 1) * DH]), None)
                    pr.op("vector", lambda v, h=h: v.tensor_copy(
                        StaD[0:64, h * DH:(h + 1) * DH], dsc[0:64, 0:64]),
                        "sEv")
                    pr.wait("tensor", "sEv", pr.current("sEv"))
                    mm(pg[1][0:64, 0:128], StaD[0:64, h * DH:(h + 1) * DH],
                       QfD[0:64, h * CD:(h + 1) * CD], True, False)
                    tv = mm(pg[1][0:64, 0:128], StaD[0:64, h * DH:(h + 1) * DH],
                            PWD[0:64, h * CD:(h + 1) * CD], False, True, "sT")
                    pr.wait("vector", "sT", tv)

                    def f(v, h=h):
                        v.tensor_add(dsc[0:64, 0:128], pg[1][0:64, 0:128],
                                     O1D[0:64, h * CD:(h + 1) * CD])
                        return v.tensor_copy(Ofm[0:64, h * CD:(h + 1) * CD],
                                             dsc[0:64, 0:128])
                    pr.op("vector", f, "sEv")
                for o in range(HB):
                    bank = pg[2 + (o % 2)]
                    pr.wait("tensor", "sEv", pr.current("sEv"))
                    for h in range(NH):
                        mm(bank[:, 0:CD],
                           woutD[0:64, h * D + o * 128:h * D + o * 128 + 128],
                           Ofm[0:64, h * CD:(h + 1) * CD],
                           h == 0, h == NH - 1, "sT" if h == NH - 1 else None)
                    tv = pr.current("sT")
                    pr.wait("vector", "sT", tv)
                    pr.op("vector", lambda v, o=o, bank=bank: v.tensor_copy(
                        s1b[:, o * CD:(o + 1) * CD], bank[:, 0:CD]), "sEv")
                if "dbgD" in taps:
                    pr.wait("sync", "sEv")
                    pr.dma("sync", d_s1[dc], s1b[:], "sO")
                pr.wait("vector", "sEv")

                def f(v, gco=gco):
                    for hb in range(HB):
                        v.tensor_mul(s1b[:, hb * CD:(hb + 1) * CD],
                                     s1b[:, hb * CD:(hb + 1) * CD],
                                     g1b[:, hb * 1024 + gco:
                                         hb * 1024 + gco + CD])
                    return v.engine_nop()
                pr.op("vector", f, "sV")
                pr.wait("tensor", "sV")
                for hb in range(HB):
                    pr.wait("tensor", "sEv", pr.current("sEv"))
                    ttv = pr.op("tensor", lambda t, hb=hb: t.transpose(
                        pt[:, 0:128], s1b[:, hb * CD:(hb + 1) * CD], id_f[:]),
                        "sT")
                    pr.wait("vector", "sT", ttv)
                    pr.op("vector", lambda v, hb=hb: v.tensor_copy(
                        sgtD[:, hb * 128:(hb + 1) * 128], pt[:, 0:128]), "sEv")
                pr.wait("vector", "sEv")
                pr.op("vector", lambda v: v.tensor_add(x1tmD[:], x1tmD[:],
                                                       sgtD[:]), "sV")
                pr.wait("sync", "sV")
                pr.dma("sync", out_d[r0:r0 + CD, :], x1tmD[:], "sO")
        pr.wait("sync", "sO")

        pr.emit(es.enter_context(nc.Block()))
    return nc, outs


def kernel(**inputs):
    ims = host_prep(inputs, TOK)
    nc, outs = build(TOK)
    res = run_bass_kernel_spmd(nc, ims, core_ids=list(range(8)))
    shards = [np.asarray(res.results[c]["out"]) for c in range(8)]
    full = np.stack([np.concatenate(shards[0:4], 0),
                     np.concatenate(shards[4:8], 0)], 0)
    return np.ascontiguousarray(full, np.float32)
